# version 16
# baseline (speedup 1.0000x reference)
import numpy as np
import ml_dtypes
from contextlib import ExitStack

import concourse.bass as bass
import concourse.mybir as mybir
from concourse.bass_utils import run_bass_kernel_spmd

F32 = mybir.dt.float32
BF16 = mybir.dt.bfloat16
AF = mybir.ActivationFunctionType
ALU = mybir.AluOpType
AX = mybir.AxisListType
BF = ml_dtypes.bfloat16

B, N, D = 8, 2048, 1024
NT, DT = N // 128, D // 128
HIDDEN, QKD = 2048, 2048
HEADS, DH, INNER = 8, 64, 512
WIN, NW = 512, 4
HID, HIDP = 2730, 2816
KH = HIDP // 128
EPS = 1e-5
NEG = -1e30

ENGINES = ("tensor", "vector", "scalar", "gpsimd", "sync")
TRUNC_M = 12
CHAIN_ON = True
VPART = 3



class Prog:

    R = 8

    def __init__(self, nc, stack):
        self.nc = nc
        self.sems = {e: stack.enter_context(nc.semaphore("sem_" + e)) for e in ENGINES}
        self.dsems = {e: [stack.enter_context(nc.semaphore(f"dsem_{e}{r}"))
                          for r in range(self.R)] for e in ENGINES}
        self.cnt = {e: 0 for e in ENGINES}
        self.dcnt = {e: 0 for e in ENGINES}
        self.drot = {e: [0] * self.R for e in ENGINES}
        self.cov = {e: {} for e in ENGINES}
        self.pending = {e: [] for e in ENGINES}
        self.res = {}
        self.n_tasks = 0

    def task(self, eng, emit, reads=(), writes=(), tag="", n_dma=0):
        deps = {}

        def add(tok):
            if tok is None:
                return
            k, v = tok[:-1], tok[-1]
            if deps.get(k, 0) < v:
                deps[k] = v

        for r in reads:
            e = self.res.get(r)
            if e:
                add(e["w"])
        for r in writes:
            e = self.res.get(r)
            if e:
                add(e["w"])
                for t in e["r"]:
                    add(t)
        if eng == "tensor":
            deps = {k: v for k, v in deps.items() if k != ("c", "tensor")}
        if n_dma:
            assert n_dma == 1
            r = self.dcnt[eng] % self.R
            self.dcnt[eng] += 1
            prev = self.drot[eng][r]
            self.drot[eng][r] = prev + 16
            if prev and deps.get(("d", eng, r), 0) < prev:
                deps[("d", eng, r)] = prev
            comp = ("d", eng, r, prev + 16)
            self.pending[eng].append((deps, emit, r, tag))
        else:
            self.cnt[eng] += 1
            comp = ("c", eng, self.cnt[eng])
            self.pending[eng].append((deps, emit, None, tag))
        self.n_tasks += 1
        for r in reads:
            self.res.setdefault(r, {"w": None, "r": []})["r"].append(comp)
        for r in writes:
            self.res[r] = {"w": comp, "r": []}

    def dma(self, eng, out, in_, reads=(), writes=(), tag=""):
        def emit(e, out=out, in_=in_):
            return [e.dma_start(out=out, in_=in_)]
        self.task(eng, emit, reads, writes, tag, n_dma=1)

    def wait_only(self, eng, deps, tag="wait"):
        self.pending[eng].append((dict(deps), None, 0, tag))

    def flush(self, name):
        for e in ENGINES:
            if self.dcnt[e]:
                self.wait_only(e, {("d", e, r): self.drot[e][r]
                                   for r in range(self.R) if self.drot[e][r]},
                               "fence")
        with self.nc.named_scope(name), self.nc.Block() as block:
            for e in ENGINES:
                tasks = self.pending[e]
                self.pending[e] = []
                if not tasks:
                    continue

                def gen(e=e, tasks=tasks):
                    def _(eng):
                        cov = self.cov[e]
                        for (deps, emit, dslot, tag) in tasks:
                            for key, val in sorted(deps.items()):
                                if cov.get(key, 0) >= val:
                                    continue
                                if key[0] == "d":
                                    sem = self.dsems[key[1]][key[2]]
                                else:
                                    sem = self.sems[key[1]]
                                eng.wait_ge(sem, val)
                                cov[key] = val
                            if emit is None:
                                continue
                            if dslot is not None:
                                insts = emit(eng)
                                assert len(insts) == 1, tag
                                insts[0].then_inc(self.dsems[e][dslot], 16)
                            else:
                                inst = emit(eng)
                                if inst is None:
                                    inst = eng.nop()
                                inst.then_inc(self.sems[e], 1)
                    return _
                getattr(block, e)(gen())
        for e in ENGINES:
            for e2 in ENGINES:
                self.cov[e][("c", e2)] = self.cnt[e2]
                for r in range(self.R):
                    self.cov[e][("d", e2, r)] = self.drot[e2][r]


def host_constants():
    c = {}
    c["identity_bf"] = np.eye(128, dtype=BF)
    c["identity_f32"] = np.eye(128, dtype=np.float32)
    c["ones1_bf"] = np.ones((1, 128), dtype=BF)
    c["ones_col_bf"] = np.ones((128, 1), dtype=BF)
    c["ones512_bf"] = np.ones((1, 512), dtype=BF)
    e2c = np.zeros((128, 2), dtype=np.float32)
    e2c[0:64, 0] = 1.0
    e2c[64:128, 1] = 1.0
    c["e2c_bf"] = e2c.astype(BF)
    e2t = np.zeros((2, 128), dtype=np.float32)
    e2t[0, 0:64] = 1.0
    e2t[1, 64:128] = 1.0
    c["e2t"] = e2t
    R = np.zeros((128, 128), dtype=np.float32)
    for base in (0, 64):
        for i in range(32):
            R[base + i, base + 32 + i] = -1.0
            R[base + 32 + i, base + i] = 1.0
    c["rotT_bf"] = R.T.copy().astype(BF)
    half = np.arange(0, DH, 2, dtype=np.float64)
    inv_freq = 1.0 / (10000.0 ** (half / DH))
    tpos = np.arange(N, dtype=np.float64)
    freqs = np.concatenate([tpos[:, None] * inv_freq[None, :]] * 2, axis=-1)
    base = (half + 0.4 * DH) / (1.4 * DH)
    power = (tpos - N // 2) / float(WIN // 2)
    scale = np.concatenate([base[None, :] ** power[:, None]] * 2, axis=-1)
    cos, sin = np.cos(freqs), np.sin(freqs)
    QKS = 8.0
    tab = np.stack([(cos * scale * QKS).T, (sin * scale * QKS).T,
                    (cos / scale).T, (sin / scale).T])
    c["rotary"] = np.tile(tab, (1, 2, 1)).astype(BF)
    m1 = np.zeros((4, 128, 1024), dtype=BF)
    j = np.arange(1024)
    for it in range(4):
        for pp in range(128):
            il = it * 128 + pp
            m1[it, pp] = np.where((j >= il) & (j <= il + WIN), 0.0, NEG).astype(BF)
    c["mask1"] = m1
    return c


def prep_inputs(inputs):
    g = {}
    f32 = np.float32
    asf = lambda a: np.ascontiguousarray(np.asarray(a, dtype=f32))
    g["x"] = asf(inputs["L0_attn_x"])
    g["W_hidden"] = asf(inputs["L0_attn_W_hidden"])
    g["W_qk"] = asf(inputs["L0_attn_W_qk"])
    g["W_out"] = asf(inputs["L0_attn_W_out"])
    g["bh_t"] = asf(inputs["L0_attn_b_hidden"]).reshape(32, 128).T.copy()
    g["bqk_row"] = asf(inputs["L0_attn_b_qk"]).reshape(1, QKD)
    g["bout_t"] = asf(inputs["L0_attn_b_out"]).reshape(8, 128).T.copy()
    g["lng_t"] = asf(inputs["L0_attn_ln_g"]).reshape(8, 128).T.copy()
    g["lnb_t"] = asf(inputs["L0_attn_ln_b"]).reshape(8, 128).T.copy()
    osg = asf(inputs["L0_attn_os_gamma"])
    osb = asf(inputs["L0_attn_os_beta"])
    g["gb_row"] = np.ascontiguousarray(
        np.concatenate([osg[0], osb[0], osg[1], osb[1]]).reshape(1, 4 * QKD))
    g["wqkv"] = asf(inputs["L0_ffn_wqkv"])
    g["wo"] = asf(inputs["L0_ffn_wo"])
    w1 = asf(inputs["L0_ffn_w1"])
    w2 = asf(inputs["L0_ffn_w2"])
    b1 = asf(inputs["L0_ffn_b1"])
    b2 = asf(inputs["L0_ffn_b2"])
    w1a = np.zeros((2, D, HIDP), f32)
    w1g = np.zeros((2, D, HIDP), f32)
    w1a[:, :, :HID] = w1[:, :, :HID]
    w1g[:, :, :HID] = w1[:, :, HID:]
    w1ag = np.zeros((2, KH, D, 256), f32)
    for m in range(KH):
        w1ag[:, m, :, 0:128] = w1a[:, :, m * 128:(m + 1) * 128]
        w1ag[:, m, :, 128:256] = w1g[:, :, m * 128:(m + 1) * 128]
    g["w1ag"] = np.ascontiguousarray(w1ag)
    w2p = np.zeros((2, HIDP, D), f32)
    w2p[:, :HID, :] = w2
    w2cb = np.zeros((2, 8, HIDP, 128), f32)
    for mo in range(8):
        w2cb[:, mo] = w2p[:, :, mo * 128:(mo + 1) * 128]
    g["w2cb"] = np.ascontiguousarray(w2cb)
    g["b2_row"] = np.ascontiguousarray(b2.reshape(2, 1, D))
    b1a = np.zeros((2, HIDP), f32)
    b1g = np.zeros((2, HIDP), f32)
    b1a[:, :HID] = b1[:, :HID]
    b1g[:, :HID] = b1[:, HID:]
    g["b1a_row"] = np.ascontiguousarray(b1a.reshape(2, 1, HIDP))
    g["b1g_t"] = np.ascontiguousarray(b1g.reshape(2, KH, 128).transpose(0, 2, 1))
    g["flg_t"] = np.ascontiguousarray(
        asf(inputs["L0_ffn_ln_g"]).reshape(2, 8, 128).transpose(0, 2, 1))
    g["flb_t"] = np.ascontiguousarray(
        asf(inputs["L0_ffn_ln_b"]).reshape(2, 8, 128).transpose(0, 2, 1))
    qs = asf(inputs["L0_ffn_q_scale"])
    ks = asf(inputs["L0_ffn_k_scale"])
    g["qs_tile"] = np.ascontiguousarray(
        np.broadcast_to(np.tile(qs, (1, 2))[:, None, :], (2, 2, 128)).copy())
    g["ks_tile"] = np.ascontiguousarray(
        np.broadcast_to(np.tile(ks, (1, 2))[:, None, :], (2, 2, 128)).copy())
    return g


INPUT_NAMES = [
    "W_hidden", "W_qk", "W_out", "bh_t", "bqk_row", "bout_t", "lng_t", "lnb_t",
    "gb_row", "wqkv", "wo", "w1ag", "w2cb", "b2_row", "b1a_row", "b1g_t",
    "flg_t", "flb_t", "qs_tile", "ks_tile"]
CONST_NAMES = ["e2t", "e2c_bf", "rotT_bf", "identity_f32", "identity_bf",
               "ones1_bf", "ones_col_bf", "ones512_bf", "rotary", "mask1"]


class Build:
    def __init__(self, debug=(), stop_after=None):
        self.nc = bass.Bass("TRN2", target_bir_lowering=False)
        self.dbg = set(debug)
        self.stop_after = stop_after
        self.h = {}
        self.dbg_h = {}
        self.out_res = []

    def inp(self, name, shape, dt=F32):
        self.h[name] = self.nc.dram_tensor(
            name, list(shape), dt, kind="ExternalInput").ap()

    def declare_io(self):
        nc = self.nc
        self.inp("x", (N, D))
        self.inp("W_hidden", (D, 2 * HIDDEN))
        self.inp("W_qk", (D, QKD))
        self.inp("W_out", (HIDDEN, D))
        self.inp("bh_t", (128, 32))
        self.inp("bqk_row", (1, QKD))
        self.inp("bout_t", (128, 8))
        self.inp("lng_t", (128, 8))
        self.inp("lnb_t", (128, 8))
        self.inp("gb_row", (1, 4 * QKD))
        self.inp("wqkv", (2, D, 3 * INNER))
        self.inp("wo", (2, INNER, D))
        self.inp("w1ag", (2, KH, D, 256))
        self.inp("w2cb", (2, 8, HIDP, 128))
        self.inp("b2_row", (2, 1, D))
        self.inp("b1a_row", (2, 1, HIDP))
        self.inp("b1g_t", (2, 128, KH))
        self.inp("flg_t", (2, 128, 8))
        self.inp("flb_t", (2, 128, 8))
        self.inp("qs_tile", (2, 2, 128))
        self.inp("ks_tile", (2, 2, 128))
        self.inp("e2t", (2, 128))
        self.inp("e2c_bf", (128, 2), BF16)
        self.inp("rotT_bf", (128, 128), BF16)
        self.inp("identity_f32", (128, 128))
        self.inp("identity_bf", (128, 128), BF16)
        self.inp("ones1_bf", (1, 128), BF16)
        self.inp("ones_col_bf", (128, 1), BF16)
        self.inp("ones512_bf", (1, 512), BF16)
        self.inp("rotary", (4, 128, N), BF16)
        self.inp("mask1", (4, 128, 1024), BF16)
        self.out_h = nc.dram_tensor("out", [N, D], F32, kind="ExternalOutput").ap()
        self.xn_scr = nc.dram_tensor("xn_scr", [DT, 128, N], F32).ap()
        self.s_scr = nc.dram_tensor("s_scr", [DT, 128, N], F32).ap()
        for name, shape in [
                ("dbg_xn", (DT, 128, N)), ("dbg_normed", (DT, 128, N)),
                ("dbg_qk", (NT, 128, QKD)), ("dbg_rowsum", (128, NT)),
                ("dbg_vg", (NT, 128, N)), ("dbg_s", (DT, 128, N)),
                ("dbg_qp", (4, 128, N)), ("dbg_kp", (4, 128, N)),
                ("dbg_vt", (NT, 128, INNER)),
                ("dbg_oall", (4, 128, N)), ("dbg_xf", (DT, 128, N)),
                ("dbg_n2", (DT, 128, N)), ("dbg_h1", (KH, 128, N)),
                ("dbg_pb", (8, 128, 1024)), ("dbg_s2", (8, 128, 1024)),
                ("dbg_xf2", (DT, 128, N))]:
            if name in self.dbg:
                self.dbg_h[name] = self.nc.dram_tensor(
                    name, list(shape), F32, kind="ExternalOutput").ap()

    def dump_bf(self, name, src_fn, count, reads):
        hd = self.dbg_h.get(name)
        if hd is None:
            return
        for i in range(count):
            rid = f"{name}.{i}"
            self.p.dma("gpsimd", hd[i], src_fn(i), reads=reads, writes=(rid,),
                       tag=f"dbg.{name}{i}")
            self.out_res.append(rid)

    def dump_f32(self, name, src_fn, count, reads):
        hd = self.dbg_h.get(name)
        if hd is None:
            return
        for i in range(count):
            rid = f"{name}.{i}"
            self.p.dma("sync", hd[i], src_fn(i), reads=reads, writes=(rid,),
                       tag=f"dbg.{name}{i}")
            self.out_res.append(rid)

    def build(self):
        self.declare_io()
        nc = self.nc
        with ExitStack() as TOP:
            E = TOP.enter_context
            self.p = Prog(nc, TOP)
            p = self.p

            self.ABF = E(nc.sbuf_tensor("ABF", [128, DT, N], BF16))
            self.ident_bf = E(nc.sbuf_tensor("ident_bf", [128, 128], BF16))
            self.ident_f32 = E(nc.sbuf_tensor("ident_f32", [128, 128], F32))
            self.ones1 = E(nc.sbuf_tensor("ones1", [1, 128], BF16))
            self.rowsum = E(nc.sbuf_tensor("rowsum", [128, NT], F32))
            self.small = E(nc.sbuf_tensor("small", [128, 48], F32))
            self.bh_sb = E(nc.sbuf_tensor("bh_sb", [128, 32], F32))
            self.bout_sb = E(nc.sbuf_tensor("bout_sb", [128, 8], F32))
            self.lng_sb = E(nc.sbuf_tensor("lng_sb", [128, 8], F32))
            self.lnb_sb = E(nc.sbuf_tensor("lnb_sb", [128, 8], F32))
            self.epsv = E(nc.sbuf_tensor("epsv", [128, 1], F32))
            self.ones_col = E(nc.sbuf_tensor("ones_col", [128, 1], BF16))
            self.ones512 = E(nc.sbuf_tensor("ones512", [1, 512], BF16))
            self.e2t_sb = E(nc.sbuf_tensor("e2t_sb", [2, 128], F32))

            for nm, sb in [("identity_bf", self.ident_bf),
                           ("identity_f32", self.ident_f32),
                           ("ones1_bf", self.ones1), ("bh_t", self.bh_sb),
                           ("ones_col_bf", self.ones_col),
                           ("ones512_bf", self.ones512), ("e2t", self.e2t_sb),
                           ("bout_t", self.bout_sb), ("lng_t", self.lng_sb),
                           ("lnb_t", self.lnb_sb)]:
                p.dma("sync", sb[:], self.h[nm][:], writes=(nm,), tag="const")
            p.task("vector", lambda e: e.memset(self.epsv[:], EPS),
                   writes=("epsv",), tag="c.eps")

            with ExitStack() as SQG:
                self.QK = SQG.enter_context(
                    nc.sbuf_tensor("QK", [128, NT, QKD], BF16))
                self.GBC = SQG.enter_context(
                    nc.sbuf_tensor("GBC", [128, 4, QKD], BF16))
                with ExitStack() as SWQK:
                    self.WQK = SWQK.enter_context(
                        nc.sbuf_tensor("WQK", [128, DT, QKD], BF16))
                    self.phase0()
                    if self.stop_after != "p0":
                        self.attn_p1()
                if self.stop_after not in ("p0", "p1"):
                    self.attn_p2()
            if self.stop_after not in ("p0", "p1", "p2"):
                self.attn_p34()
                if self.stop_after not in ("p3", "p4"):
                    self.ffn_slot()

            p.wait_only("sync", self._res_deps(self.out_res), "drain")
            p.flush("drain")
        return nc, self.dbg_h

    def _res_deps(self, names):
        deps = {}
        for r in names:
            e = self.p.res.get(r)
            if e and e["w"] is not None:
                tok = e["w"]
                k, v = tok[:-1], tok[-1]
                if deps.get(k, 0) < v:
                    deps[k] = v
        return deps

    def phase0(self):
        nc, p = self.nc, self.p
        h = self.h
        with ExitStack() as S:
            XT = S.enter_context(nc.sbuf_tensor("XT", [128, 4, D], F32))
            XNT = S.enter_context(nc.sbuf_tensor("XNT", [128, 4, D], F32))
            NTB = S.enter_context(nc.sbuf_tensor("NTB", [128, 4, D], BF16))
            XNW = S.enter_context(nc.sbuf_tensor("XNW", [128, 2, DT, 128], F32))
            ST6 = S.enter_context(nc.sbuf_tensor("ST6", [128, 4, 12], F32))
            MV = S.enter_context(nc.sbuf_tensor("MV", [128, 4, 2], F32))
            pbf = S.enter_context(nc.psum_tensor("pbf", [128, 2, 1024], BF16))
            pxf = S.enter_context(nc.psum_tensor("pxf", [128, 2, 1024], F32))
            for k in range(DT):
                p.dma("gpsimd", self.WQK[:, k, :],
                      h["W_qk"][k * 128:(k + 1) * 128, :],
                      writes=(f"WQK{k}",), tag=f"p0.wqk{k}")
            small = self.small
            ABF = self.ABF

            for t in range(NT):
                b = t % 4
                bp = t % 2
                p.dma("sync", XT[:, b, :], h["x"][t * 128:(t + 1) * 128, :],
                      writes=(f"XT{b}",), tag=f"p0.load{t}")

                def stats(src_ap, src_res, rstd_col, tagp, b):
                    p.task("vector", lambda e, v=src_ap, b=b: (
                        e.bn_stats(ST6[:, b, 0:6], v[:, 0:512]),
                        e.bn_stats(ST6[:, b, 6:12], v[:, 512:1024]))[-1],
                        reads=(src_res,), writes=(f"ST6{b}",), tag=tagp + ".bn")
                    p.task("vector", lambda e, b=b: e.bn_aggr(MV[:, b, :], ST6[:, b, :]),
                           reads=(f"ST6{b}",), writes=(f"MV{b}",), tag=tagp + ".ag")
                    p.task("scalar", lambda e, b=b: e.activation(
                        small[:, 8 + b:9 + b], MV[:, b, 1:2], AF.Ln,
                        bias=self.epsv[:], scale=1.0),
                        reads=(f"MV{b}", "epsv"), writes=(f"lnv{b}",), tag=tagp + ".ln")
                    p.task("scalar", lambda e, rc=rstd_col, b=b: e.activation(
                        small[:, rc:rc + 1], small[:, 8 + b:9 + b], AF.Exp, scale=-0.5),
                        reads=(f"lnv{b}",), writes=(f"rstd{rstd_col}",), tag=tagp + ".ex")

                stats(XT[:, b, :], f"XT{b}", 16 + b, f"p0.s1_{t}", b)
                p.task("vector", lambda e, b=b: e.tensor_scalar(
                    XNT[:, b, :], XT[:, b, :], MV[:, b, 0:1], small[:, 16 + b:17 + b],
                    op0=ALU.subtract, op1=ALU.mult),
                    reads=(f"XT{b}", f"MV{b}", f"rstd{16 + b}"),
                    writes=(f"XNT{b}",), tag=f"p0.xn{t}")
                stats(XNT[:, b, :], f"XNT{b}", 20 + b, f"p0.s2_{t}", b)
                p.task("vector", lambda e, b=b: e.tensor_scalar(
                    NTB[:, b, :], XNT[:, b, :], MV[:, b, 0:1], small[:, 20 + b:21 + b],
                    op0=ALU.subtract, op1=ALU.mult),
                    reads=(f"XNT{b}", f"MV{b}", f"rstd{20 + b}"),
                    writes=(f"NTB{b}",), tag=f"p0.nt{t}")

                def tp_n(e, b=b, bp=bp):
                    for k in range(DT):
                        mm = e.transpose(pbf[:, bp, k * 128:(k + 1) * 128],
                                         NTB[:, b, k * 128:(k + 1) * 128],
                                         self.ident_bf[:])
                    return mm
                p.task("tensor", tp_n, reads=(f"NTB{b}", "identity_bf"),
                       writes=(f"pbf{bp}",), tag=f"p0.tpn{t}")

                def ev_n(e, t=t, bp=bp):
                    for k in range(DT):
                        mm = e.activation(
                            ABF[:, k, t * 128:(t + 1) * 128],
                            pbf[:, bp, k * 128:(k + 1) * 128],
                            AF.Identity, bias=self.lnb_sb[:, k:k + 1],
                            scale=self.lng_sb[:, k:k + 1])
                    return mm
                p.task("scalar", ev_n, reads=(f"pbf{bp}", "lng_t", "lnb_t"),
                       writes=(f"ABF.{t}",), tag=f"p0.evn{t}")

                def tp_x(e, b=b, bp=bp):
                    for k in range(DT):
                        mm = e.transpose(pxf[:, bp, k * 128:(k + 1) * 128],
                                         XNT[:, b, k * 128:(k + 1) * 128],
                                         self.ident_f32[:])
                    return mm
                p.task("tensor", tp_x, reads=(f"XNT{b}", "identity_f32"),
                       writes=(f"pxf{bp}",), tag=f"p0.tpx{t}")

                def ev_x(e, t=t, b=b, bp=bp):
                    return e.tensor_copy(
                        XNW[:, bp, :, :],
                        pxf[:, bp, :].rearrange("p (a b) -> p a b", b=128))
                p.task("vector", ev_x, reads=(f"pxf{bp}",),
                       writes=(f"XNW{bp}",), tag=f"p0.evx{t}")
                p.dma("gpsimd",
                      self.xn_scr.rearrange("k p n -> p k n")[:, :, t * 128:(t + 1) * 128],
                      XNW[:, bp, :, :],
                      reads=(f"XNW{bp}",), writes=(f"xn_scr.{t}",),
                      tag=f"p0.spill{t}")

            allxn = tuple(f"xn_scr.{t}" for t in range(NT))
            for k in range(DT):
                p.wait_only("sync", {}, "noop")
            if "dbg_xn" in self.dbg_h:
                for k in range(DT):
                    rid = f"dbg_xn.{k}"
                    p.dma("gpsimd", self.dbg_h["dbg_xn"][k], self.xn_scr[k],
                          reads=allxn, writes=(rid,), tag=rid)
                    self.out_res.append(rid)
            self.dump_bf("dbg_normed", lambda i: ABF[:, i, :], DT,
                         tuple(f"ABF.{t}" for t in range(NT)))
            p.flush("p0")

    def attn_p1(self):
        nc, p, h = self.nc, self.p, self.h
        ABF, QK, GBC, WQK = self.ABF, self.QK, self.GBC, self.WQK
        abf_res = tuple(f"ABF.{t}" for t in range(NT))
        with ExitStack() as S1a:
            GBR = S1a.enter_context(nc.sbuf_tensor("GBR", [1, 4 * QKD], BF16))
            BQR = S1a.enter_context(nc.sbuf_tensor("BQR", [1, QKD], BF16))
            pqk = S1a.enter_context(nc.psum_tensor("pqk", [128, 2, 512], F32))
            pgb = S1a.enter_context(nc.psum_tensor("pgb", [128, 2, 512], F32))

            p.dma("gpsimd", GBR[:], h["gb_row"][:], writes=("GBR",), tag="p1.gbr")
            p.dma("gpsimd", BQR[:], h["bqk_row"][:], writes=("BQR",), tag="p1.bqr")
            for r in range(4):
                for c in range(4):
                    pb = (r * 4 + c) % 2
                    p.task("tensor", lambda e, r=r, c=c, pb=pb: e.matmul(
                        pgb[:, pb, :], self.ones1[:],
                        GBR[:, r * QKD + c * 512:r * QKD + (c + 1) * 512],
                        start=True, stop=True),
                        reads=("GBR", "ones1_bf"), writes=(f"pgb{pb}",),
                        tag=f"p1.gbmm{r}.{c}")
                    p.task("vector", lambda e, r=r, c=c, pb=pb: e.tensor_copy(
                        GBC[:, r, c * 512:(c + 1) * 512], pgb[:, pb, :]),
                        reads=(f"pgb{pb}",), writes=(f"GBC{r}.{c}",),
                        tag=f"p1.gbev{r}.{c}")

            wqk_res = tuple(f"WQK{k}" for k in range(DT))
            for t in range(NT):
                for c in range(4):
                    pb = (t * 4 + c) % 2
                    def qkmm(e, t=t, c=c, pb=pb):
                        for k in range(DT):
                            e.matmul(pqk[:, pb, :],
                                     ABF[:, k, t * 128:(t + 1) * 128],
                                     WQK[:, k, c * 512:(c + 1) * 512],
                                     start=(k == 0), stop=False)
                        return e.matmul(pqk[:, pb, :], self.ones1[:],
                                        BQR[:, c * 512:(c + 1) * 512],
                                        start=False, stop=True)
                    p.task("tensor", qkmm,
                           reads=abf_res + wqk_res + ("BQR", "ones1_bf"),
                           writes=(f"pqk{pb}",), tag=f"p1.mm{t}.{c}")
                    p.task("scalar", lambda e, t=t, c=c, pb=pb: e.activation(
                        QK[:, t, c * 512:(c + 1) * 512], pqk[:, pb, :],
                        AF.Silu),
                        reads=(f"pqk{pb}",), writes=(f"QK{t}.{c}",),
                        tag=f"p1.silu{t}.{c}")
            qk_res = tuple(f"QK{t}.{c}" for t in range(NT) for c in range(4))
            self.dump_bf("dbg_qk", lambda i: QK[:, i, :], NT, qk_res)
            p.flush("p1")

    def attn_p2(self):
        nc, p, h = self.nc, self.p, self.h
        QK, GBC = self.QK, self.GBC
        gbc_res = tuple(f"GBC{r}.{c}" for r in range(4) for c in range(4))
        small = self.small
        with ExitStack() as S2:
            KF = S2.enter_context(nc.sbuf_tensor("KF", [128, NT, QKD], BF16))
            QT2 = S2.enter_context(nc.sbuf_tensor("QT2", [128, 2, NT, 256], BF16))
            EB = S2.enter_context(nc.sbuf_tensor("EB", [128, QKD], BF16))
            psim = S2.enter_context(nc.psum_tensor("psim", [128, 2, QKD], F32))

            for t in range(NT):
                p.task("vector", lambda e, t=t: e.tensor_tensor(
                    KF[:, t, :], QK[:, t, :], GBC[:, 2, :], op=ALU.mult),
                    reads=(f"QK{t}.0", f"QK{t}.1", f"QK{t}.2", f"QK{t}.3")
                    + gbc_res,
                    writes=(f"KF{t}",), tag=f"p2.km{t}")
                p.task("vector", lambda e, t=t: e.tensor_tensor(
                    KF[:, t, :], KF[:, t, :], GBC[:, 3, :], op=ALU.add),
                    reads=(f"KF{t}",), writes=(f"KF{t}",), tag=f"p2.ka{t}")
            kf_res = tuple(f"KF{t}" for t in range(NT))

            def emit_qprep(dp):
                qp = dp % 2
                for kt in range(NT):
                    p.task("vector", lambda e, dp=dp, kt=kt, qp=qp: e.tensor_tensor(
                        QT2[:, qp, kt, :], QK[:, kt, dp * 256:(dp + 1) * 256],
                        GBC[:, 0, dp * 256:(dp + 1) * 256], op=ALU.mult),
                        reads=(f"QK{kt}.{dp // 2}",) + gbc_res,
                        writes=(f"QT2{qp}.{kt}",), tag=f"p2.qm{dp}.{kt}")
                    p.task("vector", lambda e, dp=dp, kt=kt, qp=qp: e.tensor_tensor(
                        QT2[:, qp, kt, :], QT2[:, qp, kt, :],
                        GBC[:, 1, dp * 256:(dp + 1) * 256], op=ALU.add),
                        reads=(f"QT2{qp}.{kt}",), writes=(f"QT2{qp}.{kt}",),
                        tag=f"p2.qa{dp}.{kt}")

            emit_qprep(0)
            for dp in range(8):
                qp = dp % 2
                if dp + 1 < 8:
                    emit_qprep(dp + 1)
                qt_res = tuple(f"QT2{qp}.{kt}" for kt in range(NT))
                for dt in range(dp * 2, dp * 2 + 2):
                    pb = dt % 2
                    dl = (dt % 2) * 128
                    def simmm(e, dt=dt, pb=pb, dl=dl, qp=qp):
                        for c in range(4):
                            for kt in range(NT):
                                mm = e.matmul(
                                    psim[:, pb, c * 512:(c + 1) * 512],
                                    QT2[:, qp, kt, dl:dl + 128],
                                    KF[:, kt, c * 512:(c + 1) * 512],
                                    start=(kt == 0), stop=(kt == NT - 1))
                        return mm
                    p.task("tensor", simmm, reads=qt_res + kf_res,
                           writes=(f"psim{pb}",), tag=f"p2.sim{dt}")
                    sb = pb * 4
                    p.task("vector", lambda e, pb=pb, sb=sb: e.tensor_reduce(
                        small[:, sb:sb + 1], psim[:, pb, :], axis=AX.X, op=ALU.max),
                        reads=(f"psim{pb}",), writes=(f"p2max{pb}",),
                        tag=f"p2.max{dt}")
                    p.task("vector", lambda e, sb=sb: e.tensor_scalar(
                        small[:, sb + 1:sb + 2], small[:, sb:sb + 1], -1.0 / N,
                        None, op0=ALU.mult),
                        reads=(f"p2max{pb}",), writes=(f"p2bias{pb}",),
                        tag=f"p2.b{dt}")
                    p.task("scalar", lambda e, pb=pb, sb=sb: e.activation(
                        EB[:, :], psim[:, pb, :], AF.Exp,
                        bias=small[:, sb + 1:sb + 2], scale=1.0 / N,
                        accum_out=small[:, sb + 2:sb + 3]),
                        reads=(f"psim{pb}", f"p2bias{pb}"),
                        writes=("EB0", f"p2z{pb}"), tag=f"p2.exp{dt}")
                    p.task("vector", lambda e, sb=sb: e.reciprocal(
                        small[:, sb + 3:sb + 4], small[:, sb + 2:sb + 3]),
                        reads=(f"p2z{pb}",), writes=(f"p2rz{pb}",),
                        tag=f"p2.rz{dt}")
                    p.task("vector", lambda e, dt=dt, sb=sb: e.tensor_tensor(
                        self.rowsum[:, dt:dt + 1], small[:, sb + 2:sb + 3],
                        small[:, sb + 3:sb + 4], op=ALU.mult),
                        reads=(f"p2z{pb}", f"p2rz{pb}"),
                        writes=(f"rowsum{dt}",), tag=f"p2.rs{dt}")
            if "dbg_rowsum" in self.dbg_h:
                p.dma("sync", self.dbg_h["dbg_rowsum"][:], self.rowsum[:],
                      reads=tuple(f"rowsum{dt}" for dt in range(NT)),
                      writes=("dbg_rowsum.0",), tag="dbg.rowsum")
                self.out_res.append("dbg_rowsum.0")
            p.flush("p2")

    def attn_p34(self):
        nc, p, h = self.nc, self.p, self.h
        ABF = self.ABF
        abf_res = tuple(f"ABF.{t}" for t in range(NT))
        with ExitStack() as S3:
            VG = S3.enter_context(nc.sbuf_tensor("VG", [128, NT, N], BF16))
            with ExitStack() as S3a:
                WH = S3a.enter_context(
                    nc.sbuf_tensor("WH", [128, DT, 2 * HIDDEN], BF16))
                VT = S3a.enter_context(nc.sbuf_tensor("VT", [128, 2, 2048], BF16))
                GT = S3a.enter_context(nc.sbuf_tensor("GT", [128, 2, 2048], BF16))
                pv = S3a.enter_context(nc.psum_tensor("pv", [128, 2, 512], F32))
                pg = S3a.enter_context(nc.psum_tensor("pg", [128, 2, 512], F32))
                for k in range(DT):
                    p.dma("gpsimd", WH[:, k, :],
                          h["W_hidden"][k * 128:(k + 1) * 128, :],
                          writes=(f"WH{k}",), tag=f"p3.wh{k}")
                wh_res = tuple(f"WH{k}" for k in range(DT))
                for hp in range(NT):
                    pb = hp % 2
                    for c in range(4):
                        sc = c % 2
                        def vmm(e, hp=hp, c=c, sc=sc):
                            for k in range(DT):
                                mm = e.matmul(
                                    pv[:, sc, :],
                                    WH[:, k, hp * 128:(hp + 1) * 128],
                                    ABF[:, k, c * 512:(c + 1) * 512],
                                    start=(k == 0), stop=(k == DT - 1))
                            return mm
                        p.task("tensor", vmm, reads=abf_res + wh_res,
                               writes=(f"pv{sc}",), tag=f"p3.vmm{hp}.{c}")
                        def gmm(e, hp=hp, c=c, sc=sc):
                            for k in range(DT):
                                mm = e.matmul(
                                    pg[:, sc, :],
                                    WH[:, k, HIDDEN + hp * 128:
                                       HIDDEN + (hp + 1) * 128],
                                    ABF[:, k, c * 512:(c + 1) * 512],
                                    start=(k == 0), stop=(k == DT - 1))
                            return mm
                        p.task("tensor", gmm, reads=abf_res + wh_res,
                               writes=(f"pg{sc}",), tag=f"p3.gmm{hp}.{c}")
                        p.task("scalar", lambda e, hp=hp, c=c, sc=sc, pb=pb:
                               e.activation(
                                   VT[:, pb, c * 512:(c + 1) * 512], pv[:, sc, :],
                                   AF.Silu, bias=self.bh_sb[:, hp:hp + 1]),
                               reads=(f"pv{sc}", "bh_t"),
                               writes=(f"VT{pb}.{c}",), tag=f"p3.sv{hp}.{c}")
                        p.task("scalar", lambda e, hp=hp, c=c, sc=sc, pb=pb:
                               e.activation(
                                   GT[:, pb, c * 512:(c + 1) * 512], pg[:, sc, :],
                                   AF.Silu, bias=self.bh_sb[:, 16 + hp:17 + hp]),
                               reads=(f"pg{sc}", "bh_t"),
                               writes=(f"GT{pb}.{c}",), tag=f"p3.sg{hp}.{c}")
                    p.task("vector", lambda e, hp=hp, pb=pb: e.tensor_tensor(
                        VG[:, hp, :], VT[:, pb, :], GT[:, pb, :], op=ALU.mult),
                        reads=tuple(f"VT{pb}.{c}" for c in range(4))
                        + tuple(f"GT{pb}.{c}" for c in range(4)),
                        writes=(f"VGm{hp}",), tag=f"p3.vg{hp}")
                    p.task("vector", lambda e, hp=hp: e.tensor_scalar(
                        VG[:, hp, :], VG[:, hp, :],
                        self.rowsum[:, hp:hp + 1], None, op0=ALU.mult),
                        reads=(f"VGm{hp}", f"rowsum{hp}"),
                        writes=(f"VG{hp}",), tag=f"p3.vgr{hp}")
                vg_res = tuple(f"VG{hp}" for hp in range(NT))
                self.dump_bf("dbg_vg", lambda i: VG[:, i, :], NT, vg_res)
                p.flush("p3")
            if self.stop_after == "p3":
                return

            with ExitStack() as S4:
                WO = S4.enter_context(nc.sbuf_tensor("WO", [128, NT, D], BF16))
                XNB = S4.enter_context(nc.sbuf_tensor("XNB", [128, 2, N], F32))
                SB = S4.enter_context(nc.sbuf_tensor("SB", [128, 2, N], F32))
                ps = S4.enter_context(nc.psum_tensor("ps", [128, 2, 512], F32))
                for k in range(NT):
                    p.dma("gpsimd", WO[:, k, :],
                          h["W_out"][k * 128:(k + 1) * 128, :],
                          writes=(f"WO{k}",), tag=f"p4.wo{k}")
                wo_res = tuple(f"WO{k}" for k in range(NT))
                for mo in range(DT):
                    b = mo % 2
                    p.dma("sync", XNB[:, b, :], self.xn_scr[mo],
                          reads=tuple(f"xn_scr.{t}" for t in range(NT)),
                          writes=(f"XNB{b}",), tag=f"p4.xn{mo}")
                    for c in range(4):
                        pb = c % 2
                        def smm(e, mo=mo, c=c, pb=pb):
                            for k in range(NT):
                                mm = e.matmul(
                                    ps[:, pb, :],
                                    WO[:, k, mo * 128:(mo + 1) * 128],
                                    VG[:, k, c * 512:(c + 1) * 512],
                                    start=(k == 0), stop=(k == NT - 1))
                            return mm
                        p.task("tensor", smm, reads=vg_res + wo_res,
                               writes=(f"ps{pb}",), tag=f"p4.mm{mo}.{c}")
                        p.task("scalar", lambda e, c=c, pb=pb, b=b, mo=mo:
                               e.activation(
                                   SB[:, b, c * 512:(c + 1) * 512], ps[:, pb, :],
                                   AF.Identity, bias=self.bout_sb[:, mo:mo + 1]),
                               reads=(f"ps{pb}", "bout_t"), writes=(f"SB{b}",),
                               tag=f"p4.ev{mo}.{c}")
                    p.task("vector", lambda e, b=b: e.tensor_tensor(
                        SB[:, b, :], SB[:, b, :], XNB[:, b, :], op=ALU.add),
                        reads=(f"XNB{b}", f"SB{b}"),
                        writes=(f"SB{b}",), tag=f"p4.add{mo}")
                    p.dma("gpsimd", self.s_scr[mo], SB[:, b, :],
                          reads=(f"SB{b}",), writes=(f"s_scr{mo}",),
                          tag=f"p4.spill{mo}")
                    if "dbg_s" in self.dbg_h:
                        rid = f"dbg_s.{mo}"
                        p.dma("gpsimd", self.dbg_h["dbg_s"][mo], SB[:, b, :],
                              reads=(f"SB{b}",), writes=(rid,), tag=rid)
                        self.out_res.append(rid)
                p.flush("p4")

    def ffn_slot(self):
        nc, p, h = self.nc, self.p, self.h
        with ExitStack() as SF:
            XF = SF.enter_context(nc.sbuf_tensor("XF", [128, DT, N], F32))
            ROTT = SF.enter_context(nc.sbuf_tensor("ROTT", [128, 128], BF16))
            E2C = SF.enter_context(nc.sbuf_tensor("E2C", [128, 2], BF16))
            self.XF = XF
            self.ROTTsb, self.E2Csb = ROTT, E2C
            for k in range(DT):
                p.dma("sync", XF[:, k, :], self.xn_scr[k],
                      reads=tuple(f"xn_scr.{t}" for t in range(NT)),
                      writes=(f"XF{k}",), tag=f"pf.xf{k}")
            p.dma("sync", ROTT[:], h["rotT_bf"][:], writes=("ROTT",), tag="pf.rt")
            p.dma("sync", E2C[:], h["e2c_bf"][:], writes=("E2C",), tag="pf.e2c")
            for l in range(2):
                if self.ffn_layer(l):
                    return
            if self.stop_after is None or self.stop_after == "final":
                self.final_phase()

    def ffn_layer(self, l):
        nc, p, h = self.nc, self.p, self.h
        ABF, XF = self.ABF, self.XF
        sa = self.stop_after or ""
        xf_res = tuple(f"XF{k}" for k in range(DT))

        for k in range(DT):
            if k < 4:
                p.task("vector", lambda e, k=k: e.tensor_copy(
                    ABF[:, k, :], XF[:, k, :]),
                    reads=(f"XF{k}",), writes=(f"ABF.{l}{k}",), tag=f"f{l}.sh{k}")
            else:
                def shift(e, k=k):
                    e.memset(ABF[:, k, 0:1], 0.0)
                    return e.tensor_copy(ABF[:, k, 1:N], XF[:, k, 0:N - 1])
                p.task("vector", shift, reads=(f"XF{k}",),
                       writes=(f"ABF.{l}{k}",), tag=f"f{l}.sh{k}")
        abf_res = tuple(f"ABF.{l}{k}" for k in range(DT))

        with ExitStack() as SL:
            QP = SL.enter_context(nc.sbuf_tensor(f"QP{l}", [128, 4, N], BF16))
            KP = SL.enter_context(nc.sbuf_tensor(f"KP{l}", [128, 4, N], BF16))
            VTM = SL.enter_context(nc.sbuf_tensor(f"VTM{l}", [128, NT, INNER], BF16))

            with ExitStack() as S:
                ROT = S.enter_context(nc.sbuf_tensor(f"ROT{l}", [128, 2, N], BF16))
                WQKV = S.enter_context(
                    nc.sbuf_tensor(f"WQKV{l}", [128, DT, 512], BF16))
                QW = S.enter_context(nc.sbuf_tensor(f"QW{l}", [128, 3, N], BF16))
                SQ = S.enter_context(nc.sbuf_tensor(f"SQb{l}", [128, 2, 512], BF16))
                T1 = S.enter_context(nc.sbuf_tensor(f"T1{l}", [128, 2, 512], BF16))
                T2 = None
                VF = S.enter_context(nc.sbuf_tensor(f"VF{l}", [128, 2, N], BF16))
                ROWS = S.enter_context(nc.sbuf_tensor(f"ROWS{l}", [2, 2, 512], F32))
                RSTDB = S.enter_context(nc.sbuf_tensor(f"RSTDB{l}", [2, 2, N], BF16))
                ESC = S.enter_context(nc.sbuf_tensor(f"ESC{l}", [2, 2, 128], BF16))
                QSL = S.enter_context(nc.sbuf_tensor(f"QSL{l}", [2, 2, 128], BF16))
                pqkv = S.enter_context(nc.psum_tensor(f"pqkv{l}", [128, 2, 512], F32))
                psq = S.enter_context(nc.psum_tensor(f"psq{l}", [2, 2, 512], F32))
                paux = S.enter_context(nc.psum_tensor(f"paux{l}", [128, 2, 512], F32))
                ptv = S.enter_context(nc.psum_tensor(f"ptv{l}", [128, 2, 1024], BF16))
                self.ROTsb = ROT

                for r in range(2):
                    p.dma("sync", ROT[:, r, :], h["rotary"][r],
                          writes=(f"ROT{r}",), tag=f"f{l}.rotq{r}")
                p.dma("gpsimd", QSL[:, 0, :], h["qs_tile"][l], writes=("QSL0",),
                      tag=f"f{l}.qs")
                p.dma("gpsimd", QSL[:, 1, :], h["ks_tile"][l], writes=("QSL1",),
                      tag=f"f{l}.ks")
                for r in range(2):
                    p.task("vector", lambda e, r=r: e.tensor_tensor(
                        ESC[:, r, :], self.e2t_sb[:], QSL[:, r, :], op=ALU.mult),
                        reads=(f"QSL{r}", "e2t"), writes=(f"ESC{r}",),
                        tag=f"f{l}.esc{r}")

                def post_work(m):
                    pb = m % 2
                    if m < 8:
                        self._qk_chain(l, m, pb, QP if m < 4 else KP,
                                       QW, SQ, T1, T2, ROWS, RSTDB, ESC,
                                       psq, paux)
                        return
                    vm = m - 8
                    vf_res = tuple(f"VF{pb}.{c}" for c in range(4))
                    for jg in range(4):
                        sg = jg % 2
                        def vtp(e, vm=vm, jg=jg, sg=sg, pb=pb):
                            for jj in range(4):
                                jc = jg * 4 + jj
                                mm = e.transpose(
                                    ptv[:, sg, jj * 128:(jj + 1) * 128],
                                    VF[:, pb, jc * 128:(jc + 1) * 128],
                                    self.ident_bf[:])
                            return mm
                        p.task("tensor", vtp,
                               reads=vf_res + ("identity_bf",),
                               writes=(f"ptv{sg}",),
                               tag=f"f{l}.vtp{vm}.{jg}")
                        def vev(e, vm=vm, jg=jg, sg=sg):
                            return e.tensor_copy(
                                VTM[:, jg * 4:(jg + 1) * 4,
                                    vm * 128:(vm + 1) * 128],
                                ptv[:, sg, 0:512].rearrange(
                                    "p (jj c) -> p jj c", jj=4))
                        p.task("vector", vev, reads=(f"ptv{sg}",),
                               writes=(f"VTM{vm}.{jg}",),
                               tag=f"f{l}.vevt{vm}.{jg}")

                pq = []
                for stage in range(3):
                    for k in range(DT):
                        p.dma("gpsimd", WQKV[:, k, :],
                              h["wqkv"][l][k * 128:(k + 1) * 128,
                                           stage * 512:(stage + 1) * 512],
                              writes=(f"WQKV{k}",), tag=f"f{l}.wqkv{stage}.{k}")
                    wqkv_res = tuple(f"WQKV{k}" for k in range(DT))
                    for ms in range(4):
                        m = stage * 4 + ms
                        pb = m % 2
                        for c in range(4):
                            sc = c % 2
                            def qkvmm(e, ms=ms, c=c, sc=sc):
                                for k in range(DT):
                                    mm = e.matmul(
                                        pqkv[:, sc, :],
                                        WQKV[:, k, ms * 128:(ms + 1) * 128],
                                        ABF[:, k, c * 512:(c + 1) * 512],
                                        start=(k == 0), stop=(k == DT - 1))
                                return mm
                            p.task("tensor", qkvmm, reads=abf_res + wqkv_res,
                                   writes=(f"pqkv{sc}",), tag=f"f{l}.qkv{m}.{c}")
                            if m < 8:
                                q3 = m % 3
                                p.task("scalar", lambda e, c=c, sc=sc, q3=q3:
                                       e.activation(
                                           QW[:, q3, c * 512:(c + 1) * 512],
                                           pqkv[:, sc, :], AF.Identity),
                                       reads=(f"pqkv{sc}",),
                                       writes=(f"QW{q3}.{c}",),
                                       tag=f"f{l}.qev{m}.{c}")
                            else:
                                p.task("vector", lambda e, c=c, sc=sc, pb=pb:
                                       e.tensor_copy(
                                           VF[:, pb, c * 512:(c + 1) * 512],
                                           pqkv[:, sc, :]),
                                       reads=(f"pqkv{sc}",), writes=(f"VF{pb}.{c}",),
                                       tag=f"f{l}.vev{m}.{c}")
                        pq.append(m)
                        while pq and m >= pq[0] + (2 if pq[0] < 8 else 1):
                            m0 = pq.pop(0)
                            post_work(m0)
                            if m0 == 3:
                                for r in range(2):
                                    p.dma("sync", ROT[:, r, :], h["rotary"][2 + r],
                                          writes=(f"ROT{r}",), tag=f"f{l}.rotk{r}")
                for m0 in pq:
                    post_work(m0)
                qp_res = tuple(f"QPf{m}" for m in range(4))
                kp_res = tuple(f"KPf{m}" for m in range(4))
                vtm_res = tuple(f"VTM{vm}.{jg}" for vm in range(4) for jg in range(4))
                if l == 0:
                    self.dump_bf("dbg_qp", lambda i: QP[:, i, :], 4, qp_res)
                    self.dump_bf("dbg_kp", lambda i: KP[:, i, :], 4, kp_res)
                    self.dump_bf("dbg_vt", lambda i: VTM[:, i, :], NT, vtm_res)
                p.flush(f"f{l}.qkv")
            if sa == f"f{l}qkv":
                return True

            oa_res = tuple(f"OA{hp}.{w}" for hp in range(4) for w in range(NW))
            with ExitStack() as S34:
                OA = S34.enter_context(nc.sbuf_tensor(f"OA{l}", [128, 4, N], BF16))
                with ExitStack() as S3:
                    PB = S3.enter_context(
                        nc.sbuf_tensor(f"PB{l}", [128, 2, 2, 1024], BF16))
                    PT = S3.enter_context(
                        nc.sbuf_tensor(f"PT{l}", [128, 2, 8, 512], BF16))
                    MSK = S3.enter_context(
                        nc.sbuf_tensor(f"MSK{l}", [128, 4, 1024], BF16))
                    self.MSKsb = MSK
                    for r in range(4):
                        p.dma("sync", MSK[:, r, :], h["mask1"][r],
                              writes=(f"MSK{r}",), tag=f"f{l}.msk{r}")
                    pS = S3.enter_context(
                        nc.psum_tensor(f"pS{l}", [128, 2, 1024], F32))
                    pPT = S3.enter_context(
                        nc.psum_tensor(f"pPT{l}", [128, 2, 1024], BF16))
                    pO = S3.enter_context(
                        nc.psum_tensor(f"pO{l}", [128, 2, 512], F32))
                    if "dbg_s2" in self.dbg_h:
                        self.SDUMP = S3.enter_context(
                            nc.sbuf_tensor(f"SDUMP{l}", [128, 1024], F32))
                    self._local_attn(l, QP, KP, VTM, OA, PB, PT, pS, pPT, pO)
                    if l == 0:
                        self.dump_bf("dbg_oall", lambda i: OA[:, i, :], 4, oa_res)
                    p.flush(f"f{l}.la")
                if sa == f"f{l}la":
                    return True

                with ExitStack() as S3b:
                    WOF = S3b.enter_context(
                        nc.sbuf_tensor(f"WOF{l}", [128, 4, D], BF16))
                    pwo = S3b.enter_context(
                        nc.psum_tensor(f"pwo{l}", [128, 2, 512], F32))
                    for k in range(4):
                        p.dma("gpsimd", WOF[:, k, :],
                              h["wo"][l][k * 128:(k + 1) * 128, :],
                              writes=(f"WOF{k}",), tag=f"f{l}.wo{k}")
                    wof_res = tuple(f"WOF{k}" for k in range(4))
                    for mo in range(DT):
                        for c in range(4):
                            pb = (mo * 4 + c) % 2
                            def womm(e, mo=mo, c=c, pb=pb):
                                for k in range(4):
                                    mm = e.matmul(
                                        pwo[:, pb, :],
                                        WOF[:, k, mo * 128:(mo + 1) * 128],
                                        OA[:, k, c * 512:(c + 1) * 512],
                                        start=(k == 0), stop=(k == 3))
                                return mm
                            p.task("tensor", womm, reads=oa_res + wof_res,
                                   writes=(f"pwo{pb}",), tag=f"f{l}.womm{mo}.{c}")
                            p.task("vector", lambda e, mo=mo, c=c, pb=pb:
                                   e.tensor_tensor(
                                       XF[:, mo, c * 512:(c + 1) * 512],
                                       XF[:, mo, c * 512:(c + 1) * 512],
                                       pwo[:, pb, :], op=ALU.add),
                                   reads=(f"pwo{pb}", f"XF{mo}"),
                                   writes=(f"XF{mo}",),
                                   tag=f"f{l}.woadd{mo}.{c}")
                    if l == 0:
                        self.dump_f32("dbg_xf", lambda i: XF[:, i, :], DT, xf_res)
                    p.flush(f"f{l}.attn")
            if sa == f"f{l}attn":
                return True

        with ExitStack() as S5:
            N2 = S5.enter_context(nc.sbuf_tensor(f"N2{l}", [128, DT, N], BF16))
            self.N2 = N2
            with ExitStack() as S5a:
                TDIF = S5a.enter_context(
                    nc.sbuf_tensor(f"TDIF{l}", [128, DT, N], BF16))
                MUB = S5a.enter_context(nc.sbuf_tensor(f"MUB{l}", [128, 2, N], BF16))
                SQS = S5a.enter_context(nc.sbuf_tensor(f"SQS{l}", [128, 2, 512], BF16))
                MRB = S5a.enter_context(nc.sbuf_tensor(f"MRB{l}", [1, 2, N], BF16))
                LNF = S5a.enter_context(nc.sbuf_tensor(f"LNF{l}", [1, N], F32))
                T2B = S5a.enter_context(nc.sbuf_tensor(f"T2B{l}", [128, 2, N], BF16))
                FLG = S5a.enter_context(nc.sbuf_tensor(f"FLG{l}", [128, 8], F32))
                FLB = S5a.enter_context(nc.sbuf_tensor(f"FLB{l}", [128, 8], F32))
                pst = S5a.enter_context(nc.psum_tensor(f"pst{l}", [1, 2, 512], F32))
                pex2 = S5a.enter_context(nc.psum_tensor(f"pex2{l}", [128, 2, 512], F32))
                p.dma("sync", FLG[:], h["flg_t"][l], writes=("FLG",), tag=f"f{l}.flg")
                p.dma("sync", FLB[:], h["flb_t"][l], writes=("FLB",), tag=f"f{l}.flb")
                for k in range(DT):
                    if k < 4:
                        p.task("vector", lambda e, k=k: e.tensor_copy(
                            ABF[:, k, :], XF[:, k, :]),
                            reads=(f"XF{k}",), writes=(f"ABF.{l}{k}",),
                            tag=f"f{l}.sh2{k}")
                    else:
                        def shift2(e, k=k):
                            e.memset(ABF[:, k, 0:1], 0.0)
                            return e.tensor_copy(ABF[:, k, 1:N], XF[:, k, 0:N - 1])
                        p.task("vector", shift2, reads=(f"XF{k}",),
                               writes=(f"ABF.{l}{k}",), tag=f"f{l}.sh2{k}")
                abf_res2 = tuple(f"ABF.{l}{k}" for k in range(DT))
                for c in range(4):
                    sc = c % 2
                    def sxmm(e, c=c, sc=sc):
                        for k in range(DT):
                            mm = e.matmul(pst[:, sc, :], self.ones_col[:],
                                          ABF[:, k, c * 512:(c + 1) * 512],
                                          start=(k == 0), stop=(k == DT - 1))
                        return mm
                    p.task("tensor", sxmm, reads=abf_res2 + ("ones_col_bf",),
                           writes=(f"pst{sc}",), tag=f"f{l}.sx{c}")
                    p.task("scalar", lambda e, c=c, sc=sc: e.activation(
                        MRB[:, 0, c * 512:(c + 1) * 512], pst[:, sc, :],
                        AF.Identity, scale=1.0 / D),
                        reads=(f"pst{sc}",), writes=(f"MRBmu.{c}",),
                        tag=f"f{l}.sxe{c}")
                mu_res = tuple(f"MRBmu.{c}" for c in range(4))
                for c in range(4):
                    pb = c % 2
                    p.task("tensor", lambda e, c=c, pb=pb: e.matmul(
                        pex2[:, pb, :], self.ones1[:],
                        MRB[:, 0, c * 512:(c + 1) * 512], start=True, stop=True),
                        reads=mu_res + ("ones1_bf",), writes=(f"pex2{pb}",),
                        tag=f"f{l}.mex{c}")
                    p.task("vector", lambda e, c=c, pb=pb: e.tensor_copy(
                        MUB[:, 0, c * 512:(c + 1) * 512], pex2[:, pb, :]),
                        reads=(f"pex2{pb}",), writes=(f"MUB0.{c}",),
                        tag=f"f{l}.mexe{c}")
                mub0_res = tuple(f"MUB0.{c}" for c in range(4))
                for k in range(DT):
                    p.task("vector", lambda e, k=k: e.tensor_tensor(
                        TDIF[:, k, :], ABF[:, k, :], MUB[:, 0, :], op=ALU.subtract),
                        reads=(f"ABF.{l}{k}",) + mub0_res, writes=(f"TDIF{k}",),
                        tag=f"f{l}.td{k}")
                tdif_res = tuple(f"TDIF{k}" for k in range(DT))
                for c in range(4):
                    sc = c % 2
                    for k in range(DT):
                        p.task("vector", lambda e, k=k, c=c: e.tensor_tensor(
                            SQS[:, k % 2, :], TDIF[:, k, c * 512:(c + 1) * 512],
                            TDIF[:, k, c * 512:(c + 1) * 512], op=ALU.mult),
                            reads=(f"TDIF{k}",), writes=(f"SQS{k % 2}",),
                            tag=f"f{l}.sq{c}.{k}")
                        p.task("tensor", lambda e, k=k, c=c, sc=sc: e.matmul(
                            pst[:, sc, :], self.ones_col[:], SQS[:, k % 2, :],
                            start=(k == 0), stop=(k == DT - 1)),
                            reads=(f"SQS{k % 2}", "ones_col_bf"),
                            writes=(f"pst{sc}",),
                            tag=f"f{l}.vmm{c}.{k}")
                    p.task("scalar", lambda e, c=c, sc=sc: e.activation(
                        LNF[:, c * 512:(c + 1) * 512], pst[:, sc, :], AF.Ln,
                        scale=1.0 / D, bias=self.epsv[0:1, :]),
                        reads=(f"pst{sc}", "epsv"), writes=(f"LNF.{c}",),
                        tag=f"f{l}.vln{c}")
                p.task("scalar", lambda e: e.activation(
                    MRB[:, 1, :], LNF[:], AF.Exp, scale=-0.5),
                    reads=tuple(f"LNF.{c}" for c in range(4)),
                    writes=("MRBrs",), tag=f"f{l}.vex")
                for c in range(4):
                    pb = c % 2
                    p.task("tensor", lambda e, c=c, pb=pb: e.matmul(
                        pex2[:, pb, :], self.ones1[:],
                        MRB[:, 1, c * 512:(c + 1) * 512], start=True, stop=True),
                        reads=("MRBrs", "ones1_bf"), writes=(f"pex2{pb}",),
                        tag=f"f{l}.rex{c}")
                    p.task("vector", lambda e, c=c, pb=pb: e.tensor_copy(
                        MUB[:, 1, c * 512:(c + 1) * 512], pex2[:, pb, :]),
                        reads=(f"pex2{pb}",), writes=(f"MUB1.{c}",),
                        tag=f"f{l}.rexe{c}")
                mub1_res = tuple(f"MUB1.{c}" for c in range(4))
                for k in range(DT):
                    pb = k % 2
                    p.task("vector", lambda e, k=k, pb=pb: e.tensor_tensor(
                        T2B[:, pb, :], TDIF[:, k, :], MUB[:, 1, :], op=ALU.mult),
                        reads=(f"TDIF{k}",) + mub1_res, writes=(f"T2B{pb}",),
                        tag=f"f{l}.n2m{k}")
                    p.task("scalar", lambda e, k=k, pb=pb: e.activation(
                        N2[:, k, :], T2B[:, pb, :], AF.Identity,
                        bias=FLB[:, k:k + 1], scale=FLG[:, k:k + 1]),
                        reads=(f"T2B{pb}", "FLG", "FLB"), writes=(f"N2.{k}",),
                        tag=f"f{l}.n2f{k}")
                n2_res = tuple(f"N2.{k}" for k in range(DT))
                if l == 0:
                    self.dump_bf("dbg_n2", lambda i: N2[:, i, :], DT, n2_res)
                p.flush(f"f{l}.ln")
            if sa == f"f{l}ln":
                return True

            with ExitStack() as S6:
                W1B = S6.enter_context(
                    nc.sbuf_tensor(f"W1B{l}", [128, 2, DT, 256], BF16))
                W2B = S6.enter_context(
                    nc.sbuf_tensor(f"W2B{l}", [128, 2, KH, 128], BF16))
                PROD = S6.enter_context(
                    nc.sbuf_tensor(f"PROD{l}", [128, KH, 1024], BF16))
                GEB = S6.enter_context(nc.sbuf_tensor(f"GEB{l}", [128, 2, 1024], BF16))
                B1AR = S6.enter_context(nc.sbuf_tensor(f"B1AR{l}", [1, HIDP], BF16))
                B1G = S6.enter_context(nc.sbuf_tensor(f"B1G{l}", [128, KH], F32))
                B2R = S6.enter_context(nc.sbuf_tensor(f"B2R{l}", [1, D], BF16))
                self.b1g_sb = B1G
                p.dma("gpsimd", B1AR[:], h["b1a_row"][l], writes=("B1AR",),
                      tag=f"f{l}.b1ar")
                p.dma("sync", B1G[:], h["b1g_t"][l], writes=("B1G",),
                      tag=f"f{l}.b1g")
                p.dma("gpsimd", B2R[:], h["b2_row"][l], writes=("B2R",),
                      tag=f"f{l}.b2r")
                prod_res = tuple(f"PROD.{m}" for m in range(KH))
                for nh in range(2):
                    nh0 = nh * 1024
                    with ExitStack() as S6a:
                        pa = S6a.enter_context(
                            nc.psum_tensor(f"pa{l}{nh}", [128, 2, 1024], F32))
                        pg2 = S6a.enter_context(
                            nc.psum_tensor(f"pg2{l}{nh}", [128, 2, 1024], F32))
                        def w1dma(m):
                            p.dma("gpsimd", W1B[:, m % 2, :, :],
                                  h["w1ag"][l][m].rearrange("(k p) c -> p k c",
                                                            p=128),
                                  writes=(f"W1B{m % 2}",), tag=f"f{l}.w1d{nh}.{m}")
                        w1dma(0)
                        w1dma(1)
                        for m in range(KH):
                            self._ffn_mm(l, nh, m, m % 2, pa, pg2, W1B, B1AR,
                                         GEB, PROD)
                            if m + 2 < KH:
                                w1dma(m + 2)
                        p.flush(f"f{l}.w1.{nh}")
                    with ExitStack() as S6b:
                        px = S6b.enter_context(
                            nc.psum_tensor(f"px{l}{nh}", [128, 2, 512], F32))
                        def w2dma(mo):
                            p.dma("gpsimd", W2B[:, mo % 2, :, :],
                                  h["w2cb"][l][mo].rearrange("(k p) c -> p k c",
                                                             p=128),
                                  writes=(f"W2B{mo % 2}",), tag=f"f{l}.w2d{nh}.{mo}")
                        w2dma(0)
                        w2dma(1)
                        for mo in range(DT):
                            s2 = mo % 2
                            for cc in range(2):
                                pb = cc
                                def w2mm(e, mo=mo, s2=s2, cc=cc, pb=pb):
                                    for k in range(KH):
                                        e.matmul(px[:, pb, :], W2B[:, s2, k, :],
                                                 PROD[:, k, cc * 512:(cc + 1) * 512],
                                                 start=(k == 0), stop=False)
                                    return e.matmul(
                                        px[:, pb, :],
                                        B2R[:, mo * 128:(mo + 1) * 128],
                                        self.ones512[:], start=False, stop=True)
                                p.task("tensor", w2mm,
                                       reads=prod_res + (f"W2B{s2}", "B2R",
                                                         "ones512_bf"),
                                       writes=(f"px{pb}",),
                                       tag=f"f{l}.w2mm{nh}.{mo}.{cc}")
                                p.task("vector", lambda e, mo=mo, cc=cc, nh0=nh0,
                                       pb=pb: e.tensor_tensor(
                                    XF[:, mo, nh0 + cc * 512:nh0 + (cc + 1) * 512],
                                    XF[:, mo, nh0 + cc * 512:nh0 + (cc + 1) * 512],
                                    px[:, pb, :], op=ALU.add),
                                    reads=(f"px{pb}", f"XF{mo}"),
                                    writes=(f"XF{mo}",),
                                    tag=f"f{l}.xadd{nh}.{mo}.{cc}")
                            if mo + 2 < DT:
                                w2dma(mo + 2)
                        p.flush(f"f{l}.w2.{nh}")
                if l == 0 and "dbg_xf2" in self.dbg_h:
                    self.dump_f32("dbg_xf2", lambda i: XF[:, i, :], DT, xf_res)
                p.flush(f"f{l}.ffn")
            if sa == f"f{l}ffn":
                return True
        return False

    def _ffn_mm(self, l, nh, m, sl, pa, pg2, W1B, B1AR, GEB, PROD):
        p = self.p
        nh0 = nh * 1024
        mp = m % 2
        n2_res = tuple(f"N2.{k}" for k in range(DT))
        for cc in range(2):
            def amm(e, m=m, sl=sl, cc=cc, nh0=nh0, mp=mp):
                for k in range(DT):
                    e.matmul(pa[:, mp, cc * 512:(cc + 1) * 512],
                             W1B[:, sl, k, 0:128],
                             self.N2[:, k, nh0 + cc * 512:nh0 + (cc + 1) * 512],
                             start=(k == 0), stop=False)
                return e.matmul(pa[:, mp, cc * 512:(cc + 1) * 512],
                                B1AR[:, m * 128:(m + 1) * 128],
                                self.ones512[:], start=False, stop=True)
            p.task("tensor", amm,
                   reads=n2_res + (f"W1B{sl}", "B1AR", "ones512_bf"),
                   writes=(f"pa{mp}",), tag=f"f{l}.amm{nh}.{m}.{cc}")
            def gmm(e, m=m, sl=sl, cc=cc, nh0=nh0, mp=mp):
                for k in range(DT):
                    mm = e.matmul(pg2[:, mp, cc * 512:(cc + 1) * 512],
                                  W1B[:, sl, k, 128:256],
                                  self.N2[:, k, nh0 + cc * 512:nh0 + (cc + 1) * 512],
                                  start=(k == 0), stop=(k == DT - 1))
                return mm
            p.task("tensor", gmm, reads=n2_res + (f"W1B{sl}",),
                   writes=(f"pg2{mp}",), tag=f"f{l}.gmm{nh}.{m}.{cc}")
        p.task("scalar", lambda e, m=m, mp=mp: e.activation(
            GEB[:, mp, :], pg2[:, mp, :],
            AF.Gelu, bias=self.b1g_sb[:, m:m + 1]),
            reads=(f"pg2{mp}", "B1G"), writes=(f"GEB{mp}", f"pg2{mp}"),
            tag=f"f{l}.gelu{nh}.{m}")
        p.task("vector", lambda e, m=m, mp=mp: e.tensor_tensor(
            PROD[:, m, :], pa[:, mp, :],
            GEB[:, mp, :], op=ALU.mult),
            reads=(f"GEB{mp}", f"pa{mp}"),
            writes=(f"PROD.{m}", f"pa{mp}"), tag=f"f{l}.prod{nh}.{m}")

    def _qk_chain(self, l, m, pb, DSTP, QW, SQ, T1, T2, ROWS, RSTDB, ESC,
                  psq, paux):
        p = self.p
        is_q = m < 4
        mt = m if is_q else m - 4
        er = 0 if is_q else 1
        q3 = m % 3
        qw_res = tuple(f"QW{q3}.{c}" for c in range(4))
        dst_res = ("QPf" if is_q else "KPf") + str(mt)
        for c in range(4):
            p.task("vector", lambda e, c=c, pb=pb, q3=q3: e.tensor_tensor(
                SQ[:, pb, :], QW[:, q3, c * 512:(c + 1) * 512],
                QW[:, q3, c * 512:(c + 1) * 512], op=ALU.mult),
                reads=(f"QW{q3}.{c}",), writes=(f"SQ{pb}",), tag=f"f{l}.sqr{m}.{c}")
            p.task("tensor", lambda e, c=c, pb=pb: e.matmul(
                psq[:, pb, :], self.E2Csb[:], SQ[:, pb, :],
                start=True, stop=True),
                reads=(f"SQ{pb}", "E2C"), writes=(f"psq{pb}",),
                tag=f"f{l}.ssq{m}.{c}")
            p.task("scalar", lambda e, c=c, pb=pb: e.activation(
                ROWS[:, pb, :], psq[:, pb, :], AF.Sqrt),
                reads=(f"psq{pb}",), writes=(f"LNR{pb}",),
                tag=f"f{l}.l2sq{m}.{c}")
            p.task("vector", lambda e, pb=pb: e.reciprocal(
                ROWS[:, pb, :], ROWS[:, pb, :]),
                reads=(f"LNR{pb}",), writes=(f"LNR{pb}",), tag=f"f{l}.l2rc{m}.{c}")
            p.task("vector", lambda e, c=c, pb=pb: e.tensor_scalar(
                RSTDB[:, pb, c * 512:(c + 1) * 512], ROWS[:, pb, :], 1e12,
                None, op0=ALU.min),
                reads=(f"LNR{pb}",), writes=(f"RSTDB{pb}.{c}",),
                tag=f"f{l}.clip{m}.{c}")
        for c in range(4):
            p.task("tensor", lambda e, c=c, pb=pb: e.matmul(
                paux[:, pb, :], ESC[:, er, :], RSTDB[:, pb, c * 512:(c + 1) * 512],
                start=True, stop=True),
                reads=(f"RSTDB{pb}.{c}", f"ESC{er}"), writes=(f"paux{pb}",),
                tag=f"f{l}.exp{m}.{c}")
            p.task("vector", lambda e, c=c, pb=pb, q3=q3: e.tensor_tensor(
                QW[:, q3, c * 512:(c + 1) * 512],
                QW[:, q3, c * 512:(c + 1) * 512], paux[:, pb, :], op=ALU.mult),
                reads=(f"paux{pb}",) + qw_res, writes=(f"QW{q3}.{c}",),
                tag=f"f{l}.qn{m}.{c}")
        for c in range(4):
            p.task("tensor", lambda e, c=c, pb=pb, q3=q3: e.matmul(
                paux[:, pb, :], self.ROTTsb[:],
                QW[:, q3, c * 512:(c + 1) * 512], start=True, stop=True),
                reads=(f"QW{q3}.{c}", "ROTT"), writes=(f"paux{pb}",),
                tag=f"f{l}.rot{m}.{c}")
            p.task("vector", lambda e, c=c, mt=mt, pb=pb: e.tensor_tensor(
                DSTP[:, mt, c * 512:(c + 1) * 512], paux[:, pb, :],
                self.ROTsb[:, 1, c * 512:(c + 1) * 512], op=ALU.mult),
                reads=(f"paux{pb}", "ROT1"), writes=(dst_res,),
                tag=f"f{l}.t2{m}.{c}")
            p.task("vector", lambda e, c=c, pb=pb, q3=q3: e.tensor_tensor(
                T1[:, pb, :], QW[:, q3, c * 512:(c + 1) * 512],
                self.ROTsb[:, 0, c * 512:(c + 1) * 512], op=ALU.mult),
                reads=(f"QW{q3}.{c}", "ROT0"), writes=(f"T1{pb}",),
                tag=f"f{l}.t1{m}.{c}")
            p.task("vector", lambda e, c=c, mt=mt, pb=pb: e.tensor_tensor(
                DSTP[:, mt, c * 512:(c + 1) * 512],
                DSTP[:, mt, c * 512:(c + 1) * 512], T1[:, pb, :],
                op=ALU.add),
                reads=(f"T1{pb}", dst_res), writes=(dst_res,),
                tag=f"f{l}.rsum{m}.{c}")

    def _local_attn(self, l, QP, KP, VTM, OA, PB, PT, pS, pPT, pO):
        p = self.p
        small = self.small
        qp_res = tuple(f"QPf{m}" for m in range(4))
        kp_res = tuple(f"KPf{m}" for m in range(4))
        vtm_res = tuple(f"VTM{vm}.{jg}" for vm in range(4) for jg in range(4))

        def band(w, it):
            if w == 0:
                return 0, (it + 1) * 128
            return it * 128, 640

        def s_task(hp, w, it, u, jt0, iv):
            jlo, W = band(w, it)
            moff = 512 if w == 0 else 0
            chunks = []
            pos = jlo
            while pos < jlo + W:
                end = min(jlo + W, (pos // 512 + 1) * 512)
                chunks.append((pos, end - pos))
                pos = end
            if w == 0:
                mblocks = [(it * 128, 128)]
            else:
                mblocks = [(jlo, 128), (jlo + 512, 128)]
            def smm(e, hp=hp, w=w, iv=iv, jt0=jt0, u=u, chunks=chunks,
                    moff=moff, mblocks=mblocks):
                for (lo, n) in chunks:
                    e.matmul(
                        pS[:, u, lo:lo + n],
                        QP[u * 64:(u + 1) * 64, hp, iv:iv + 128],
                        KP[u * 64:(u + 1) * 64, hp,
                           jt0 * 128 + lo:jt0 * 128 + lo + n],
                        start=True, stop=False)
                for (lo, n) in mblocks:
                    mm = e.matmul(
                        pS[:, u, lo:lo + n],
                        self.ident_bf[:],
                        self.MSKsb[:, it, moff + lo:moff + lo + n],
                        start=False, stop=True)
                return mm
            p.task("tensor", smm,
                   reads=qp_res + kp_res + ("identity_bf", f"MSK{it}"),
                   writes=(f"pS{u}",), tag=f"f{l}.S{hp}.{w}.{it}.{u}")

        def softmax(hp, w, it):
            jlo, W = band(w, it)
            for u in range(2):
                sc = 8 + u * 8 + (it % 2) * 4
                p.task("vector", lambda e, u=u, sc=sc, jlo=jlo, W=W: e.tensor_reduce(
                    small[:, sc:sc + 1], pS[:, u, jlo:jlo + W], axis=AX.X,
                    op=ALU.max, negate=True),
                    reads=(f"pS{u}",), writes=(f"negm{u}.{it % 2}",),
                    tag=f"f{l}.mx{hp}.{w}.{it}.{u}")
            for u in range(2):
                sc = 8 + u * 8 + (it % 2) * 4
                p.task("scalar", lambda e, u=u, sc=sc, it=it, jlo=jlo, W=W:
                       e.activation(
                           PB[:, it % 2, u, jlo:jlo + W], pS[:, u, jlo:jlo + W],
                           AF.Exp, bias=small[:, sc:sc + 1],
                           accum_out=small[:, sc + 1:sc + 2]),
                       reads=(f"pS{u}", f"negm{u}.{it % 2}"),
                       writes=(f"PB{u}.{it % 2}", f"z{u}.{it % 2}"),
                       tag=f"f{l}.ex{hp}.{w}.{it}.{u}")
            for u in range(2):
                sc = 8 + u * 8 + (it % 2) * 4
                p.task("vector", lambda e, sc=sc: e.reciprocal(
                    small[:, sc + 2:sc + 3], small[:, sc + 1:sc + 2]),
                    reads=(f"z{u}.{it % 2}",), writes=(f"rz{u}.{it % 2}",),
                    tag=f"f{l}.rz{hp}.{w}.{it}.{u}")
            for u in range(2):
                sc = 8 + u * 8 + (it % 2) * 4
                p.task("vector", lambda e, u=u, sc=sc, it=it, jlo=jlo, W=W:
                       e.tensor_scalar(
                           PB[:, it % 2, u, jlo:jlo + W],
                           PB[:, it % 2, u, jlo:jlo + W],
                           small[:, sc + 2:sc + 3], None, op0=ALU.mult),
                       reads=(f"PB{u}.{it % 2}", f"rz{u}.{it % 2}"),
                       writes=(f"PB{u}.{it % 2}",),
                       tag=f"f{l}.nm{hp}.{w}.{it}.{u}")
                if l == 0 and hp == 0 and w == 1 and "dbg_pb" in self.dbg_h:
                    rid = f"dbg_pb.{it * 2 + u}"
                    p.dma("gpsimd", self.dbg_h["dbg_pb"][it * 2 + u],
                          PB[:, it % 2, u, :], reads=(f"PB{u}.{it % 2}",),
                          writes=(rid,), tag=rid)
                    self.out_res.append(rid)

        def emit_tp(hp, w, it):
            jlo, W = band(w, it)
            jb0 = jlo // 128
            nb = W // 128
            for u in range(2):
                def ptp(e, u=u, it=it, jb0=jb0, nb=nb):
                    for jj in range(nb):
                        jb = jb0 + jj
                        mm = e.transpose(
                            pPT[:, u, jj * 128:(jj + 1) * 128],
                            PB[:, it % 2, u, jb * 128:(jb + 1) * 128],
                            self.ident_bf[:])
                    return mm
                p.task("tensor", ptp, reads=(f"PB{u}.{it % 2}", "identity_bf"),
                       writes=(f"pPT{u}",), tag=f"f{l}.tp{hp}.{w}.{it}.{u}")
                def pev(e, u=u, it=it, jb0=jb0, nb=nb):
                    return e.activation(
                        PT[:, u, jb0:jb0 + nb, it * 128:(it + 1) * 128],
                        pPT[:, u, 0:nb * 128].rearrange("p (a b) -> p a b", b=128),
                        AF.Identity)
                p.task("scalar", pev, reads=(f"pPT{u}",),
                       writes=(f"PTc{u}.{it}",),
                       tag=f"f{l}.pe{hp}.{w}.{it}.{u}")

        def icols(w, jt, njt):
            if w == 0:
                return jt * 128, 512 - jt * 128
            lo = max(0, jt - 4) * 128
            hi = min(4, jt + 1) * 128
            return lo, hi - lo

        for hp in range(4):
            for w in range(NW):
                jt0 = 0 if w == 0 else (w - 1) * 4
                njt = 4 if w == 0 else 8
                for it in range(4):
                    iv = w * 512 + it * 128
                    for u in range(2):
                        s_task(hp, w, it, u, jt0, iv)
                    if it > 0:
                        emit_tp(hp, w, it - 1)
                    softmax(hp, w, it)
                emit_tp(hp, w, 3)
                pt_res = tuple(f"PTc{u}.{it}" for u in range(2) for it in range(4))
                pbo = w % 2
                jt_order = list(range(njt)) if w == 0 else [3, 4, 2, 5, 1, 6, 0, 7]
                def omm(e, hp=hp, w=w, jt0=jt0, pbo=pbo, jt_order=jt_order):
                    for u in range(2):
                        hh = hp * 2 + u
                        first = True
                        for jt in jt_order:
                            ilo, n = icols(w, jt, 0)
                            mm = e.matmul(
                                pO[u * 64:(u + 1) * 64, pbo, ilo:ilo + n],
                                VTM[:, jt0 + jt, hh * 64:(hh + 1) * 64],
                                PT[:, u, jt, ilo:ilo + n],
                                start=first, stop=(jt == jt_order[-1]),
                                tile_position=(0, u * 64))
                            first = False
                    return mm
                p.task("tensor", omm, reads=pt_res + vtm_res,
                       writes=(f"pO{pbo}",), tag=f"f{l}.O{hp}.{w}")
                p.task("scalar", lambda e, hp=hp, w=w, pbo=pbo: e.activation(
                    OA[:, hp, w * 512:(w + 1) * 512], pO[:, pbo, :],
                    AF.Identity),
                    reads=(f"pO{pbo}",), writes=(f"OA{hp}.{w}",),
                    tag=f"f{l}.Oe{hp}.{w}")

    def final_phase(self):
        nc, p, h = self.nc, self.p, self.h
        XF = self.XF
        with ExitStack() as S:
            SS = S.enter_context(nc.sbuf_tensor("SS", [128, 2, N], F32))
            XB = S.enter_context(nc.sbuf_tensor("XB", [128, 2, D], F32))
            OT = S.enter_context(nc.sbuf_tensor("OT", [128, 2, D], F32))
            pfin = S.enter_context(nc.psum_tensor("pfin", [128, 2, 1024], F32))
            for k in range(DT):
                b = k % 2
                p.task("scalar", lambda e, k=k: e.activation(
                    XF[:, k, :], XF[:, k, :], AF.Silu),
                    reads=(f"XF{k}",), writes=(f"XF{k}",), tag=f"fin.silu{k}")
                if "dbg_g" in self.dbg_h:
                    rid = f"dbg_g.{k}"
                    p.dma("sync", self.dbg_h["dbg_g"][k], XF[:, k, :],
                          reads=(f"XF{k}",), writes=(rid,), tag=rid)
                    self.out_res.append(rid)
                p.dma("sync", SS[:, b, :], self.s_scr[k],
                      reads=(f"s_scr{k}",), writes=(f"SS{b}",), tag=f"fin.s{k}")
                p.task("vector", lambda e, k=k, b=b: e.tensor_tensor(
                    XF[:, k, :], XF[:, k, :], SS[:, b, :], op=ALU.mult),
                    reads=(f"XF{k}", f"SS{b}"), writes=(f"XF{k}",),
                    tag=f"fin.sg{k}")
            for t in range(NT):
                b = t % 2
                p.dma("sync", XB[:, b, :], h["x"][t * 128:(t + 1) * 128, :],
                      reads=(f"XB{b}c",), writes=(f"XB{b}",), tag=f"fin.x{t}")
                def tpf(e, t=t, b=b):
                    for k in range(DT):
                        mm = e.transpose(pfin[:, b, k * 128:(k + 1) * 128],
                                         XF[:, k, t * 128:(t + 1) * 128],
                                         self.ident_f32[:])
                    return mm
                p.task("tensor", tpf,
                       reads=tuple(f"XF{k}" for k in range(DT)) + ("identity_f32",),
                       writes=(f"pfin{b}",), tag=f"fin.tp{t}")
                p.task("vector", lambda e, t=t, b=b: e.tensor_tensor(
                    OT[:, b, :], XB[:, b, :], pfin[:, b, :], op=ALU.add),
                    reads=(f"pfin{b}", f"XB{b}"), writes=(f"OT{b}", f"XB{b}c"),
                    tag=f"fin.add{t}")
                rid = f"out{t}"
                p.dma("gpsimd", self.out_h[t * 128:(t + 1) * 128, :], OT[:, b, :],
                      reads=(f"OT{b}",), writes=(rid,), tag=rid)
                self.out_res.append(rid)
            p.flush("final")




def build_nc(debug=(), stop_after=None):
    b = Build(debug, stop_after)
    nc, dbg = b.build()
    return nc, dbg


def kernel(**inputs):
    g = prep_inputs(inputs)
    c = host_constants()
    nc, _ = build_nc()
    base = {}
    for n in INPUT_NAMES:
        base[n] = g[n]
    for n in CONST_NAMES:
        base[n] = c[n]
    in_maps = []
    for b in range(B):
        m = dict(base)
        m["x"] = np.ascontiguousarray(g["x"][b])
        in_maps.append(m)
    res = run_bass_kernel_spmd(nc, in_maps, core_ids=list(range(B)))
    return np.stack([res.results[b]["out"] for b in range(B)], axis=0)


# version 18
# speedup vs baseline: 1.0436x; 1.0436x over previous
import numpy as np
import ml_dtypes
from contextlib import ExitStack

import concourse.bass as bass
import concourse.mybir as mybir
from concourse.bass_utils import run_bass_kernel_spmd

F32 = mybir.dt.float32
BF16 = mybir.dt.bfloat16
AF = mybir.ActivationFunctionType
ALU = mybir.AluOpType
AX = mybir.AxisListType
BF = ml_dtypes.bfloat16

B, N, D = 8, 2048, 1024
NT, DT = N // 128, D // 128
HIDDEN, QKD = 2048, 2048
HEADS, DH, INNER = 8, 64, 512
WIN, NW = 512, 4
HID, HIDP = 2730, 2816
KH = HIDP // 128
EPS = 1e-5
NEG = -1e30

ENGINES = ("tensor", "vector", "scalar", "gpsimd", "sync")
TRUNC_M = 12
CHAIN_ON = True
VPART = 3



class Prog:

    R = 8

    def __init__(self, nc, stack):
        self.nc = nc
        self.sems = {e: stack.enter_context(nc.semaphore("sem_" + e)) for e in ENGINES}
        self.dsems = {e: [stack.enter_context(nc.semaphore(f"dsem_{e}{r}"))
                          for r in range(self.R)] for e in ENGINES}
        self.cnt = {e: 0 for e in ENGINES}
        self.dcnt = {e: 0 for e in ENGINES}
        self.drot = {e: [0] * self.R for e in ENGINES}
        self.cov = {e: {} for e in ENGINES}
        self.pending = {e: [] for e in ENGINES}
        self.res = {}
        self.n_tasks = 0

    def task(self, eng, emit, reads=(), writes=(), tag="", n_dma=0):
        deps = {}

        def add(tok):
            if tok is None:
                return
            k, v = tok[:-1], tok[-1]
            if deps.get(k, 0) < v:
                deps[k] = v

        for r in reads:
            e = self.res.get(r)
            if e:
                add(e["w"])
        for r in writes:
            e = self.res.get(r)
            if e:
                add(e["w"])
                for t in e["r"]:
                    add(t)
        if eng == "tensor":
            deps = {k: v for k, v in deps.items() if k != ("c", "tensor")}
        if n_dma:
            assert n_dma == 1
            r = self.dcnt[eng] % self.R
            self.dcnt[eng] += 1
            prev = self.drot[eng][r]
            self.drot[eng][r] = prev + 16
            if prev and deps.get(("d", eng, r), 0) < prev:
                deps[("d", eng, r)] = prev
            comp = ("d", eng, r, prev + 16)
            self.pending[eng].append((deps, emit, r, tag))
        else:
            self.cnt[eng] += 1
            comp = ("c", eng, self.cnt[eng])
            self.pending[eng].append((deps, emit, None, tag))
        self.n_tasks += 1
        for r in reads:
            self.res.setdefault(r, {"w": None, "r": []})["r"].append(comp)
        for r in writes:
            self.res[r] = {"w": comp, "r": []}

    def dma(self, eng, out, in_, reads=(), writes=(), tag=""):
        def emit(e, out=out, in_=in_):
            return [e.dma_start(out=out, in_=in_)]
        self.task(eng, emit, reads, writes, tag, n_dma=1)

    def wait_only(self, eng, deps, tag="wait"):
        self.pending[eng].append((dict(deps), None, 0, tag))

    def flush(self, name):
        for e in ENGINES:
            if self.dcnt[e]:
                self.wait_only(e, {("d", e, r): self.drot[e][r]
                                   for r in range(self.R) if self.drot[e][r]},
                               "fence")
        with self.nc.named_scope(name), self.nc.Block() as block:
            for e in ENGINES:
                tasks = self.pending[e]
                self.pending[e] = []
                if not tasks:
                    continue

                def gen(e=e, tasks=tasks):
                    def _(eng):
                        cov = self.cov[e]
                        for (deps, emit, dslot, tag) in tasks:
                            for key, val in sorted(deps.items()):
                                if cov.get(key, 0) >= val:
                                    continue
                                if key[0] == "d":
                                    sem = self.dsems[key[1]][key[2]]
                                else:
                                    sem = self.sems[key[1]]
                                eng.wait_ge(sem, val)
                                cov[key] = val
                            if emit is None:
                                continue
                            if dslot is not None:
                                insts = emit(eng)
                                assert len(insts) == 1, tag
                                insts[0].then_inc(self.dsems[e][dslot], 16)
                            else:
                                inst = emit(eng)
                                if inst is None:
                                    inst = eng.nop()
                                inst.then_inc(self.sems[e], 1)
                    return _
                getattr(block, e)(gen())
        for e in ENGINES:
            for e2 in ENGINES:
                self.cov[e][("c", e2)] = self.cnt[e2]
                for r in range(self.R):
                    self.cov[e][("d", e2, r)] = self.drot[e2][r]


def host_constants():
    c = {}
    c["identity_bf"] = np.eye(128, dtype=BF)
    c["identity_f32"] = np.eye(128, dtype=np.float32)
    c["ones1_bf"] = np.ones((1, 128), dtype=BF)
    c["ones_col_bf"] = np.ones((128, 1), dtype=BF)
    c["ones512_bf"] = np.ones((1, 512), dtype=BF)
    e2c = np.zeros((128, 2), dtype=np.float32)
    e2c[0:64, 0] = 1.0
    e2c[64:128, 1] = 1.0
    c["e2c_bf"] = e2c.astype(BF)
    e2t = np.zeros((2, 128), dtype=np.float32)
    e2t[0, 0:64] = 1.0
    e2t[1, 64:128] = 1.0
    c["e2t"] = e2t
    R = np.zeros((128, 128), dtype=np.float32)
    for base in (0, 64):
        for i in range(32):
            R[base + i, base + 32 + i] = -1.0
            R[base + 32 + i, base + i] = 1.0
    c["rotT_bf"] = R.T.copy().astype(BF)
    half = np.arange(0, DH, 2, dtype=np.float64)
    inv_freq = 1.0 / (10000.0 ** (half / DH))
    tpos = np.arange(N, dtype=np.float64)
    freqs = np.concatenate([tpos[:, None] * inv_freq[None, :]] * 2, axis=-1)
    base = (half + 0.4 * DH) / (1.4 * DH)
    power = (tpos - N // 2) / float(WIN // 2)
    scale = np.concatenate([base[None, :] ** power[:, None]] * 2, axis=-1)
    cos, sin = np.cos(freqs), np.sin(freqs)
    QKS = 8.0
    tab = np.stack([(cos * scale * QKS).T, (sin * scale * QKS).T,
                    (cos / scale).T, (sin / scale).T])
    c["rotary"] = np.tile(tab, (1, 2, 1)).astype(BF)
    m1 = np.zeros((4, 128, 1024), dtype=BF)
    j = np.arange(1024)
    for it in range(4):
        for pp in range(128):
            il = it * 128 + pp
            m1[it, pp] = np.where((j >= il) & (j <= il + WIN), 0.0, NEG).astype(BF)
    c["mask1"] = m1
    return c


def prep_inputs(inputs):
    g = {}
    f32 = np.float32
    asf = lambda a: np.ascontiguousarray(np.asarray(a, dtype=f32))
    g["x"] = asf(inputs["L0_attn_x"])
    g["W_hidden"] = asf(inputs["L0_attn_W_hidden"])
    g["W_qk"] = asf(inputs["L0_attn_W_qk"])
    g["W_out"] = asf(inputs["L0_attn_W_out"])
    g["bh_t"] = asf(inputs["L0_attn_b_hidden"]).reshape(32, 128).T.copy()
    g["bqk_row"] = asf(inputs["L0_attn_b_qk"]).reshape(1, QKD)
    g["bout_t"] = asf(inputs["L0_attn_b_out"]).reshape(8, 128).T.copy()
    g["lng_t"] = asf(inputs["L0_attn_ln_g"]).reshape(8, 128).T.copy()
    g["lnb_t"] = asf(inputs["L0_attn_ln_b"]).reshape(8, 128).T.copy()
    osg = asf(inputs["L0_attn_os_gamma"])
    osb = asf(inputs["L0_attn_os_beta"])
    g["gb_row"] = np.ascontiguousarray(
        np.concatenate([osg[0], osb[0], osg[1], osb[1]]).reshape(1, 4 * QKD))
    g["wqkv"] = asf(inputs["L0_ffn_wqkv"])
    g["wo"] = asf(inputs["L0_ffn_wo"])
    w1 = asf(inputs["L0_ffn_w1"])
    w2 = asf(inputs["L0_ffn_w2"])
    b1 = asf(inputs["L0_ffn_b1"])
    b2 = asf(inputs["L0_ffn_b2"])
    w1a = np.zeros((2, D, HIDP), f32)
    w1g = np.zeros((2, D, HIDP), f32)
    w1a[:, :, :HID] = w1[:, :, :HID]
    w1g[:, :, :HID] = w1[:, :, HID:]
    w1ag = np.zeros((2, KH, D, 256), f32)
    for m in range(KH):
        w1ag[:, m, :, 0:128] = w1a[:, :, m * 128:(m + 1) * 128]
        w1ag[:, m, :, 128:256] = w1g[:, :, m * 128:(m + 1) * 128]
    g["w1ag"] = np.ascontiguousarray(w1ag)
    w2p = np.zeros((2, HIDP, D), f32)
    w2p[:, :HID, :] = w2
    w2cb = np.zeros((2, 8, HIDP, 128), f32)
    for mo in range(8):
        w2cb[:, mo] = w2p[:, :, mo * 128:(mo + 1) * 128]
    g["w2cb"] = np.ascontiguousarray(w2cb)
    g["b2_row"] = np.ascontiguousarray(b2.reshape(2, 1, D))
    b1a = np.zeros((2, HIDP), f32)
    b1g = np.zeros((2, HIDP), f32)
    b1a[:, :HID] = b1[:, :HID]
    b1g[:, :HID] = b1[:, HID:]
    g["b1a_row"] = np.ascontiguousarray(b1a.reshape(2, 1, HIDP))
    g["b1g_t"] = np.ascontiguousarray(b1g.reshape(2, KH, 128).transpose(0, 2, 1))
    g["flg_t"] = np.ascontiguousarray(
        asf(inputs["L0_ffn_ln_g"]).reshape(2, 8, 128).transpose(0, 2, 1))
    g["flb_t"] = np.ascontiguousarray(
        asf(inputs["L0_ffn_ln_b"]).reshape(2, 8, 128).transpose(0, 2, 1))
    qs = asf(inputs["L0_ffn_q_scale"])
    ks = asf(inputs["L0_ffn_k_scale"])
    g["qs_tile"] = np.ascontiguousarray(
        np.broadcast_to(np.tile(qs, (1, 2))[:, None, :], (2, 2, 128)).copy())
    g["ks_tile"] = np.ascontiguousarray(
        np.broadcast_to(np.tile(ks, (1, 2))[:, None, :], (2, 2, 128)).copy())
    return g


INPUT_NAMES = [
    "W_hidden", "W_qk", "W_out", "bh_t", "bqk_row", "bout_t", "lng_t", "lnb_t",
    "gb_row", "wqkv", "wo", "w1ag", "w2cb", "b2_row", "b1a_row", "b1g_t",
    "flg_t", "flb_t", "qs_tile", "ks_tile"]
CONST_NAMES = ["e2t", "e2c_bf", "rotT_bf", "identity_f32", "identity_bf",
               "ones1_bf", "ones_col_bf", "ones512_bf", "rotary", "mask1"]


class Build:
    def __init__(self, debug=(), stop_after=None):
        self.nc = bass.Bass("TRN2", target_bir_lowering=False)
        self.dbg = set(debug)
        self.stop_after = stop_after
        self.h = {}
        self.dbg_h = {}
        self.out_res = []

    def inp(self, name, shape, dt=F32):
        self.h[name] = self.nc.dram_tensor(
            name, list(shape), dt, kind="ExternalInput").ap()

    def declare_io(self):
        nc = self.nc
        self.inp("x", (N, D))
        self.inp("W_hidden", (D, 2 * HIDDEN))
        self.inp("W_qk", (D, QKD))
        self.inp("W_out", (HIDDEN, D))
        self.inp("bh_t", (128, 32))
        self.inp("bqk_row", (1, QKD))
        self.inp("bout_t", (128, 8))
        self.inp("lng_t", (128, 8))
        self.inp("lnb_t", (128, 8))
        self.inp("gb_row", (1, 4 * QKD))
        self.inp("wqkv", (2, D, 3 * INNER))
        self.inp("wo", (2, INNER, D))
        self.inp("w1ag", (2, KH, D, 256))
        self.inp("w2cb", (2, 8, HIDP, 128))
        self.inp("b2_row", (2, 1, D))
        self.inp("b1a_row", (2, 1, HIDP))
        self.inp("b1g_t", (2, 128, KH))
        self.inp("flg_t", (2, 128, 8))
        self.inp("flb_t", (2, 128, 8))
        self.inp("qs_tile", (2, 2, 128))
        self.inp("ks_tile", (2, 2, 128))
        self.inp("e2t", (2, 128))
        self.inp("e2c_bf", (128, 2), BF16)
        self.inp("rotT_bf", (128, 128), BF16)
        self.inp("identity_f32", (128, 128))
        self.inp("identity_bf", (128, 128), BF16)
        self.inp("ones1_bf", (1, 128), BF16)
        self.inp("ones_col_bf", (128, 1), BF16)
        self.inp("ones512_bf", (1, 512), BF16)
        self.inp("rotary", (4, 128, N), BF16)
        self.inp("mask1", (4, 128, 1024), BF16)
        self.out_h = nc.dram_tensor("out", [N, D], F32, kind="ExternalOutput").ap()
        self.xn_scr = nc.dram_tensor("xn_scr", [DT, 128, N], F32).ap()
        self.s_scr = nc.dram_tensor("s_scr", [DT, 128, N], F32).ap()
        for name, shape in [
                ("dbg_xn", (DT, 128, N)), ("dbg_normed", (DT, 128, N)),
                ("dbg_qk", (NT, 128, QKD)), ("dbg_rowsum", (128, NT)),
                ("dbg_vg", (NT, 128, N)), ("dbg_s", (DT, 128, N)),
                ("dbg_qp", (4, 128, N)), ("dbg_kp", (4, 128, N)),
                ("dbg_vt", (NT, 128, INNER)),
                ("dbg_oall", (4, 128, N)), ("dbg_xf", (DT, 128, N)),
                ("dbg_n2", (DT, 128, N)), ("dbg_h1", (KH, 128, N)),
                ("dbg_pb", (8, 128, 1024)), ("dbg_s2", (8, 128, 1024)),
                ("dbg_xf2", (DT, 128, N))]:
            if name in self.dbg:
                self.dbg_h[name] = self.nc.dram_tensor(
                    name, list(shape), F32, kind="ExternalOutput").ap()

    def dump_bf(self, name, src_fn, count, reads):
        hd = self.dbg_h.get(name)
        if hd is None:
            return
        for i in range(count):
            rid = f"{name}.{i}"
            self.p.dma("gpsimd", hd[i], src_fn(i), reads=reads, writes=(rid,),
                       tag=f"dbg.{name}{i}")
            self.out_res.append(rid)

    def dump_f32(self, name, src_fn, count, reads):
        hd = self.dbg_h.get(name)
        if hd is None:
            return
        for i in range(count):
            rid = f"{name}.{i}"
            self.p.dma("sync", hd[i], src_fn(i), reads=reads, writes=(rid,),
                       tag=f"dbg.{name}{i}")
            self.out_res.append(rid)

    def build(self):
        self.declare_io()
        nc = self.nc
        with ExitStack() as TOP:
            E = TOP.enter_context
            self.p = Prog(nc, TOP)
            p = self.p

            self.ABF = E(nc.sbuf_tensor("ABF", [128, DT, N], BF16))
            self.ident_bf = E(nc.sbuf_tensor("ident_bf", [128, 128], BF16))
            self.ident_f32 = E(nc.sbuf_tensor("ident_f32", [128, 128], F32))
            self.ones1 = E(nc.sbuf_tensor("ones1", [1, 128], BF16))
            self.rowsum = E(nc.sbuf_tensor("rowsum", [128, NT], F32))
            self.small = E(nc.sbuf_tensor("small", [128, 48], F32))
            self.bh_sb = E(nc.sbuf_tensor("bh_sb", [128, 32], F32))
            self.bout_sb = E(nc.sbuf_tensor("bout_sb", [128, 8], F32))
            self.lng_sb = E(nc.sbuf_tensor("lng_sb", [128, 8], F32))
            self.lnb_sb = E(nc.sbuf_tensor("lnb_sb", [128, 8], F32))
            self.epsv = E(nc.sbuf_tensor("epsv", [128, 1], F32))
            self.ones_col = E(nc.sbuf_tensor("ones_col", [128, 1], BF16))
            self.ones512 = E(nc.sbuf_tensor("ones512", [1, 512], BF16))
            self.e2t_sb = E(nc.sbuf_tensor("e2t_sb", [2, 128], F32))

            for nm, sb in [("identity_bf", self.ident_bf),
                           ("identity_f32", self.ident_f32),
                           ("ones1_bf", self.ones1), ("bh_t", self.bh_sb),
                           ("ones_col_bf", self.ones_col),
                           ("ones512_bf", self.ones512), ("e2t", self.e2t_sb),
                           ("bout_t", self.bout_sb), ("lng_t", self.lng_sb),
                           ("lnb_t", self.lnb_sb)]:
                p.dma("sync", sb[:], self.h[nm][:], writes=(nm,), tag="const")
            p.task("vector", lambda e: e.memset(self.epsv[:], EPS),
                   writes=("epsv",), tag="c.eps")

            with ExitStack() as SQG:
                self.QK = SQG.enter_context(
                    nc.sbuf_tensor("QK", [128, NT, QKD], BF16))
                self.GBC = SQG.enter_context(
                    nc.sbuf_tensor("GBC", [128, 4, QKD], BF16))
                with ExitStack() as SWQK:
                    self.WQK = SWQK.enter_context(
                        nc.sbuf_tensor("WQK", [128, DT, QKD], BF16))
                    self.phase0()
                    if self.stop_after != "p0":
                        self.attn_p1()
                if self.stop_after not in ("p0", "p1"):
                    self.attn_p2()
            if self.stop_after not in ("p0", "p1", "p2"):
                self.attn_p34()
                if self.stop_after not in ("p3", "p4"):
                    self.ffn_slot()

            p.wait_only("sync", self._res_deps(self.out_res), "drain")
            p.flush("drain")
        return nc, self.dbg_h

    def _res_deps(self, names):
        deps = {}
        for r in names:
            e = self.p.res.get(r)
            if e and e["w"] is not None:
                tok = e["w"]
                k, v = tok[:-1], tok[-1]
                if deps.get(k, 0) < v:
                    deps[k] = v
        return deps

    def phase0(self):
        nc, p = self.nc, self.p
        h = self.h
        with ExitStack() as S:
            XT = S.enter_context(nc.sbuf_tensor("XT", [128, 4, D], F32))
            XNT = S.enter_context(nc.sbuf_tensor("XNT", [128, 4, D], F32))
            NTB = S.enter_context(nc.sbuf_tensor("NTB", [128, 4, D], BF16))
            XNW = S.enter_context(nc.sbuf_tensor("XNW", [128, 2, DT, 128], F32))
            ST6 = S.enter_context(nc.sbuf_tensor("ST6", [128, 4, 12], F32))
            MV = S.enter_context(nc.sbuf_tensor("MV", [128, 4, 2], F32))
            pbf = S.enter_context(nc.psum_tensor("pbf", [128, 2, 1024], BF16))
            pxf = S.enter_context(nc.psum_tensor("pxf", [128, 2, 1024], F32))
            small = self.small
            ABF = self.ABF

            def stats(src_ap, src_res, rstd_col, tagp, b):
                p.task("vector", lambda e, v=src_ap, b=b: (
                    e.bn_stats(ST6[:, b, 0:6], v[:, 0:512]),
                    e.bn_stats(ST6[:, b, 6:12], v[:, 512:1024]))[-1],
                    reads=(src_res,), writes=(f"ST6{b}",), tag=tagp + ".bn")
                p.task("vector", lambda e, b=b: e.bn_aggr(MV[:, b, :], ST6[:, b, :]),
                       reads=(f"ST6{b}",), writes=(f"MV{b}",), tag=tagp + ".ag")
                p.task("scalar", lambda e, b=b: e.activation(
                    small[:, 8 + b:9 + b], MV[:, b, 1:2], AF.Ln,
                    bias=self.epsv[:], scale=1.0),
                    reads=(f"MV{b}", "epsv"), writes=(f"lnv{b}",), tag=tagp + ".ln")
                p.task("scalar", lambda e, rc=rstd_col, b=b: e.activation(
                    small[:, rc:rc + 1], small[:, 8 + b:9 + b], AF.Exp, scale=-0.5),
                    reads=(f"lnv{b}",), writes=(f"rstd{rstd_col}",), tag=tagp + ".ex")

            def stageA(t):
                b = t % 4
                p.dma("sync", XT[:, b, :], h["x"][t * 128:(t + 1) * 128, :],
                      writes=(f"XT{b}",), tag=f"p0.load{t}")
                stats(XT[:, b, :], f"XT{b}", 16 + b, f"p0.s1_{t}", b)

            def stageB(t):
                b = t % 4
                p.task("vector", lambda e, b=b: e.tensor_scalar(
                    XNT[:, b, :], XT[:, b, :], MV[:, b, 0:1], small[:, 16 + b:17 + b],
                    op0=ALU.subtract, op1=ALU.mult),
                    reads=(f"XT{b}", f"MV{b}", f"rstd{16 + b}"),
                    writes=(f"XNT{b}",), tag=f"p0.xn{t}")
                stats(XNT[:, b, :], f"XNT{b}", 20 + b, f"p0.s2_{t}", b)

            def stageC(t):
                b = t % 4
                bp = t % 2
                p.task("vector", lambda e, b=b: e.tensor_scalar(
                    NTB[:, b, :], XNT[:, b, :], MV[:, b, 0:1], small[:, 20 + b:21 + b],
                    op0=ALU.subtract, op1=ALU.mult),
                    reads=(f"XNT{b}", f"MV{b}", f"rstd{20 + b}"),
                    writes=(f"NTB{b}",), tag=f"p0.nt{t}")

                def tp_n(e, b=b, bp=bp):
                    for k in range(DT):
                        mm = e.transpose(pbf[:, bp, k * 128:(k + 1) * 128],
                                         NTB[:, b, k * 128:(k + 1) * 128],
                                         self.ident_bf[:])
                    return mm
                p.task("tensor", tp_n, reads=(f"NTB{b}", "identity_bf"),
                       writes=(f"pbf{bp}",), tag=f"p0.tpn{t}")

                def ev_n(e, t=t, bp=bp):
                    for k in range(DT):
                        mm = e.activation(
                            ABF[:, k, t * 128:(t + 1) * 128],
                            pbf[:, bp, k * 128:(k + 1) * 128],
                            AF.Identity, bias=self.lnb_sb[:, k:k + 1],
                            scale=self.lng_sb[:, k:k + 1])
                    return mm
                p.task("scalar", ev_n, reads=(f"pbf{bp}", "lng_t", "lnb_t"),
                       writes=(f"ABF.{t}",), tag=f"p0.evn{t}")

                def tp_x(e, b=b, bp=bp):
                    for k in range(DT):
                        mm = e.transpose(pxf[:, bp, k * 128:(k + 1) * 128],
                                         XNT[:, b, k * 128:(k + 1) * 128],
                                         self.ident_f32[:])
                    return mm
                p.task("tensor", tp_x, reads=(f"XNT{b}", "identity_f32"),
                       writes=(f"pxf{bp}",), tag=f"p0.tpx{t}")

                def ev_x(e, t=t, b=b, bp=bp):
                    return e.tensor_copy(
                        XNW[:, bp, :, :],
                        pxf[:, bp, :].rearrange("p (a b) -> p a b", b=128))
                p.task("vector", ev_x, reads=(f"pxf{bp}",),
                       writes=(f"XNW{bp}",), tag=f"p0.evx{t}")
                p.dma("gpsimd",
                      self.xn_scr.rearrange("k p n -> p k n")[:, :, t * 128:(t + 1) * 128],
                      XNW[:, bp, :, :],
                      reads=(f"XNW{bp}",), writes=(f"xn_scr.{t}",),
                      tag=f"p0.spill{t}")

            stageA(0)
            stageA(1)
            stageB(0)
            for t in range(NT):
                if t < DT:
                    p.dma("gpsimd", self.WQK[:, t, :],
                          h["W_qk"][t * 128:(t + 1) * 128, :],
                          writes=(f"WQK{t}",), tag=f"p0.wqk{t}")
                if t + 2 < NT:
                    stageA(t + 2)
                if t + 1 < NT:
                    stageB(t + 1)
                stageC(t)

            allxn = tuple(f"xn_scr.{t}" for t in range(NT))
            for k in range(DT):
                p.wait_only("sync", {}, "noop")
            if "dbg_xn" in self.dbg_h:
                for k in range(DT):
                    rid = f"dbg_xn.{k}"
                    p.dma("gpsimd", self.dbg_h["dbg_xn"][k], self.xn_scr[k],
                          reads=allxn, writes=(rid,), tag=rid)
                    self.out_res.append(rid)
            self.dump_bf("dbg_normed", lambda i: ABF[:, i, :], DT,
                         tuple(f"ABF.{t}" for t in range(NT)))
            p.flush("p0")

    def attn_p1(self):
        nc, p, h = self.nc, self.p, self.h
        ABF, QK, GBC, WQK = self.ABF, self.QK, self.GBC, self.WQK
        abf_res = tuple(f"ABF.{t}" for t in range(NT))
        with ExitStack() as S1a:
            GBR = S1a.enter_context(nc.sbuf_tensor("GBR", [1, 4 * QKD], BF16))
            BQR = S1a.enter_context(nc.sbuf_tensor("BQR", [1, QKD], BF16))
            pqk = S1a.enter_context(nc.psum_tensor("pqk", [128, 2, 512], F32))
            pgb = S1a.enter_context(nc.psum_tensor("pgb", [128, 2, 512], F32))

            p.dma("gpsimd", GBR[:], h["gb_row"][:], writes=("GBR",), tag="p1.gbr")
            p.dma("gpsimd", BQR[:], h["bqk_row"][:], writes=("BQR",), tag="p1.bqr")
            for r in range(4):
                for c in range(4):
                    pb = (r * 4 + c) % 2
                    p.task("tensor", lambda e, r=r, c=c, pb=pb: e.matmul(
                        pgb[:, pb, :], self.ones1[:],
                        GBR[:, r * QKD + c * 512:r * QKD + (c + 1) * 512],
                        start=True, stop=True),
                        reads=("GBR", "ones1_bf"), writes=(f"pgb{pb}",),
                        tag=f"p1.gbmm{r}.{c}")
                    p.task("vector", lambda e, r=r, c=c, pb=pb: e.tensor_copy(
                        GBC[:, r, c * 512:(c + 1) * 512], pgb[:, pb, :]),
                        reads=(f"pgb{pb}",), writes=(f"GBC{r}.{c}",),
                        tag=f"p1.gbev{r}.{c}")

            wqk_res = tuple(f"WQK{k}" for k in range(DT))
            for t in range(NT):
                for c in range(4):
                    pb = (t * 4 + c) % 2
                    def qkmm(e, t=t, c=c, pb=pb):
                        for k in range(DT):
                            e.matmul(pqk[:, pb, :],
                                     ABF[:, k, t * 128:(t + 1) * 128],
                                     WQK[:, k, c * 512:(c + 1) * 512],
                                     start=(k == 0), stop=False)
                        return e.matmul(pqk[:, pb, :], self.ones1[:],
                                        BQR[:, c * 512:(c + 1) * 512],
                                        start=False, stop=True)
                    p.task("tensor", qkmm,
                           reads=abf_res + wqk_res + ("BQR", "ones1_bf"),
                           writes=(f"pqk{pb}",), tag=f"p1.mm{t}.{c}")
                    p.task("scalar", lambda e, t=t, c=c, pb=pb: e.activation(
                        QK[:, t, c * 512:(c + 1) * 512], pqk[:, pb, :],
                        AF.Silu),
                        reads=(f"pqk{pb}",), writes=(f"QK{t}.{c}",),
                        tag=f"p1.silu{t}.{c}")
            qk_res = tuple(f"QK{t}.{c}" for t in range(NT) for c in range(4))
            self.dump_bf("dbg_qk", lambda i: QK[:, i, :], NT, qk_res)
            p.flush("p1")

    def attn_p2(self):
        nc, p, h = self.nc, self.p, self.h
        QK, GBC = self.QK, self.GBC
        gbc_res = tuple(f"GBC{r}.{c}" for r in range(4) for c in range(4))
        small = self.small
        with ExitStack() as S2:
            KF = S2.enter_context(nc.sbuf_tensor("KF", [128, NT, QKD], BF16))
            QT2 = S2.enter_context(nc.sbuf_tensor("QT2", [128, 2, NT, 256], BF16))
            EB = S2.enter_context(nc.sbuf_tensor("EB", [128, QKD], BF16))
            psim = S2.enter_context(nc.psum_tensor("psim", [128, 2, QKD], F32))

            for t in range(NT):
                p.task("vector", lambda e, t=t: e.tensor_tensor(
                    KF[:, t, :], QK[:, t, :], GBC[:, 2, :], op=ALU.mult),
                    reads=(f"QK{t}.0", f"QK{t}.1", f"QK{t}.2", f"QK{t}.3")
                    + gbc_res,
                    writes=(f"KF{t}",), tag=f"p2.km{t}")
                p.task("vector", lambda e, t=t: e.tensor_tensor(
                    KF[:, t, :], KF[:, t, :], GBC[:, 3, :], op=ALU.add),
                    reads=(f"KF{t}",), writes=(f"KF{t}",), tag=f"p2.ka{t}")
            kf_res = tuple(f"KF{t}" for t in range(NT))

            def emit_qprep(dp):
                qp = dp % 2
                for kt in range(NT):
                    p.task("vector", lambda e, dp=dp, kt=kt, qp=qp: e.tensor_tensor(
                        QT2[:, qp, kt, :], QK[:, kt, dp * 256:(dp + 1) * 256],
                        GBC[:, 0, dp * 256:(dp + 1) * 256], op=ALU.mult),
                        reads=(f"QK{kt}.{dp // 2}",) + gbc_res,
                        writes=(f"QT2{qp}.{kt}",), tag=f"p2.qm{dp}.{kt}")
                    p.task("vector", lambda e, dp=dp, kt=kt, qp=qp: e.tensor_tensor(
                        QT2[:, qp, kt, :], QT2[:, qp, kt, :],
                        GBC[:, 1, dp * 256:(dp + 1) * 256], op=ALU.add),
                        reads=(f"QT2{qp}.{kt}",), writes=(f"QT2{qp}.{kt}",),
                        tag=f"p2.qa{dp}.{kt}")

            emit_qprep(0)
            for dp in range(8):
                qp = dp % 2
                if dp + 1 < 8:
                    emit_qprep(dp + 1)
                qt_res = tuple(f"QT2{qp}.{kt}" for kt in range(NT))
                for dt in range(dp * 2, dp * 2 + 2):
                    pb = dt % 2
                    dl = (dt % 2) * 128
                    def simmm(e, dt=dt, pb=pb, dl=dl, qp=qp):
                        for c in range(4):
                            for kt in range(NT):
                                mm = e.matmul(
                                    psim[:, pb, c * 512:(c + 1) * 512],
                                    QT2[:, qp, kt, dl:dl + 128],
                                    KF[:, kt, c * 512:(c + 1) * 512],
                                    start=(kt == 0), stop=(kt == NT - 1))
                        return mm
                    p.task("tensor", simmm, reads=qt_res + kf_res,
                           writes=(f"psim{pb}",), tag=f"p2.sim{dt}")
                    sb = pb * 4
                    p.task("vector", lambda e, pb=pb, sb=sb: e.tensor_reduce(
                        small[:, sb:sb + 1], psim[:, pb, :], axis=AX.X, op=ALU.max),
                        reads=(f"psim{pb}",), writes=(f"p2max{pb}",),
                        tag=f"p2.max{dt}")
                    p.task("vector", lambda e, sb=sb: e.tensor_scalar(
                        small[:, sb + 1:sb + 2], small[:, sb:sb + 1], -1.0 / N,
                        None, op0=ALU.mult),
                        reads=(f"p2max{pb}",), writes=(f"p2bias{pb}",),
                        tag=f"p2.b{dt}")
                    p.task("scalar", lambda e, pb=pb, sb=sb: e.activation(
                        EB[:, :], psim[:, pb, :], AF.Exp,
                        bias=small[:, sb + 1:sb + 2], scale=1.0 / N,
                        accum_out=small[:, sb + 2:sb + 3]),
                        reads=(f"psim{pb}", f"p2bias{pb}"),
                        writes=("EB0", f"p2z{pb}"), tag=f"p2.exp{dt}")
                    p.task("vector", lambda e, sb=sb: e.reciprocal(
                        small[:, sb + 3:sb + 4], small[:, sb + 2:sb + 3]),
                        reads=(f"p2z{pb}",), writes=(f"p2rz{pb}",),
                        tag=f"p2.rz{dt}")
                    p.task("vector", lambda e, dt=dt, sb=sb: e.tensor_tensor(
                        self.rowsum[:, dt:dt + 1], small[:, sb + 2:sb + 3],
                        small[:, sb + 3:sb + 4], op=ALU.mult),
                        reads=(f"p2z{pb}", f"p2rz{pb}"),
                        writes=(f"rowsum{dt}",), tag=f"p2.rs{dt}")
            if "dbg_rowsum" in self.dbg_h:
                p.dma("sync", self.dbg_h["dbg_rowsum"][:], self.rowsum[:],
                      reads=tuple(f"rowsum{dt}" for dt in range(NT)),
                      writes=("dbg_rowsum.0",), tag="dbg.rowsum")
                self.out_res.append("dbg_rowsum.0")
            p.flush("p2")

    def attn_p34(self):
        nc, p, h = self.nc, self.p, self.h
        ABF = self.ABF
        abf_res = tuple(f"ABF.{t}" for t in range(NT))
        with ExitStack() as S3:
            VG = S3.enter_context(nc.sbuf_tensor("VG", [128, NT, N], BF16))
            with ExitStack() as S3a:
                WH = S3a.enter_context(
                    nc.sbuf_tensor("WH", [128, DT, 2 * HIDDEN], BF16))
                VT = S3a.enter_context(nc.sbuf_tensor("VT", [128, 2, 2048], BF16))
                GT = S3a.enter_context(nc.sbuf_tensor("GT", [128, 2, 2048], BF16))
                pv = S3a.enter_context(nc.psum_tensor("pv", [128, 2, 512], F32))
                pg = S3a.enter_context(nc.psum_tensor("pg", [128, 2, 512], F32))
                for k in range(DT):
                    p.dma("gpsimd", WH[:, k, :],
                          h["W_hidden"][k * 128:(k + 1) * 128, :],
                          writes=(f"WH{k}",), tag=f"p3.wh{k}")
                wh_res = tuple(f"WH{k}" for k in range(DT))
                for hp in range(NT):
                    pb = hp % 2
                    for c in range(4):
                        sc = c % 2
                        def vmm(e, hp=hp, c=c, sc=sc):
                            for k in range(DT):
                                mm = e.matmul(
                                    pv[:, sc, :],
                                    WH[:, k, hp * 128:(hp + 1) * 128],
                                    ABF[:, k, c * 512:(c + 1) * 512],
                                    start=(k == 0), stop=(k == DT - 1))
                            return mm
                        p.task("tensor", vmm, reads=abf_res + wh_res,
                               writes=(f"pv{sc}",), tag=f"p3.vmm{hp}.{c}")
                        def gmm(e, hp=hp, c=c, sc=sc):
                            for k in range(DT):
                                mm = e.matmul(
                                    pg[:, sc, :],
                                    WH[:, k, HIDDEN + hp * 128:
                                       HIDDEN + (hp + 1) * 128],
                                    ABF[:, k, c * 512:(c + 1) * 512],
                                    start=(k == 0), stop=(k == DT - 1))
                            return mm
                        p.task("tensor", gmm, reads=abf_res + wh_res,
                               writes=(f"pg{sc}",), tag=f"p3.gmm{hp}.{c}")
                        p.task("scalar", lambda e, hp=hp, c=c, sc=sc, pb=pb:
                               e.activation(
                                   VT[:, pb, c * 512:(c + 1) * 512], pv[:, sc, :],
                                   AF.Silu, bias=self.bh_sb[:, hp:hp + 1]),
                               reads=(f"pv{sc}", "bh_t"),
                               writes=(f"VT{pb}.{c}",), tag=f"p3.sv{hp}.{c}")
                        p.task("scalar", lambda e, hp=hp, c=c, sc=sc, pb=pb:
                               e.activation(
                                   GT[:, pb, c * 512:(c + 1) * 512], pg[:, sc, :],
                                   AF.Silu, bias=self.bh_sb[:, 16 + hp:17 + hp]),
                               reads=(f"pg{sc}", "bh_t"),
                               writes=(f"GT{pb}.{c}",), tag=f"p3.sg{hp}.{c}")
                    p.task("vector", lambda e, hp=hp, pb=pb: e.tensor_tensor(
                        VG[:, hp, :], VT[:, pb, :], GT[:, pb, :], op=ALU.mult),
                        reads=tuple(f"VT{pb}.{c}" for c in range(4))
                        + tuple(f"GT{pb}.{c}" for c in range(4)),
                        writes=(f"VGm{hp}",), tag=f"p3.vg{hp}")
                    p.task("vector", lambda e, hp=hp: e.tensor_scalar(
                        VG[:, hp, :], VG[:, hp, :],
                        self.rowsum[:, hp:hp + 1], None, op0=ALU.mult),
                        reads=(f"VGm{hp}", f"rowsum{hp}"),
                        writes=(f"VG{hp}",), tag=f"p3.vgr{hp}")
                vg_res = tuple(f"VG{hp}" for hp in range(NT))
                self.dump_bf("dbg_vg", lambda i: VG[:, i, :], NT, vg_res)
                p.flush("p3")
            if self.stop_after == "p3":
                return

            with ExitStack() as S4:
                WO = S4.enter_context(nc.sbuf_tensor("WO", [128, NT, D], BF16))
                XNB = S4.enter_context(nc.sbuf_tensor("XNB", [128, 2, N], F32))
                SB = S4.enter_context(nc.sbuf_tensor("SB", [128, 2, N], F32))
                ps = S4.enter_context(nc.psum_tensor("ps", [128, 2, 512], F32))
                for k in range(NT):
                    p.dma("gpsimd", WO[:, k, :],
                          h["W_out"][k * 128:(k + 1) * 128, :],
                          writes=(f"WO{k}",), tag=f"p4.wo{k}")
                wo_res = tuple(f"WO{k}" for k in range(NT))
                for mo in range(DT):
                    b = mo % 2
                    p.dma("sync", XNB[:, b, :], self.xn_scr[mo],
                          reads=tuple(f"xn_scr.{t}" for t in range(NT)),
                          writes=(f"XNB{b}",), tag=f"p4.xn{mo}")
                    for c in range(4):
                        pb = c % 2
                        def smm(e, mo=mo, c=c, pb=pb):
                            for k in range(NT):
                                mm = e.matmul(
                                    ps[:, pb, :],
                                    WO[:, k, mo * 128:(mo + 1) * 128],
                                    VG[:, k, c * 512:(c + 1) * 512],
                                    start=(k == 0), stop=(k == NT - 1))
                            return mm
                        p.task("tensor", smm, reads=vg_res + wo_res,
                               writes=(f"ps{pb}",), tag=f"p4.mm{mo}.{c}")
                        p.task("scalar", lambda e, c=c, pb=pb, b=b, mo=mo:
                               e.activation(
                                   SB[:, b, c * 512:(c + 1) * 512], ps[:, pb, :],
                                   AF.Identity, bias=self.bout_sb[:, mo:mo + 1]),
                               reads=(f"ps{pb}", "bout_t"), writes=(f"SB{b}",),
                               tag=f"p4.ev{mo}.{c}")
                    p.task("vector", lambda e, b=b: e.tensor_tensor(
                        SB[:, b, :], SB[:, b, :], XNB[:, b, :], op=ALU.add),
                        reads=(f"XNB{b}", f"SB{b}"),
                        writes=(f"SB{b}",), tag=f"p4.add{mo}")
                    p.dma("gpsimd", self.s_scr[mo], SB[:, b, :],
                          reads=(f"SB{b}",), writes=(f"s_scr{mo}",),
                          tag=f"p4.spill{mo}")
                    if "dbg_s" in self.dbg_h:
                        rid = f"dbg_s.{mo}"
                        p.dma("gpsimd", self.dbg_h["dbg_s"][mo], SB[:, b, :],
                              reads=(f"SB{b}",), writes=(rid,), tag=rid)
                        self.out_res.append(rid)
                p.flush("p4")

    def ffn_slot(self):
        nc, p, h = self.nc, self.p, self.h
        with ExitStack() as SF:
            XF = SF.enter_context(nc.sbuf_tensor("XF", [128, DT, N], F32))
            ROTT = SF.enter_context(nc.sbuf_tensor("ROTT", [128, 128], BF16))
            E2C = SF.enter_context(nc.sbuf_tensor("E2C", [128, 2], BF16))
            self.XF = XF
            self.ROTTsb, self.E2Csb = ROTT, E2C
            for k in range(DT):
                p.dma("sync", XF[:, k, :], self.xn_scr[k],
                      reads=tuple(f"xn_scr.{t}" for t in range(NT)),
                      writes=(f"XF{k}",), tag=f"pf.xf{k}")
            p.dma("sync", ROTT[:], h["rotT_bf"][:], writes=("ROTT",), tag="pf.rt")
            p.dma("sync", E2C[:], h["e2c_bf"][:], writes=("E2C",), tag="pf.e2c")
            for l in range(2):
                if self.ffn_layer(l):
                    return
            if self.stop_after is None or self.stop_after == "final":
                self.final_phase()

    def ffn_layer(self, l):
        nc, p, h = self.nc, self.p, self.h
        ABF, XF = self.ABF, self.XF
        sa = self.stop_after or ""
        xf_res = tuple(f"XF{k}" for k in range(DT))

        for k in range(DT):
            if k < 4:
                p.task("vector", lambda e, k=k: e.tensor_copy(
                    ABF[:, k, :], XF[:, k, :]),
                    reads=(f"XF{k}",), writes=(f"ABF.{l}{k}",), tag=f"f{l}.sh{k}")
            else:
                def shift(e, k=k):
                    e.memset(ABF[:, k, 0:1], 0.0)
                    return e.tensor_copy(ABF[:, k, 1:N], XF[:, k, 0:N - 1])
                p.task("vector", shift, reads=(f"XF{k}",),
                       writes=(f"ABF.{l}{k}",), tag=f"f{l}.sh{k}")
        abf_res = tuple(f"ABF.{l}{k}" for k in range(DT))

        with ExitStack() as SL:
            QP = SL.enter_context(nc.sbuf_tensor(f"QP{l}", [128, 4, N], BF16))
            KP = SL.enter_context(nc.sbuf_tensor(f"KP{l}", [128, 4, N], BF16))
            VTM = SL.enter_context(nc.sbuf_tensor(f"VTM{l}", [128, NT, INNER], BF16))

            with ExitStack() as S:
                ROT = S.enter_context(nc.sbuf_tensor(f"ROT{l}", [128, 2, N], BF16))
                WQKV = S.enter_context(
                    nc.sbuf_tensor(f"WQKV{l}", [128, DT, 512], BF16))
                QW = S.enter_context(nc.sbuf_tensor(f"QW{l}", [128, 2, N], BF16))
                SQ = S.enter_context(nc.sbuf_tensor(f"SQb{l}", [128, 2, 512], BF16))
                T1 = S.enter_context(nc.sbuf_tensor(f"T1{l}", [128, 1, 512], BF16))
                T2 = None
                VF = S.enter_context(nc.sbuf_tensor(f"VF{l}", [128, 2, N], BF16))
                ROWS = S.enter_context(nc.sbuf_tensor(f"ROWS{l}", [128, 2, N], F32))
                RSTDB = S.enter_context(nc.sbuf_tensor(f"RSTDB{l}", [2, 2, N], BF16))
                ESC = S.enter_context(nc.sbuf_tensor(f"ESC{l}", [2, 2, 128], BF16))
                QSL = S.enter_context(nc.sbuf_tensor(f"QSL{l}", [2, 2, 128], BF16))
                pqkv = S.enter_context(nc.psum_tensor(f"pqkv{l}", [128, 2, 512], F32))
                psq = S.enter_context(nc.psum_tensor(f"psq{l}", [2, 2, 512], F32))
                pex = S.enter_context(nc.psum_tensor(f"pex{l}", [128, 1, 512], F32))
                pR = S.enter_context(nc.psum_tensor(f"pR{l}", [128, 1, 512], F32))
                ptv = S.enter_context(nc.psum_tensor(f"ptv{l}", [128, 2, 1024], BF16))
                self.ROTsb = ROT

                for r in range(2):
                    p.dma("sync", ROT[:, r, :], h["rotary"][r],
                          writes=(f"ROT{r}",), tag=f"f{l}.rotq{r}")
                p.dma("gpsimd", QSL[:, 0, :], h["qs_tile"][l], writes=("QSL0",),
                      tag=f"f{l}.qs")
                p.dma("gpsimd", QSL[:, 1, :], h["ks_tile"][l], writes=("QSL1",),
                      tag=f"f{l}.ks")
                for r in range(2):
                    p.task("vector", lambda e, r=r: e.tensor_tensor(
                        ESC[:, r, :], self.e2t_sb[:], QSL[:, r, :], op=ALU.mult),
                        reads=(f"QSL{r}", "e2t"), writes=(f"ESC{r}",),
                        tag=f"f{l}.esc{r}")

                def post_work(m):
                    pb = m % 2
                    if m < 8:
                        self._qk_chain(l, m, pb, QP if m < 4 else KP,
                                       QW, SQ, T1, T2, ROWS, RSTDB, ESC,
                                       psq, pex, pR)
                        return
                    vm = m - 8
                    vf_res = tuple(f"VF{pb}.{c}" for c in range(4))
                    for jg in range(4):
                        sg = jg % 2
                        def vtp(e, vm=vm, jg=jg, sg=sg, pb=pb):
                            for jj in range(4):
                                jc = jg * 4 + jj
                                mm = e.transpose(
                                    ptv[:, sg, jj * 128:(jj + 1) * 128],
                                    VF[:, pb, jc * 128:(jc + 1) * 128],
                                    self.ident_bf[:])
                            return mm
                        p.task("tensor", vtp,
                               reads=vf_res + ("identity_bf",),
                               writes=(f"ptv{sg}",),
                               tag=f"f{l}.vtp{vm}.{jg}")
                        def vev(e, vm=vm, jg=jg, sg=sg):
                            return e.tensor_copy(
                                VTM[:, jg * 4:(jg + 1) * 4,
                                    vm * 128:(vm + 1) * 128],
                                ptv[:, sg, 0:512].rearrange(
                                    "p (jj c) -> p jj c", jj=4))
                        p.task("vector", vev, reads=(f"ptv{sg}",),
                               writes=(f"VTM{vm}.{jg}",),
                               tag=f"f{l}.vevt{vm}.{jg}")

                pending = None
                for stage in range(3):
                    for k in range(DT):
                        p.dma("gpsimd", WQKV[:, k, :],
                              h["wqkv"][l][k * 128:(k + 1) * 128,
                                           stage * 512:(stage + 1) * 512],
                              writes=(f"WQKV{k}",), tag=f"f{l}.wqkv{stage}.{k}")
                    wqkv_res = tuple(f"WQKV{k}" for k in range(DT))
                    for ms in range(4):
                        m = stage * 4 + ms
                        pb = m % 2
                        for c in range(4):
                            sc = c % 2
                            def qkvmm(e, ms=ms, c=c, sc=sc):
                                for k in range(DT):
                                    mm = e.matmul(
                                        pqkv[:, sc, :],
                                        WQKV[:, k, ms * 128:(ms + 1) * 128],
                                        ABF[:, k, c * 512:(c + 1) * 512],
                                        start=(k == 0), stop=(k == DT - 1))
                                return mm
                            p.task("tensor", qkvmm, reads=abf_res + wqkv_res,
                                   writes=(f"pqkv{sc}",), tag=f"f{l}.qkv{m}.{c}")
                            if m < 8:
                                p.task("scalar", lambda e, c=c, sc=sc, pb=pb:
                                       e.activation(
                                           QW[:, pb, c * 512:(c + 1) * 512],
                                           pqkv[:, sc, :], AF.Identity),
                                       reads=(f"pqkv{sc}",),
                                       writes=(f"QW{pb}.{c}",),
                                       tag=f"f{l}.qev{m}.{c}")
                            else:
                                p.task("vector", lambda e, c=c, sc=sc, pb=pb:
                                       e.tensor_copy(
                                           VF[:, pb, c * 512:(c + 1) * 512],
                                           pqkv[:, sc, :]),
                                       reads=(f"pqkv{sc}",), writes=(f"VF{pb}.{c}",),
                                       tag=f"f{l}.vev{m}.{c}")
                        if pending is not None:
                            post_work(pending)
                            if pending == 3:
                                for r in range(2):
                                    p.dma("sync", ROT[:, r, :], h["rotary"][2 + r],
                                          writes=(f"ROT{r}",), tag=f"f{l}.rotk{r}")
                        pending = m
                post_work(pending)
                qp_res = tuple(f"QPf{m}" for m in range(4))
                kp_res = tuple(f"KPf{m}" for m in range(4))
                vtm_res = tuple(f"VTM{vm}.{jg}" for vm in range(4) for jg in range(4))
                if l == 0:
                    self.dump_bf("dbg_qp", lambda i: QP[:, i, :], 4, qp_res)
                    self.dump_bf("dbg_kp", lambda i: KP[:, i, :], 4, kp_res)
                    self.dump_bf("dbg_vt", lambda i: VTM[:, i, :], NT, vtm_res)
                p.flush(f"f{l}.qkv")
            if sa == f"f{l}qkv":
                return True

            oa_res = tuple(f"OA{hp}.{w}" for hp in range(4) for w in range(NW))
            with ExitStack() as S34:
                OA = S34.enter_context(nc.sbuf_tensor(f"OA{l}", [128, 4, N], BF16))
                with ExitStack() as S3:
                    PB = S3.enter_context(
                        nc.sbuf_tensor(f"PB{l}", [128, 2, 2, 1024], BF16))
                    PT = S3.enter_context(
                        nc.sbuf_tensor(f"PT{l}", [128, 2, 8, 512], BF16))
                    MSK = S3.enter_context(
                        nc.sbuf_tensor(f"MSK{l}", [128, 4, 1024], BF16))
                    self.MSKsb = MSK
                    for r in range(4):
                        p.dma("sync", MSK[:, r, :], h["mask1"][r],
                              writes=(f"MSK{r}",), tag=f"f{l}.msk{r}")
                    pS = S3.enter_context(
                        nc.psum_tensor(f"pS{l}", [128, 3, 1024], F32))
                    pPT = S3.enter_context(
                        nc.psum_tensor(f"pPT{l}", [128, 1024], BF16))
                    pO = S3.enter_context(
                        nc.psum_tensor(f"pO{l}", [128, 512], F32))
                    if "dbg_s2" in self.dbg_h:
                        self.SDUMP = S3.enter_context(
                            nc.sbuf_tensor(f"SDUMP{l}", [128, 1024], F32))
                    self._local_attn(l, QP, KP, VTM, OA, PB, PT, pS, pPT, pO)
                    if l == 0:
                        self.dump_bf("dbg_oall", lambda i: OA[:, i, :], 4, oa_res)
                    p.flush(f"f{l}.la")
                if sa == f"f{l}la":
                    return True

                with ExitStack() as S3b:
                    WOF = S3b.enter_context(
                        nc.sbuf_tensor(f"WOF{l}", [128, 4, D], BF16))
                    pwo = S3b.enter_context(
                        nc.psum_tensor(f"pwo{l}", [128, 2, 512], F32))
                    for k in range(4):
                        p.dma("gpsimd", WOF[:, k, :],
                              h["wo"][l][k * 128:(k + 1) * 128, :],
                              writes=(f"WOF{k}",), tag=f"f{l}.wo{k}")
                    wof_res = tuple(f"WOF{k}" for k in range(4))
                    for mo in range(DT):
                        for c in range(4):
                            pb = (mo * 4 + c) % 2
                            def womm(e, mo=mo, c=c, pb=pb):
                                for k in range(4):
                                    mm = e.matmul(
                                        pwo[:, pb, :],
                                        WOF[:, k, mo * 128:(mo + 1) * 128],
                                        OA[:, k, c * 512:(c + 1) * 512],
                                        start=(k == 0), stop=(k == 3))
                                return mm
                            p.task("tensor", womm, reads=oa_res + wof_res,
                                   writes=(f"pwo{pb}",), tag=f"f{l}.womm{mo}.{c}")
                            p.task("vector", lambda e, mo=mo, c=c, pb=pb:
                                   e.tensor_tensor(
                                       XF[:, mo, c * 512:(c + 1) * 512],
                                       XF[:, mo, c * 512:(c + 1) * 512],
                                       pwo[:, pb, :], op=ALU.add),
                                   reads=(f"pwo{pb}", f"XF{mo}"),
                                   writes=(f"XF{mo}",),
                                   tag=f"f{l}.woadd{mo}.{c}")
                    if l == 0:
                        self.dump_f32("dbg_xf", lambda i: XF[:, i, :], DT, xf_res)
                    p.flush(f"f{l}.attn")
            if sa == f"f{l}attn":
                return True

        with ExitStack() as S5:
            N2 = S5.enter_context(nc.sbuf_tensor(f"N2{l}", [128, DT, N], BF16))
            self.N2 = N2
            with ExitStack() as S5a:
                TDIF = S5a.enter_context(
                    nc.sbuf_tensor(f"TDIF{l}", [128, DT, N], BF16))
                MUB = S5a.enter_context(nc.sbuf_tensor(f"MUB{l}", [128, 2, N], BF16))
                SQS = S5a.enter_context(nc.sbuf_tensor(f"SQS{l}", [128, 2, 512], BF16))
                MRB = S5a.enter_context(nc.sbuf_tensor(f"MRB{l}", [1, 2, N], BF16))
                LNF = S5a.enter_context(nc.sbuf_tensor(f"LNF{l}", [1, N], F32))
                T2B = S5a.enter_context(nc.sbuf_tensor(f"T2B{l}", [128, 2, N], BF16))
                FLG = S5a.enter_context(nc.sbuf_tensor(f"FLG{l}", [128, 8], F32))
                FLB = S5a.enter_context(nc.sbuf_tensor(f"FLB{l}", [128, 8], F32))
                pst = S5a.enter_context(nc.psum_tensor(f"pst{l}", [1, 2, 512], F32))
                pex2 = S5a.enter_context(nc.psum_tensor(f"pex2{l}", [128, 2, 512], F32))
                p.dma("sync", FLG[:], h["flg_t"][l], writes=("FLG",), tag=f"f{l}.flg")
                p.dma("sync", FLB[:], h["flb_t"][l], writes=("FLB",), tag=f"f{l}.flb")
                for k in range(DT):
                    if k < 4:
                        p.task("vector", lambda e, k=k: e.tensor_copy(
                            ABF[:, k, :], XF[:, k, :]),
                            reads=(f"XF{k}",), writes=(f"ABF.{l}{k}",),
                            tag=f"f{l}.sh2{k}")
                    else:
                        def shift2(e, k=k):
                            e.memset(ABF[:, k, 0:1], 0.0)
                            return e.tensor_copy(ABF[:, k, 1:N], XF[:, k, 0:N - 1])
                        p.task("vector", shift2, reads=(f"XF{k}",),
                               writes=(f"ABF.{l}{k}",), tag=f"f{l}.sh2{k}")
                abf_res2 = tuple(f"ABF.{l}{k}" for k in range(DT))
                for c in range(4):
                    sc = c % 2
                    def sxmm(e, c=c, sc=sc):
                        for k in range(DT):
                            mm = e.matmul(pst[:, sc, :], self.ones_col[:],
                                          ABF[:, k, c * 512:(c + 1) * 512],
                                          start=(k == 0), stop=(k == DT - 1))
                        return mm
                    p.task("tensor", sxmm, reads=abf_res2 + ("ones_col_bf",),
                           writes=(f"pst{sc}",), tag=f"f{l}.sx{c}")
                    p.task("scalar", lambda e, c=c, sc=sc: e.activation(
                        MRB[:, 0, c * 512:(c + 1) * 512], pst[:, sc, :],
                        AF.Identity, scale=1.0 / D),
                        reads=(f"pst{sc}",), writes=(f"MRBmu.{c}",),
                        tag=f"f{l}.sxe{c}")
                mu_res = tuple(f"MRBmu.{c}" for c in range(4))
                for c in range(4):
                    pb = c % 2
                    p.task("tensor", lambda e, c=c, pb=pb: e.matmul(
                        pex2[:, pb, :], self.ones1[:],
                        MRB[:, 0, c * 512:(c + 1) * 512], start=True, stop=True),
                        reads=mu_res + ("ones1_bf",), writes=(f"pex2{pb}",),
                        tag=f"f{l}.mex{c}")
                    p.task("vector", lambda e, c=c, pb=pb: e.tensor_copy(
                        MUB[:, 0, c * 512:(c + 1) * 512], pex2[:, pb, :]),
                        reads=(f"pex2{pb}",), writes=(f"MUB0.{c}",),
                        tag=f"f{l}.mexe{c}")
                mub0_res = tuple(f"MUB0.{c}" for c in range(4))
                for k in range(DT):
                    p.task("vector", lambda e, k=k: e.tensor_tensor(
                        TDIF[:, k, :], ABF[:, k, :], MUB[:, 0, :], op=ALU.subtract),
                        reads=(f"ABF.{l}{k}",) + mub0_res, writes=(f"TDIF{k}",),
                        tag=f"f{l}.td{k}")
                tdif_res = tuple(f"TDIF{k}" for k in range(DT))
                for c in range(4):
                    sc = c % 2
                    for k in range(DT):
                        p.task("vector", lambda e, k=k, c=c: e.tensor_tensor(
                            SQS[:, k % 2, :], TDIF[:, k, c * 512:(c + 1) * 512],
                            TDIF[:, k, c * 512:(c + 1) * 512], op=ALU.mult),
                            reads=(f"TDIF{k}",), writes=(f"SQS{k % 2}",),
                            tag=f"f{l}.sq{c}.{k}")
                        p.task("tensor", lambda e, k=k, c=c, sc=sc: e.matmul(
                            pst[:, sc, :], self.ones_col[:], SQS[:, k % 2, :],
                            start=(k == 0), stop=(k == DT - 1)),
                            reads=(f"SQS{k % 2}", "ones_col_bf"),
                            writes=(f"pst{sc}",),
                            tag=f"f{l}.vmm{c}.{k}")
                    p.task("scalar", lambda e, c=c, sc=sc: e.activation(
                        LNF[:, c * 512:(c + 1) * 512], pst[:, sc, :], AF.Ln,
                        scale=1.0 / D, bias=self.epsv[0:1, :]),
                        reads=(f"pst{sc}", "epsv"), writes=(f"LNF.{c}",),
                        tag=f"f{l}.vln{c}")
                p.task("scalar", lambda e: e.activation(
                    MRB[:, 1, :], LNF[:], AF.Exp, scale=-0.5),
                    reads=tuple(f"LNF.{c}" for c in range(4)),
                    writes=("MRBrs",), tag=f"f{l}.vex")
                for c in range(4):
                    pb = c % 2
                    p.task("tensor", lambda e, c=c, pb=pb: e.matmul(
                        pex2[:, pb, :], self.ones1[:],
                        MRB[:, 1, c * 512:(c + 1) * 512], start=True, stop=True),
                        reads=("MRBrs", "ones1_bf"), writes=(f"pex2{pb}",),
                        tag=f"f{l}.rex{c}")
                    p.task("vector", lambda e, c=c, pb=pb: e.tensor_copy(
                        MUB[:, 1, c * 512:(c + 1) * 512], pex2[:, pb, :]),
                        reads=(f"pex2{pb}",), writes=(f"MUB1.{c}",),
                        tag=f"f{l}.rexe{c}")
                mub1_res = tuple(f"MUB1.{c}" for c in range(4))
                for k in range(DT):
                    pb = k % 2
                    p.task("vector", lambda e, k=k, pb=pb: e.tensor_tensor(
                        T2B[:, pb, :], TDIF[:, k, :], MUB[:, 1, :], op=ALU.mult),
                        reads=(f"TDIF{k}",) + mub1_res, writes=(f"T2B{pb}",),
                        tag=f"f{l}.n2m{k}")
                    p.task("scalar", lambda e, k=k, pb=pb: e.activation(
                        N2[:, k, :], T2B[:, pb, :], AF.Identity,
                        bias=FLB[:, k:k + 1], scale=FLG[:, k:k + 1]),
                        reads=(f"T2B{pb}", "FLG", "FLB"), writes=(f"N2.{k}",),
                        tag=f"f{l}.n2f{k}")
                n2_res = tuple(f"N2.{k}" for k in range(DT))
                if l == 0:
                    self.dump_bf("dbg_n2", lambda i: N2[:, i, :], DT, n2_res)
                p.flush(f"f{l}.ln")
            if sa == f"f{l}ln":
                return True

            with ExitStack() as S6:
                W1B = S6.enter_context(
                    nc.sbuf_tensor(f"W1B{l}", [128, 2, DT, 256], BF16))
                W2B = S6.enter_context(
                    nc.sbuf_tensor(f"W2B{l}", [128, 2, KH, 128], BF16))
                PROD = S6.enter_context(
                    nc.sbuf_tensor(f"PROD{l}", [128, KH, 1024], BF16))
                GEB = S6.enter_context(nc.sbuf_tensor(f"GEB{l}", [128, 2, 1024], BF16))
                B1AR = S6.enter_context(nc.sbuf_tensor(f"B1AR{l}", [1, HIDP], BF16))
                B1G = S6.enter_context(nc.sbuf_tensor(f"B1G{l}", [128, KH], F32))
                B2R = S6.enter_context(nc.sbuf_tensor(f"B2R{l}", [1, D], BF16))
                self.b1g_sb = B1G
                p.dma("gpsimd", B1AR[:], h["b1a_row"][l], writes=("B1AR",),
                      tag=f"f{l}.b1ar")
                p.dma("sync", B1G[:], h["b1g_t"][l], writes=("B1G",),
                      tag=f"f{l}.b1g")
                p.dma("gpsimd", B2R[:], h["b2_row"][l], writes=("B2R",),
                      tag=f"f{l}.b2r")
                prod_res = tuple(f"PROD.{m}" for m in range(KH))
                for nh in range(2):
                    nh0 = nh * 1024
                    with ExitStack() as S6a:
                        pa = S6a.enter_context(
                            nc.psum_tensor(f"pa{l}{nh}", [128, 2, 1024], F32))
                        pg2 = S6a.enter_context(
                            nc.psum_tensor(f"pg2{l}{nh}", [128, 2, 1024], F32))
                        def w1dma(m):
                            p.dma("gpsimd", W1B[:, m % 2, :, :],
                                  h["w1ag"][l][m].rearrange("(k p) c -> p k c",
                                                            p=128),
                                  writes=(f"W1B{m % 2}",), tag=f"f{l}.w1d{nh}.{m}")
                        w1dma(0)
                        w1dma(1)
                        for m in range(KH):
                            self._ffn_mm(l, nh, m, m % 2, pa, pg2, W1B, B1AR,
                                         GEB, PROD)
                            if m + 2 < KH:
                                w1dma(m + 2)
                        p.flush(f"f{l}.w1.{nh}")
                    with ExitStack() as S6b:
                        px = S6b.enter_context(
                            nc.psum_tensor(f"px{l}{nh}", [128, 2, 512], F32))
                        def w2dma(mo):
                            p.dma("gpsimd", W2B[:, mo % 2, :, :],
                                  h["w2cb"][l][mo].rearrange("(k p) c -> p k c",
                                                             p=128),
                                  writes=(f"W2B{mo % 2}",), tag=f"f{l}.w2d{nh}.{mo}")
                        w2dma(0)
                        w2dma(1)
                        for mo in range(DT):
                            s2 = mo % 2
                            for cc in range(2):
                                pb = cc
                                def w2mm(e, mo=mo, s2=s2, cc=cc, pb=pb):
                                    for k in range(KH):
                                        e.matmul(px[:, pb, :], W2B[:, s2, k, :],
                                                 PROD[:, k, cc * 512:(cc + 1) * 512],
                                                 start=(k == 0), stop=False)
                                    return e.matmul(
                                        px[:, pb, :],
                                        B2R[:, mo * 128:(mo + 1) * 128],
                                        self.ones512[:], start=False, stop=True)
                                p.task("tensor", w2mm,
                                       reads=prod_res + (f"W2B{s2}", "B2R",
                                                         "ones512_bf"),
                                       writes=(f"px{pb}",),
                                       tag=f"f{l}.w2mm{nh}.{mo}.{cc}")
                                p.task("vector", lambda e, mo=mo, cc=cc, nh0=nh0,
                                       pb=pb: e.tensor_tensor(
                                    XF[:, mo, nh0 + cc * 512:nh0 + (cc + 1) * 512],
                                    XF[:, mo, nh0 + cc * 512:nh0 + (cc + 1) * 512],
                                    px[:, pb, :], op=ALU.add),
                                    reads=(f"px{pb}", f"XF{mo}"),
                                    writes=(f"XF{mo}",),
                                    tag=f"f{l}.xadd{nh}.{mo}.{cc}")
                            if mo + 2 < DT:
                                w2dma(mo + 2)
                        p.flush(f"f{l}.w2.{nh}")
                if l == 0 and "dbg_xf2" in self.dbg_h:
                    self.dump_f32("dbg_xf2", lambda i: XF[:, i, :], DT, xf_res)
                p.flush(f"f{l}.ffn")
            if sa == f"f{l}ffn":
                return True
        return False

    def _ffn_mm(self, l, nh, m, sl, pa, pg2, W1B, B1AR, GEB, PROD):
        p = self.p
        nh0 = nh * 1024
        mp = m % 2
        n2_res = tuple(f"N2.{k}" for k in range(DT))
        for cc in range(2):
            def amm(e, m=m, sl=sl, cc=cc, nh0=nh0, mp=mp):
                for k in range(DT):
                    e.matmul(pa[:, mp, cc * 512:(cc + 1) * 512],
                             W1B[:, sl, k, 0:128],
                             self.N2[:, k, nh0 + cc * 512:nh0 + (cc + 1) * 512],
                             start=(k == 0), stop=False)
                return e.matmul(pa[:, mp, cc * 512:(cc + 1) * 512],
                                B1AR[:, m * 128:(m + 1) * 128],
                                self.ones512[:], start=False, stop=True)
            p.task("tensor", amm,
                   reads=n2_res + (f"W1B{sl}", "B1AR", "ones512_bf"),
                   writes=(f"pa{mp}",), tag=f"f{l}.amm{nh}.{m}.{cc}")
            def gmm(e, m=m, sl=sl, cc=cc, nh0=nh0, mp=mp):
                for k in range(DT):
                    mm = e.matmul(pg2[:, mp, cc * 512:(cc + 1) * 512],
                                  W1B[:, sl, k, 128:256],
                                  self.N2[:, k, nh0 + cc * 512:nh0 + (cc + 1) * 512],
                                  start=(k == 0), stop=(k == DT - 1))
                return mm
            p.task("tensor", gmm, reads=n2_res + (f"W1B{sl}",),
                   writes=(f"pg2{mp}",), tag=f"f{l}.gmm{nh}.{m}.{cc}")
        p.task("scalar", lambda e, m=m, mp=mp: e.activation(
            GEB[:, mp, :], pg2[:, mp, :],
            AF.Gelu, bias=self.b1g_sb[:, m:m + 1]),
            reads=(f"pg2{mp}", "B1G"), writes=(f"GEB{mp}", f"pg2{mp}"),
            tag=f"f{l}.gelu{nh}.{m}")
        p.task("vector", lambda e, m=m, mp=mp: e.tensor_tensor(
            PROD[:, m, :], pa[:, mp, :],
            GEB[:, mp, :], op=ALU.mult),
            reads=(f"GEB{mp}", f"pa{mp}"),
            writes=(f"PROD.{m}", f"pa{mp}"), tag=f"f{l}.prod{nh}.{m}")

    def _qk_chain(self, l, m, pb, DSTP, QW, SQ, T1, T2, ROWS, RSTDB, ESC,
                  psq, pex, pR):
        p = self.p
        is_q = m < 4
        mt = m if is_q else m - 4
        er = 0 if is_q else 1
        qw_res = tuple(f"QW{pb}.{c}" for c in range(4))
        dst_res = ("QPf" if is_q else "KPf") + str(mt)
        for c in range(4):
            p.task("vector", lambda e, c=c, pb=pb: e.tensor_tensor(
                SQ[:, pb, :], QW[:, pb, c * 512:(c + 1) * 512],
                QW[:, pb, c * 512:(c + 1) * 512], op=ALU.mult),
                reads=(f"QW{pb}.{c}",), writes=(f"SQ{pb}",), tag=f"f{l}.sqr{m}.{c}")
            p.task("tensor", lambda e, c=c, pb=pb: e.matmul(
                psq[:, pb, :], self.E2Csb[:], SQ[:, pb, :],
                start=True, stop=True),
                reads=(f"SQ{pb}", "E2C"), writes=(f"psq{pb}",),
                tag=f"f{l}.ssq{m}.{c}")
            p.task("scalar", lambda e, c=c, pb=pb: e.activation(
                ROWS[0:2, pb, c * 512:(c + 1) * 512], psq[:, pb, :], AF.Ln),
                reads=(f"psq{pb}",), writes=(f"LNR{pb}.{c}",),
                tag=f"f{l}.l2ln{m}.{c}")
        lnr_res = tuple(f"LNR{pb}.{c}" for c in range(4))
        p.task("scalar", lambda e, pb=pb: e.activation(
            ROWS[0:2, pb, :], ROWS[0:2, pb, :], AF.Exp, scale=-0.5),
            reads=lnr_res, writes=(f"LNE{pb}",), tag=f"f{l}.l2ex{m}")
        p.task("vector", lambda e, pb=pb: e.tensor_scalar(
            RSTDB[:, pb, :], ROWS[0:2, pb, :], 1e12, None, op0=ALU.min),
            reads=(f"LNE{pb}",), writes=(f"RSTDB{pb}",), tag=f"f{l}.clip{m}")
        for c in range(4):
            p.task("tensor", lambda e, c=c, pb=pb: e.matmul(
                pex[:, 0, :], ESC[:, er, :], RSTDB[:, pb, c * 512:(c + 1) * 512],
                start=True, stop=True),
                reads=(f"RSTDB{pb}", f"ESC{er}"), writes=("pex0",),
                tag=f"f{l}.exp{m}.{c}")
            p.task("vector", lambda e, c=c: e.tensor_tensor(
                QW[:, pb, c * 512:(c + 1) * 512],
                QW[:, pb, c * 512:(c + 1) * 512], pex[:, 0, :], op=ALU.mult),
                reads=("pex0",) + qw_res, writes=(f"QW{pb}.{c}",),
                tag=f"f{l}.qn{m}.{c}")
        for c in range(4):
            p.task("tensor", lambda e, c=c: e.matmul(
                pR[:, 0, :], self.ROTTsb[:],
                QW[:, pb, c * 512:(c + 1) * 512], start=True, stop=True),
                reads=(f"QW{pb}.{c}", "ROTT"), writes=("pR0",),
                tag=f"f{l}.rot{m}.{c}")
            p.task("vector", lambda e, c=c, mt=mt: e.tensor_tensor(
                DSTP[:, mt, c * 512:(c + 1) * 512], pR[:, 0, :],
                self.ROTsb[:, 1, c * 512:(c + 1) * 512], op=ALU.mult),
                reads=("pR0", "ROT1"), writes=(dst_res,),
                tag=f"f{l}.t2{m}.{c}")
            p.task("vector", lambda e, c=c: e.tensor_tensor(
                T1[:, 0, :], QW[:, pb, c * 512:(c + 1) * 512],
                self.ROTsb[:, 0, c * 512:(c + 1) * 512], op=ALU.mult),
                reads=(f"QW{pb}.{c}", "ROT0"), writes=("T10",),
                tag=f"f{l}.t1{m}.{c}")
            p.task("vector", lambda e, c=c, mt=mt: e.tensor_tensor(
                DSTP[:, mt, c * 512:(c + 1) * 512],
                DSTP[:, mt, c * 512:(c + 1) * 512], T1[:, 0, :],
                op=ALU.add),
                reads=("T10", dst_res), writes=(dst_res,),
                tag=f"f{l}.rsum{m}.{c}")

    def _local_attn(self, l, QP, KP, VTM, OA, PB, PT, pS, pPT, pO):
        p = self.p
        small = self.small
        qp_res = tuple(f"QPf{m}" for m in range(4))
        kp_res = tuple(f"KPf{m}" for m in range(4))
        vtm_res = tuple(f"VTM{vm}.{jg}" for vm in range(4) for jg in range(4))

        def band(w, it):
            if w == 0:
                return 0, (it + 1) * 128
            return it * 128, 640

        def s_task(hp, w, it, u, jt0, iv):
            jlo, W = band(w, it)
            moff = 512 if w == 0 else 0
            chunks = []
            pos = jlo
            while pos < jlo + W:
                end = min(jlo + W, (pos // 512 + 1) * 512)
                chunks.append((pos, end - pos))
                pos = end
            if w == 0:
                mblocks = [(it * 128, 128)]
            else:
                mblocks = [(jlo, 128), (jlo + 512, 128)]
            sl = (it * 2 + u) % 3
            def smm(e, hp=hp, w=w, iv=iv, jt0=jt0, u=u, chunks=chunks,
                    moff=moff, mblocks=mblocks, sl=sl):
                for (lo, n) in chunks:
                    e.matmul(
                        pS[:, sl, lo:lo + n],
                        QP[u * 64:(u + 1) * 64, hp, iv:iv + 128],
                        KP[u * 64:(u + 1) * 64, hp,
                           jt0 * 128 + lo:jt0 * 128 + lo + n],
                        start=True, stop=False)
                for (lo, n) in mblocks:
                    mm = e.matmul(
                        pS[:, sl, lo:lo + n],
                        self.ident_bf[:],
                        self.MSKsb[:, it, moff + lo:moff + lo + n],
                        start=False, stop=True)
                return mm
            p.task("tensor", smm,
                   reads=qp_res + kp_res + ("identity_bf", f"MSK{it}"),
                   writes=(f"pS{sl}",), tag=f"f{l}.S{hp}.{w}.{it}.{u}")

        def softmax(hp, w, it):
            jlo, W = band(w, it)
            for u in range(2):
                sc = 8 + u * 8 + (it % 2) * 4
                sl = (it * 2 + u) % 3
                p.task("vector", lambda e, sl=sl, sc=sc, jlo=jlo, W=W: e.tensor_reduce(
                    small[:, sc:sc + 1], pS[:, sl, jlo:jlo + W], axis=AX.X,
                    op=ALU.max, negate=True),
                    reads=(f"pS{sl}",), writes=(f"negm{u}.{it % 2}",),
                    tag=f"f{l}.mx{hp}.{w}.{it}.{u}")
            for u in range(2):
                sc = 8 + u * 8 + (it % 2) * 4
                sl = (it * 2 + u) % 3
                p.task("scalar", lambda e, u=u, sl=sl, sc=sc, it=it, jlo=jlo, W=W:
                       e.activation(
                           PB[:, it % 2, u, jlo:jlo + W], pS[:, sl, jlo:jlo + W],
                           AF.Exp, bias=small[:, sc:sc + 1],
                           accum_out=small[:, sc + 1:sc + 2]),
                       reads=(f"pS{sl}", f"negm{u}.{it % 2}"),
                       writes=(f"PB{u}.{it % 2}", f"z{u}.{it % 2}"),
                       tag=f"f{l}.ex{hp}.{w}.{it}.{u}")
            for u in range(2):
                sc = 8 + u * 8 + (it % 2) * 4
                p.task("vector", lambda e, sc=sc: e.reciprocal(
                    small[:, sc + 2:sc + 3], small[:, sc + 1:sc + 2]),
                    reads=(f"z{u}.{it % 2}",), writes=(f"rz{u}.{it % 2}",),
                    tag=f"f{l}.rz{hp}.{w}.{it}.{u}")
            for u in range(2):
                sc = 8 + u * 8 + (it % 2) * 4
                p.task("vector", lambda e, u=u, sc=sc, it=it, jlo=jlo, W=W:
                       e.tensor_scalar(
                           PB[:, it % 2, u, jlo:jlo + W],
                           PB[:, it % 2, u, jlo:jlo + W],
                           small[:, sc + 2:sc + 3], None, op0=ALU.mult),
                       reads=(f"PB{u}.{it % 2}", f"rz{u}.{it % 2}"),
                       writes=(f"PB{u}.{it % 2}",),
                       tag=f"f{l}.nm{hp}.{w}.{it}.{u}")
                if l == 0 and hp == 0 and w == 1 and "dbg_pb" in self.dbg_h:
                    rid = f"dbg_pb.{it * 2 + u}"
                    p.dma("gpsimd", self.dbg_h["dbg_pb"][it * 2 + u],
                          PB[:, it % 2, u, :], reads=(f"PB{u}.{it % 2}",),
                          writes=(rid,), tag=rid)
                    self.out_res.append(rid)

        def emit_tp(hp, w, it):
            jlo, W = band(w, it)
            jb0 = jlo // 128
            nb = W // 128
            for u in range(2):
                def ptp(e, u=u, it=it, jb0=jb0, nb=nb):
                    for jj in range(nb):
                        jb = jb0 + jj
                        mm = e.transpose(
                            pPT[:, jj * 128:(jj + 1) * 128],
                            PB[:, it % 2, u, jb * 128:(jb + 1) * 128],
                            self.ident_bf[:])
                    return mm
                p.task("tensor", ptp, reads=(f"PB{u}.{it % 2}", "identity_bf"),
                       writes=("pPT",), tag=f"f{l}.tp{hp}.{w}.{it}.{u}")
                def pev(e, u=u, it=it, jb0=jb0, nb=nb):
                    return e.activation(
                        PT[:, u, jb0:jb0 + nb, it * 128:(it + 1) * 128],
                        pPT[:, 0:nb * 128].rearrange("p (a b) -> p a b", b=128),
                        AF.Identity)
                p.task("scalar", pev, reads=("pPT",),
                       writes=(f"PTc{u}.{it}",),
                       tag=f"f{l}.pe{hp}.{w}.{it}.{u}")

        def icols(w, jt, njt):
            if w == 0:
                return jt * 128, 512 - jt * 128
            lo = max(0, jt - 4) * 128
            hi = min(4, jt + 1) * 128
            return lo, hi - lo

        for hp in range(4):
            for w in range(NW):
                jt0 = 0 if w == 0 else (w - 1) * 4
                njt = 4 if w == 0 else 8
                for it in range(4):
                    iv = w * 512 + it * 128
                    for u in range(2):
                        s_task(hp, w, it, u, jt0, iv)
                    if it > 0:
                        emit_tp(hp, w, it - 1)
                    softmax(hp, w, it)
                emit_tp(hp, w, 3)
                pt_res = tuple(f"PTc{u}.{it}" for u in range(2) for it in range(4))
                jt_order = list(range(njt)) if w == 0 else [3, 4, 2, 5, 1, 6, 0, 7]
                def omm(e, hp=hp, w=w, jt0=jt0, jt_order=jt_order):
                    for u in range(2):
                        hh = hp * 2 + u
                        first = True
                        for jt in jt_order:
                            ilo, n = icols(w, jt, 0)
                            mm = e.matmul(
                                pO[u * 64:(u + 1) * 64, ilo:ilo + n],
                                VTM[:, jt0 + jt, hh * 64:(hh + 1) * 64],
                                PT[:, u, jt, ilo:ilo + n],
                                start=first, stop=(jt == jt_order[-1]),
                                tile_position=(0, u * 64))
                            first = False
                    return mm
                p.task("tensor", omm, reads=pt_res + vtm_res,
                       writes=("pO0",), tag=f"f{l}.O{hp}.{w}")
                p.task("scalar", lambda e, hp=hp, w=w: e.activation(
                    OA[:, hp, w * 512:(w + 1) * 512], pO[:, :],
                    AF.Identity),
                    reads=("pO0",), writes=(f"OA{hp}.{w}",),
                    tag=f"f{l}.Oe{hp}.{w}")

    def final_phase(self):
        nc, p, h = self.nc, self.p, self.h
        XF = self.XF
        with ExitStack() as S:
            SS = S.enter_context(nc.sbuf_tensor("SS", [128, 2, N], F32))
            XB = S.enter_context(nc.sbuf_tensor("XB", [128, 2, D], F32))
            OT = S.enter_context(nc.sbuf_tensor("OT", [128, 2, D], F32))
            pfin = S.enter_context(nc.psum_tensor("pfin", [128, 2, 1024], F32))
            for k in range(DT):
                b = k % 2
                p.task("scalar", lambda e, k=k: e.activation(
                    XF[:, k, :], XF[:, k, :], AF.Silu),
                    reads=(f"XF{k}",), writes=(f"XF{k}",), tag=f"fin.silu{k}")
                if "dbg_g" in self.dbg_h:
                    rid = f"dbg_g.{k}"
                    p.dma("sync", self.dbg_h["dbg_g"][k], XF[:, k, :],
                          reads=(f"XF{k}",), writes=(rid,), tag=rid)
                    self.out_res.append(rid)
                p.dma("sync", SS[:, b, :], self.s_scr[k],
                      reads=(f"s_scr{k}",), writes=(f"SS{b}",), tag=f"fin.s{k}")
                p.task("vector", lambda e, k=k, b=b: e.tensor_tensor(
                    XF[:, k, :], XF[:, k, :], SS[:, b, :], op=ALU.mult),
                    reads=(f"XF{k}", f"SS{b}"), writes=(f"XF{k}",),
                    tag=f"fin.sg{k}")
            for t in range(NT):
                b = t % 2
                p.dma("sync", XB[:, b, :], h["x"][t * 128:(t + 1) * 128, :],
                      reads=(f"XB{b}c",), writes=(f"XB{b}",), tag=f"fin.x{t}")
                def tpf(e, t=t, b=b):
                    for k in range(DT):
                        mm = e.transpose(pfin[:, b, k * 128:(k + 1) * 128],
                                         XF[:, k, t * 128:(t + 1) * 128],
                                         self.ident_f32[:])
                    return mm
                p.task("tensor", tpf,
                       reads=tuple(f"XF{k}" for k in range(DT)) + ("identity_f32",),
                       writes=(f"pfin{b}",), tag=f"fin.tp{t}")
                p.task("vector", lambda e, t=t, b=b: e.tensor_tensor(
                    OT[:, b, :], XB[:, b, :], pfin[:, b, :], op=ALU.add),
                    reads=(f"pfin{b}", f"XB{b}"), writes=(f"OT{b}", f"XB{b}c"),
                    tag=f"fin.add{t}")
                rid = f"out{t}"
                p.dma("gpsimd", self.out_h[t * 128:(t + 1) * 128, :], OT[:, b, :],
                      reads=(f"OT{b}",), writes=(rid,), tag=rid)
                self.out_res.append(rid)
            p.flush("final")




def build_nc(debug=(), stop_after=None):
    b = Build(debug, stop_after)
    nc, dbg = b.build()
    return nc, dbg


def kernel(**inputs):
    g = prep_inputs(inputs)
    c = host_constants()
    nc, _ = build_nc()
    base = {}
    for n in INPUT_NAMES:
        base[n] = g[n]
    for n in CONST_NAMES:
        base[n] = c[n]
    in_maps = []
    for b in range(B):
        m = dict(base)
        m["x"] = np.ascontiguousarray(g["x"][b])
        in_maps.append(m)
    res = run_bass_kernel_spmd(nc, in_maps, core_ids=list(range(B)))
    return np.stack([res.results[b]["out"] for b in range(B)], axis=0)


# version 19
# speedup vs baseline: 1.0569x; 1.0128x over previous
import numpy as np
import ml_dtypes
from contextlib import ExitStack

import concourse.bass as bass
import concourse.mybir as mybir
from concourse.bass_utils import run_bass_kernel_spmd

F32 = mybir.dt.float32
BF16 = mybir.dt.bfloat16
AF = mybir.ActivationFunctionType
ALU = mybir.AluOpType
AX = mybir.AxisListType
BF = ml_dtypes.bfloat16

B, N, D = 8, 2048, 1024
NT, DT = N // 128, D // 128
HIDDEN, QKD = 2048, 2048
HEADS, DH, INNER = 8, 64, 512
WIN, NW = 512, 4
HID, HIDP = 2730, 2816
KH = HIDP // 128
EPS = 1e-5
NEG = -1e30

ENGINES = ("tensor", "vector", "scalar", "gpsimd", "sync")
TRUNC_M = 12
CHAIN_ON = True
VPART = 3



class Prog:

    R = 8

    def __init__(self, nc, stack):
        self.nc = nc
        self.sems = {e: stack.enter_context(nc.semaphore("sem_" + e)) for e in ENGINES}
        self.dsems = {e: [stack.enter_context(nc.semaphore(f"dsem_{e}{r}"))
                          for r in range(self.R)] for e in ENGINES}
        self.cnt = {e: 0 for e in ENGINES}
        self.dcnt = {e: 0 for e in ENGINES}
        self.drot = {e: [0] * self.R for e in ENGINES}
        self.cov = {e: {} for e in ENGINES}
        self.pending = {e: [] for e in ENGINES}
        self.res = {}
        self.n_tasks = 0

    def task(self, eng, emit, reads=(), writes=(), tag="", n_dma=0):
        deps = {}

        def add(tok):
            if tok is None:
                return
            k, v = tok[:-1], tok[-1]
            if deps.get(k, 0) < v:
                deps[k] = v

        for r in reads:
            e = self.res.get(r)
            if e:
                add(e["w"])
        for r in writes:
            e = self.res.get(r)
            if e:
                add(e["w"])
                for t in e["r"]:
                    add(t)
        if eng == "tensor":
            deps = {k: v for k, v in deps.items() if k != ("c", "tensor")}
        if n_dma:
            assert n_dma == 1
            r = self.dcnt[eng] % self.R
            self.dcnt[eng] += 1
            prev = self.drot[eng][r]
            self.drot[eng][r] = prev + 16
            if prev and deps.get(("d", eng, r), 0) < prev:
                deps[("d", eng, r)] = prev
            comp = ("d", eng, r, prev + 16)
            self.pending[eng].append((deps, emit, r, tag))
        else:
            self.cnt[eng] += 1
            comp = ("c", eng, self.cnt[eng])
            self.pending[eng].append((deps, emit, None, tag))
        self.n_tasks += 1
        for r in reads:
            self.res.setdefault(r, {"w": None, "r": []})["r"].append(comp)
        for r in writes:
            self.res[r] = {"w": comp, "r": []}

    def dma(self, eng, out, in_, reads=(), writes=(), tag=""):
        def emit(e, out=out, in_=in_):
            return [e.dma_start(out=out, in_=in_)]
        self.task(eng, emit, reads, writes, tag, n_dma=1)

    def wait_only(self, eng, deps, tag="wait"):
        self.pending[eng].append((dict(deps), None, 0, tag))

    def flush(self, name):
        for e in ENGINES:
            if self.dcnt[e]:
                self.wait_only(e, {("d", e, r): self.drot[e][r]
                                   for r in range(self.R) if self.drot[e][r]},
                               "fence")
        with self.nc.named_scope(name), self.nc.Block() as block:
            for e in ENGINES:
                tasks = self.pending[e]
                self.pending[e] = []
                if not tasks:
                    continue

                def gen(e=e, tasks=tasks):
                    def _(eng):
                        cov = self.cov[e]
                        for (deps, emit, dslot, tag) in tasks:
                            for key, val in sorted(deps.items()):
                                if cov.get(key, 0) >= val:
                                    continue
                                if key[0] == "d":
                                    sem = self.dsems[key[1]][key[2]]
                                else:
                                    sem = self.sems[key[1]]
                                eng.wait_ge(sem, val)
                                cov[key] = val
                            if emit is None:
                                continue
                            if dslot is not None:
                                insts = emit(eng)
                                assert len(insts) == 1, tag
                                insts[0].then_inc(self.dsems[e][dslot], 16)
                            else:
                                inst = emit(eng)
                                if inst is None:
                                    inst = eng.nop()
                                inst.then_inc(self.sems[e], 1)
                    return _
                getattr(block, e)(gen())
        for e in ENGINES:
            for e2 in ENGINES:
                self.cov[e][("c", e2)] = self.cnt[e2]
                for r in range(self.R):
                    self.cov[e][("d", e2, r)] = self.drot[e2][r]


def host_constants():
    c = {}
    c["identity_bf"] = np.eye(128, dtype=BF)
    c["identity_f32"] = np.eye(128, dtype=np.float32)
    c["ones1_bf"] = np.ones((1, 128), dtype=BF)
    c["ones_col_bf"] = np.ones((128, 1), dtype=BF)
    c["ones512_bf"] = np.ones((1, 512), dtype=BF)
    e2c = np.zeros((128, 2), dtype=np.float32)
    e2c[0:64, 0] = 1.0
    e2c[64:128, 1] = 1.0
    c["e2c_bf"] = e2c.astype(BF)
    e2t = np.zeros((2, 128), dtype=np.float32)
    e2t[0, 0:64] = 1.0
    e2t[1, 64:128] = 1.0
    c["e2t"] = e2t
    R = np.zeros((128, 128), dtype=np.float32)
    for base in (0, 64):
        for i in range(32):
            R[base + i, base + 32 + i] = -1.0
            R[base + 32 + i, base + i] = 1.0
    c["rotT_bf"] = R.T.copy().astype(BF)
    half = np.arange(0, DH, 2, dtype=np.float64)
    inv_freq = 1.0 / (10000.0 ** (half / DH))
    tpos = np.arange(N, dtype=np.float64)
    freqs = np.concatenate([tpos[:, None] * inv_freq[None, :]] * 2, axis=-1)
    base = (half + 0.4 * DH) / (1.4 * DH)
    power = (tpos - N // 2) / float(WIN // 2)
    scale = np.concatenate([base[None, :] ** power[:, None]] * 2, axis=-1)
    cos, sin = np.cos(freqs), np.sin(freqs)
    QKS = 8.0
    tab = np.stack([(cos * scale * QKS).T, (sin * scale * QKS).T,
                    (cos / scale).T, (sin / scale).T])
    c["rotary"] = np.tile(tab, (1, 2, 1)).astype(BF)
    m1 = np.zeros((4, 128, 1024), dtype=BF)
    j = np.arange(1024)
    for it in range(4):
        for pp in range(128):
            il = it * 128 + pp
            m1[it, pp] = np.where((j >= il) & (j <= il + WIN), 0.0, NEG).astype(BF)
    c["mask1"] = m1
    return c


def prep_inputs(inputs):
    g = {}
    f32 = np.float32
    asf = lambda a: np.ascontiguousarray(np.asarray(a, dtype=f32))
    g["x"] = asf(inputs["L0_attn_x"])
    g["W_hidden"] = asf(inputs["L0_attn_W_hidden"])
    g["W_qk"] = asf(inputs["L0_attn_W_qk"])
    g["W_out"] = asf(inputs["L0_attn_W_out"])
    g["bh_t"] = asf(inputs["L0_attn_b_hidden"]).reshape(32, 128).T.copy()
    g["bqk_row"] = asf(inputs["L0_attn_b_qk"]).reshape(1, QKD)
    g["bout_t"] = asf(inputs["L0_attn_b_out"]).reshape(8, 128).T.copy()
    g["lng_t"] = asf(inputs["L0_attn_ln_g"]).reshape(8, 128).T.copy()
    g["lnb_t"] = asf(inputs["L0_attn_ln_b"]).reshape(8, 128).T.copy()
    osg = asf(inputs["L0_attn_os_gamma"])
    osb = asf(inputs["L0_attn_os_beta"])
    g["gb_row"] = np.ascontiguousarray(
        np.concatenate([osg[0], osb[0], osg[1], osb[1]]).reshape(1, 4 * QKD))
    g["wqkv"] = asf(inputs["L0_ffn_wqkv"])
    g["wo"] = asf(inputs["L0_ffn_wo"])
    w1 = asf(inputs["L0_ffn_w1"])
    w2 = asf(inputs["L0_ffn_w2"])
    b1 = asf(inputs["L0_ffn_b1"])
    b2 = asf(inputs["L0_ffn_b2"])
    w1a = np.zeros((2, D, HIDP), f32)
    w1g = np.zeros((2, D, HIDP), f32)
    w1a[:, :, :HID] = w1[:, :, :HID]
    w1g[:, :, :HID] = w1[:, :, HID:]
    w1ag = np.zeros((2, KH, D, 256), f32)
    for m in range(KH):
        w1ag[:, m, :, 0:128] = w1a[:, :, m * 128:(m + 1) * 128]
        w1ag[:, m, :, 128:256] = w1g[:, :, m * 128:(m + 1) * 128]
    g["w1ag"] = np.ascontiguousarray(w1ag)
    w2p = np.zeros((2, HIDP, D), f32)
    w2p[:, :HID, :] = w2
    w2cb = np.zeros((2, 8, HIDP, 128), f32)
    for mo in range(8):
        w2cb[:, mo] = w2p[:, :, mo * 128:(mo + 1) * 128]
    g["w2cb"] = np.ascontiguousarray(w2cb)
    g["b2_row"] = np.ascontiguousarray(b2.reshape(2, 1, D))
    b1a = np.zeros((2, HIDP), f32)
    b1g = np.zeros((2, HIDP), f32)
    b1a[:, :HID] = b1[:, :HID]
    b1g[:, :HID] = b1[:, HID:]
    g["b1a_row"] = np.ascontiguousarray(b1a.reshape(2, 1, HIDP))
    g["b1g_t"] = np.ascontiguousarray(b1g.reshape(2, KH, 128).transpose(0, 2, 1))
    g["flg_t"] = np.ascontiguousarray(
        asf(inputs["L0_ffn_ln_g"]).reshape(2, 8, 128).transpose(0, 2, 1))
    g["flb_t"] = np.ascontiguousarray(
        asf(inputs["L0_ffn_ln_b"]).reshape(2, 8, 128).transpose(0, 2, 1))
    qs = asf(inputs["L0_ffn_q_scale"])
    ks = asf(inputs["L0_ffn_k_scale"])
    g["qs_tile"] = np.ascontiguousarray(
        np.broadcast_to(np.tile(qs, (1, 2))[:, None, :], (2, 2, 128)).copy())
    g["ks_tile"] = np.ascontiguousarray(
        np.broadcast_to(np.tile(ks, (1, 2))[:, None, :], (2, 2, 128)).copy())
    return g


INPUT_NAMES = [
    "W_hidden", "W_qk", "W_out", "bh_t", "bqk_row", "bout_t", "lng_t", "lnb_t",
    "gb_row", "wqkv", "wo", "w1ag", "w2cb", "b2_row", "b1a_row", "b1g_t",
    "flg_t", "flb_t", "qs_tile", "ks_tile"]
CONST_NAMES = ["e2t", "e2c_bf", "rotT_bf", "identity_f32", "identity_bf",
               "ones1_bf", "ones_col_bf", "ones512_bf", "rotary", "mask1"]


class Build:
    def __init__(self, debug=(), stop_after=None):
        self.nc = bass.Bass("TRN2", target_bir_lowering=False)
        self.dbg = set(debug)
        self.stop_after = stop_after
        self.h = {}
        self.dbg_h = {}
        self.out_res = []

    def inp(self, name, shape, dt=F32):
        self.h[name] = self.nc.dram_tensor(
            name, list(shape), dt, kind="ExternalInput").ap()

    def declare_io(self):
        nc = self.nc
        self.inp("x", (N, D))
        self.inp("W_hidden", (D, 2 * HIDDEN))
        self.inp("W_qk", (D, QKD))
        self.inp("W_out", (HIDDEN, D))
        self.inp("bh_t", (128, 32))
        self.inp("bqk_row", (1, QKD))
        self.inp("bout_t", (128, 8))
        self.inp("lng_t", (128, 8))
        self.inp("lnb_t", (128, 8))
        self.inp("gb_row", (1, 4 * QKD))
        self.inp("wqkv", (2, D, 3 * INNER))
        self.inp("wo", (2, INNER, D))
        self.inp("w1ag", (2, KH, D, 256))
        self.inp("w2cb", (2, 8, HIDP, 128))
        self.inp("b2_row", (2, 1, D))
        self.inp("b1a_row", (2, 1, HIDP))
        self.inp("b1g_t", (2, 128, KH))
        self.inp("flg_t", (2, 128, 8))
        self.inp("flb_t", (2, 128, 8))
        self.inp("qs_tile", (2, 2, 128))
        self.inp("ks_tile", (2, 2, 128))
        self.inp("e2t", (2, 128))
        self.inp("e2c_bf", (128, 2), BF16)
        self.inp("rotT_bf", (128, 128), BF16)
        self.inp("identity_f32", (128, 128))
        self.inp("identity_bf", (128, 128), BF16)
        self.inp("ones1_bf", (1, 128), BF16)
        self.inp("ones_col_bf", (128, 1), BF16)
        self.inp("ones512_bf", (1, 512), BF16)
        self.inp("rotary", (4, 128, N), BF16)
        self.inp("mask1", (4, 128, 1024), BF16)
        self.out_h = nc.dram_tensor("out", [N, D], F32, kind="ExternalOutput").ap()
        self.xn_scr = nc.dram_tensor("xn_scr", [DT, 128, N], F32).ap()
        self.s_scr = nc.dram_tensor("s_scr", [DT, 128, N], F32).ap()
        for name, shape in [
                ("dbg_xn", (DT, 128, N)), ("dbg_normed", (DT, 128, N)),
                ("dbg_qk", (NT, 128, QKD)), ("dbg_rowsum", (128, NT)),
                ("dbg_vg", (NT, 128, N)), ("dbg_s", (DT, 128, N)),
                ("dbg_qp", (4, 128, N)), ("dbg_kp", (4, 128, N)),
                ("dbg_vt", (NT, 128, INNER)),
                ("dbg_oall", (4, 128, N)), ("dbg_xf", (DT, 128, N)),
                ("dbg_n2", (DT, 128, N)), ("dbg_h1", (KH, 128, N)),
                ("dbg_pb", (8, 128, 1024)), ("dbg_s2", (8, 128, 1024)),
                ("dbg_xf2", (DT, 128, N))]:
            if name in self.dbg:
                self.dbg_h[name] = self.nc.dram_tensor(
                    name, list(shape), F32, kind="ExternalOutput").ap()

    def dump_bf(self, name, src_fn, count, reads):
        hd = self.dbg_h.get(name)
        if hd is None:
            return
        for i in range(count):
            rid = f"{name}.{i}"
            self.p.dma("gpsimd", hd[i], src_fn(i), reads=reads, writes=(rid,),
                       tag=f"dbg.{name}{i}")
            self.out_res.append(rid)

    def dump_f32(self, name, src_fn, count, reads):
        hd = self.dbg_h.get(name)
        if hd is None:
            return
        for i in range(count):
            rid = f"{name}.{i}"
            self.p.dma("sync", hd[i], src_fn(i), reads=reads, writes=(rid,),
                       tag=f"dbg.{name}{i}")
            self.out_res.append(rid)

    def build(self):
        self.declare_io()
        nc = self.nc
        with ExitStack() as TOP:
            E = TOP.enter_context
            self.p = Prog(nc, TOP)
            p = self.p

            self.ABF = E(nc.sbuf_tensor("ABF", [128, DT, N], BF16))
            self.ident_bf = E(nc.sbuf_tensor("ident_bf", [128, 128], BF16))
            self.ident_f32 = E(nc.sbuf_tensor("ident_f32", [128, 128], F32))
            self.ones1 = E(nc.sbuf_tensor("ones1", [1, 128], BF16))
            self.rowsum = E(nc.sbuf_tensor("rowsum", [128, NT], F32))
            self.small = E(nc.sbuf_tensor("small", [128, 48], F32))
            self.bh_sb = E(nc.sbuf_tensor("bh_sb", [128, 32], F32))
            self.bout_sb = E(nc.sbuf_tensor("bout_sb", [128, 8], F32))
            self.lng_sb = E(nc.sbuf_tensor("lng_sb", [128, 8], F32))
            self.lnb_sb = E(nc.sbuf_tensor("lnb_sb", [128, 8], F32))
            self.epsv = E(nc.sbuf_tensor("epsv", [128, 1], F32))
            self.ones_col = E(nc.sbuf_tensor("ones_col", [128, 1], BF16))
            self.ones512 = E(nc.sbuf_tensor("ones512", [1, 512], BF16))
            self.e2t_sb = E(nc.sbuf_tensor("e2t_sb", [2, 128], F32))

            for nm, sb in [("identity_bf", self.ident_bf),
                           ("identity_f32", self.ident_f32),
                           ("ones1_bf", self.ones1), ("bh_t", self.bh_sb),
                           ("ones_col_bf", self.ones_col),
                           ("ones512_bf", self.ones512), ("e2t", self.e2t_sb),
                           ("bout_t", self.bout_sb), ("lng_t", self.lng_sb),
                           ("lnb_t", self.lnb_sb)]:
                p.dma("sync", sb[:], self.h[nm][:], writes=(nm,), tag="const")
            p.task("vector", lambda e: e.memset(self.epsv[:], EPS),
                   writes=("epsv",), tag="c.eps")

            with ExitStack() as SQG:
                self.QK = SQG.enter_context(
                    nc.sbuf_tensor("QK", [128, NT, QKD], BF16))
                self.GBC = SQG.enter_context(
                    nc.sbuf_tensor("GBC", [128, 4, QKD], BF16))
                with ExitStack() as SWQK:
                    self.WQK = SWQK.enter_context(
                        nc.sbuf_tensor("WQK", [128, DT, QKD], BF16))
                    self.phase0()
                    if self.stop_after != "p0":
                        self.attn_p1()
                if self.stop_after not in ("p0", "p1"):
                    self.attn_p2()
            if self.stop_after not in ("p0", "p1", "p2"):
                self.attn_p34()
                if self.stop_after not in ("p3", "p4"):
                    self.ffn_slot()

            p.wait_only("sync", self._res_deps(self.out_res), "drain")
            p.flush("drain")
        return nc, self.dbg_h

    def _res_deps(self, names):
        deps = {}
        for r in names:
            e = self.p.res.get(r)
            if e and e["w"] is not None:
                tok = e["w"]
                k, v = tok[:-1], tok[-1]
                if deps.get(k, 0) < v:
                    deps[k] = v
        return deps

    def phase0(self):
        nc, p = self.nc, self.p
        h = self.h
        with ExitStack() as S:
            XT = S.enter_context(nc.sbuf_tensor("XT", [128, 4, D], F32))
            XNT = S.enter_context(nc.sbuf_tensor("XNT", [128, 4, D], F32))
            NTB = S.enter_context(nc.sbuf_tensor("NTB", [128, 4, D], BF16))
            XNW = S.enter_context(nc.sbuf_tensor("XNW", [128, 2, DT, 128], F32))
            ST6 = S.enter_context(nc.sbuf_tensor("ST6", [128, 4, 12], F32))
            MV = S.enter_context(nc.sbuf_tensor("MV", [128, 4, 2], F32))
            pbf = S.enter_context(nc.psum_tensor("pbf", [128, 2, 1024], BF16))
            pxf = S.enter_context(nc.psum_tensor("pxf", [128, 2, 1024], F32))
            small = self.small
            ABF = self.ABF

            def stats(src_ap, src_res, rstd_col, tagp, b):
                p.task("vector", lambda e, v=src_ap, b=b: (
                    e.bn_stats(ST6[:, b, 0:6], v[:, 0:512]),
                    e.bn_stats(ST6[:, b, 6:12], v[:, 512:1024]))[-1],
                    reads=(src_res,), writes=(f"ST6{b}",), tag=tagp + ".bn")
                p.task("vector", lambda e, b=b: e.bn_aggr(MV[:, b, :], ST6[:, b, :]),
                       reads=(f"ST6{b}",), writes=(f"MV{b}",), tag=tagp + ".ag")
                p.task("scalar", lambda e, b=b: e.activation(
                    small[:, 8 + b:9 + b], MV[:, b, 1:2], AF.Ln,
                    bias=self.epsv[:], scale=1.0),
                    reads=(f"MV{b}", "epsv"), writes=(f"lnv{b}",), tag=tagp + ".ln")
                p.task("scalar", lambda e, rc=rstd_col, b=b: e.activation(
                    small[:, rc:rc + 1], small[:, 8 + b:9 + b], AF.Exp, scale=-0.5),
                    reads=(f"lnv{b}",), writes=(f"rstd{rstd_col}",), tag=tagp + ".ex")

            def stageA(t):
                b = t % 4
                p.dma("sync", XT[:, b, :], h["x"][t * 128:(t + 1) * 128, :],
                      writes=(f"XT{b}",), tag=f"p0.load{t}")
                stats(XT[:, b, :], f"XT{b}", 16 + b, f"p0.s1_{t}", b)

            def stageB(t):
                b = t % 4
                p.task("vector", lambda e, b=b: e.tensor_scalar(
                    XNT[:, b, :], XT[:, b, :], MV[:, b, 0:1], small[:, 16 + b:17 + b],
                    op0=ALU.subtract, op1=ALU.mult),
                    reads=(f"XT{b}", f"MV{b}", f"rstd{16 + b}"),
                    writes=(f"XNT{b}",), tag=f"p0.xn{t}")
                stats(XNT[:, b, :], f"XNT{b}", 20 + b, f"p0.s2_{t}", b)

            def stageC(t):
                b = t % 4
                bp = t % 2
                p.task("vector", lambda e, b=b: e.tensor_scalar(
                    NTB[:, b, :], XNT[:, b, :], MV[:, b, 0:1], small[:, 20 + b:21 + b],
                    op0=ALU.subtract, op1=ALU.mult),
                    reads=(f"XNT{b}", f"MV{b}", f"rstd{20 + b}"),
                    writes=(f"NTB{b}",), tag=f"p0.nt{t}")

                def tp_n(e, b=b, bp=bp):
                    for k in range(DT):
                        mm = e.transpose(pbf[:, bp, k * 128:(k + 1) * 128],
                                         NTB[:, b, k * 128:(k + 1) * 128],
                                         self.ident_bf[:])
                    return mm
                p.task("tensor", tp_n, reads=(f"NTB{b}", "identity_bf"),
                       writes=(f"pbf{bp}",), tag=f"p0.tpn{t}")

                def ev_n(e, t=t, bp=bp):
                    for k in range(DT):
                        mm = e.activation(
                            ABF[:, k, t * 128:(t + 1) * 128],
                            pbf[:, bp, k * 128:(k + 1) * 128],
                            AF.Identity, bias=self.lnb_sb[:, k:k + 1],
                            scale=self.lng_sb[:, k:k + 1])
                    return mm
                p.task("scalar", ev_n, reads=(f"pbf{bp}", "lng_t", "lnb_t"),
                       writes=(f"ABF.{t}",), tag=f"p0.evn{t}")

                def tp_x(e, b=b, bp=bp):
                    for k in range(DT):
                        mm = e.transpose(pxf[:, bp, k * 128:(k + 1) * 128],
                                         XNT[:, b, k * 128:(k + 1) * 128],
                                         self.ident_f32[:])
                    return mm
                p.task("tensor", tp_x, reads=(f"XNT{b}", "identity_f32"),
                       writes=(f"pxf{bp}",), tag=f"p0.tpx{t}")

                def ev_x(e, t=t, b=b, bp=bp):
                    return e.tensor_copy(
                        XNW[:, bp, :, :],
                        pxf[:, bp, :].rearrange("p (a b) -> p a b", b=128))
                p.task("vector", ev_x, reads=(f"pxf{bp}",),
                       writes=(f"XNW{bp}",), tag=f"p0.evx{t}")
                p.dma("gpsimd",
                      self.xn_scr.rearrange("k p n -> p k n")[:, :, t * 128:(t + 1) * 128],
                      XNW[:, bp, :, :],
                      reads=(f"XNW{bp}",), writes=(f"xn_scr.{t}",),
                      tag=f"p0.spill{t}")

            stageA(0)
            stageA(1)
            stageB(0)
            for t in range(NT):
                if t < DT:
                    p.dma("gpsimd", self.WQK[:, t, :],
                          h["W_qk"][t * 128:(t + 1) * 128, :],
                          writes=(f"WQK{t}",), tag=f"p0.wqk{t}")
                if t + 2 < NT:
                    stageA(t + 2)
                if t + 1 < NT:
                    stageB(t + 1)
                stageC(t)

            allxn = tuple(f"xn_scr.{t}" for t in range(NT))
            for k in range(DT):
                p.wait_only("sync", {}, "noop")
            if "dbg_xn" in self.dbg_h:
                for k in range(DT):
                    rid = f"dbg_xn.{k}"
                    p.dma("gpsimd", self.dbg_h["dbg_xn"][k], self.xn_scr[k],
                          reads=allxn, writes=(rid,), tag=rid)
                    self.out_res.append(rid)
            self.dump_bf("dbg_normed", lambda i: ABF[:, i, :], DT,
                         tuple(f"ABF.{t}" for t in range(NT)))
            p.flush("p0")

    def attn_p1(self):
        nc, p, h = self.nc, self.p, self.h
        ABF, QK, GBC, WQK = self.ABF, self.QK, self.GBC, self.WQK
        abf_res = tuple(f"ABF.{t}" for t in range(NT))
        with ExitStack() as S1a:
            GBR = S1a.enter_context(nc.sbuf_tensor("GBR", [1, 4 * QKD], BF16))
            BQR = S1a.enter_context(nc.sbuf_tensor("BQR", [1, QKD], BF16))
            pqk = S1a.enter_context(nc.psum_tensor("pqk", [128, 2, 512], F32))
            pgb = S1a.enter_context(nc.psum_tensor("pgb", [128, 2, 512], F32))

            p.dma("gpsimd", GBR[:], h["gb_row"][:], writes=("GBR",), tag="p1.gbr")
            p.dma("gpsimd", BQR[:], h["bqk_row"][:], writes=("BQR",), tag="p1.bqr")
            for r in range(4):
                for c in range(4):
                    pb = (r * 4 + c) % 2
                    p.task("tensor", lambda e, r=r, c=c, pb=pb: e.matmul(
                        pgb[:, pb, :], self.ones1[:],
                        GBR[:, r * QKD + c * 512:r * QKD + (c + 1) * 512],
                        start=True, stop=True),
                        reads=("GBR", "ones1_bf"), writes=(f"pgb{pb}",),
                        tag=f"p1.gbmm{r}.{c}")
                    p.task("vector", lambda e, r=r, c=c, pb=pb: e.tensor_copy(
                        GBC[:, r, c * 512:(c + 1) * 512], pgb[:, pb, :]),
                        reads=(f"pgb{pb}",), writes=(f"GBC{r}.{c}",),
                        tag=f"p1.gbev{r}.{c}")

            wqk_res = tuple(f"WQK{k}" for k in range(DT))
            for t in range(NT):
                for c in range(4):
                    pb = (t * 4 + c) % 2
                    def qkmm(e, t=t, c=c, pb=pb):
                        for k in range(DT):
                            e.matmul(pqk[:, pb, :],
                                     ABF[:, k, t * 128:(t + 1) * 128],
                                     WQK[:, k, c * 512:(c + 1) * 512],
                                     start=(k == 0), stop=False)
                        return e.matmul(pqk[:, pb, :], self.ones1[:],
                                        BQR[:, c * 512:(c + 1) * 512],
                                        start=False, stop=True)
                    p.task("tensor", qkmm,
                           reads=abf_res + wqk_res + ("BQR", "ones1_bf"),
                           writes=(f"pqk{pb}",), tag=f"p1.mm{t}.{c}")
                    p.task("scalar", lambda e, t=t, c=c, pb=pb: e.activation(
                        QK[:, t, c * 512:(c + 1) * 512], pqk[:, pb, :],
                        AF.Silu),
                        reads=(f"pqk{pb}",), writes=(f"QK{t}.{c}",),
                        tag=f"p1.silu{t}.{c}")
            qk_res = tuple(f"QK{t}.{c}" for t in range(NT) for c in range(4))
            self.dump_bf("dbg_qk", lambda i: QK[:, i, :], NT, qk_res)
            p.flush("p1")

    def attn_p2(self):
        nc, p, h = self.nc, self.p, self.h
        QK, GBC = self.QK, self.GBC
        gbc_res = tuple(f"GBC{r}.{c}" for r in range(4) for c in range(4))
        small = self.small
        with ExitStack() as S2:
            KF = S2.enter_context(nc.sbuf_tensor("KF", [128, NT, QKD], BF16))
            QT2 = S2.enter_context(nc.sbuf_tensor("QT2", [128, 2, NT, 256], BF16))
            EB = S2.enter_context(nc.sbuf_tensor("EB", [128, QKD], BF16))
            psim = S2.enter_context(nc.psum_tensor("psim", [128, 2, QKD], F32))

            for t in range(NT):
                p.task("vector", lambda e, t=t: e.tensor_tensor(
                    KF[:, t, :], QK[:, t, :], GBC[:, 2, :], op=ALU.mult),
                    reads=(f"QK{t}.0", f"QK{t}.1", f"QK{t}.2", f"QK{t}.3")
                    + gbc_res,
                    writes=(f"KF{t}",), tag=f"p2.km{t}")
                p.task("vector", lambda e, t=t: e.tensor_tensor(
                    KF[:, t, :], KF[:, t, :], GBC[:, 3, :], op=ALU.add),
                    reads=(f"KF{t}",), writes=(f"KF{t}",), tag=f"p2.ka{t}")
            kf_res = tuple(f"KF{t}" for t in range(NT))

            def emit_qprep(dp):
                qp = dp % 2
                for kt in range(NT):
                    p.task("vector", lambda e, dp=dp, kt=kt, qp=qp: e.tensor_tensor(
                        QT2[:, qp, kt, :], QK[:, kt, dp * 256:(dp + 1) * 256],
                        GBC[:, 0, dp * 256:(dp + 1) * 256], op=ALU.mult),
                        reads=(f"QK{kt}.{dp // 2}",) + gbc_res,
                        writes=(f"QT2{qp}.{kt}",), tag=f"p2.qm{dp}.{kt}")
                    p.task("vector", lambda e, dp=dp, kt=kt, qp=qp: e.tensor_tensor(
                        QT2[:, qp, kt, :], QT2[:, qp, kt, :],
                        GBC[:, 1, dp * 256:(dp + 1) * 256], op=ALU.add),
                        reads=(f"QT2{qp}.{kt}",), writes=(f"QT2{qp}.{kt}",),
                        tag=f"p2.qa{dp}.{kt}")

            emit_qprep(0)
            for dp in range(8):
                qp = dp % 2
                if dp + 1 < 8:
                    emit_qprep(dp + 1)
                qt_res = tuple(f"QT2{qp}.{kt}" for kt in range(NT))
                for dt in range(dp * 2, dp * 2 + 2):
                    pb = dt % 2
                    dl = (dt % 2) * 128
                    def simmm(e, dt=dt, pb=pb, dl=dl, qp=qp):
                        for c in range(4):
                            for kt in range(NT):
                                mm = e.matmul(
                                    psim[:, pb, c * 512:(c + 1) * 512],
                                    QT2[:, qp, kt, dl:dl + 128],
                                    KF[:, kt, c * 512:(c + 1) * 512],
                                    start=(kt == 0), stop=(kt == NT - 1))
                        return mm
                    p.task("tensor", simmm, reads=qt_res + kf_res,
                           writes=(f"psim{pb}",), tag=f"p2.sim{dt}")
                    sb = pb * 4
                    p.task("vector", lambda e, pb=pb, sb=sb: e.tensor_reduce(
                        small[:, sb:sb + 1], psim[:, pb, :], axis=AX.X, op=ALU.max),
                        reads=(f"psim{pb}",), writes=(f"p2max{pb}",),
                        tag=f"p2.max{dt}")
                    p.task("vector", lambda e, sb=sb: e.tensor_scalar(
                        small[:, sb + 1:sb + 2], small[:, sb:sb + 1], -1.0 / N,
                        None, op0=ALU.mult),
                        reads=(f"p2max{pb}",), writes=(f"p2bias{pb}",),
                        tag=f"p2.b{dt}")
                    p.task("scalar", lambda e, pb=pb, sb=sb: e.activation(
                        EB[:, :], psim[:, pb, :], AF.Exp,
                        bias=small[:, sb + 1:sb + 2], scale=1.0 / N,
                        accum_out=small[:, sb + 2:sb + 3]),
                        reads=(f"psim{pb}", f"p2bias{pb}"),
                        writes=("EB0", f"p2z{pb}"), tag=f"p2.exp{dt}")
                    p.task("vector", lambda e, sb=sb: e.reciprocal(
                        small[:, sb + 3:sb + 4], small[:, sb + 2:sb + 3]),
                        reads=(f"p2z{pb}",), writes=(f"p2rz{pb}",),
                        tag=f"p2.rz{dt}")
                    p.task("vector", lambda e, dt=dt, sb=sb: e.tensor_tensor(
                        self.rowsum[:, dt:dt + 1], small[:, sb + 2:sb + 3],
                        small[:, sb + 3:sb + 4], op=ALU.mult),
                        reads=(f"p2z{pb}", f"p2rz{pb}"),
                        writes=(f"rowsum{dt}",), tag=f"p2.rs{dt}")
            if "dbg_rowsum" in self.dbg_h:
                p.dma("sync", self.dbg_h["dbg_rowsum"][:], self.rowsum[:],
                      reads=tuple(f"rowsum{dt}" for dt in range(NT)),
                      writes=("dbg_rowsum.0",), tag="dbg.rowsum")
                self.out_res.append("dbg_rowsum.0")
            p.flush("p2")

    def attn_p34(self):
        nc, p, h = self.nc, self.p, self.h
        ABF = self.ABF
        abf_res = tuple(f"ABF.{t}" for t in range(NT))
        with ExitStack() as S3:
            VG = S3.enter_context(nc.sbuf_tensor("VG", [128, NT, N], BF16))
            with ExitStack() as S3a:
                WH = S3a.enter_context(
                    nc.sbuf_tensor("WH", [128, DT, 2 * HIDDEN], BF16))
                VT = S3a.enter_context(nc.sbuf_tensor("VT", [128, 2, 2048], BF16))
                GT = S3a.enter_context(nc.sbuf_tensor("GT", [128, 2, 2048], BF16))
                pv = S3a.enter_context(nc.psum_tensor("pv", [128, 2, 512], F32))
                pg = S3a.enter_context(nc.psum_tensor("pg", [128, 2, 512], F32))
                for k in range(DT):
                    p.dma("gpsimd", WH[:, k, :],
                          h["W_hidden"][k * 128:(k + 1) * 128, :],
                          writes=(f"WH{k}",), tag=f"p3.wh{k}")
                wh_res = tuple(f"WH{k}" for k in range(DT))
                for hp in range(NT):
                    pb = hp % 2
                    for c in range(4):
                        sc = c % 2
                        def vmm(e, hp=hp, c=c, sc=sc):
                            for k in range(DT):
                                mm = e.matmul(
                                    pv[:, sc, :],
                                    WH[:, k, hp * 128:(hp + 1) * 128],
                                    ABF[:, k, c * 512:(c + 1) * 512],
                                    start=(k == 0), stop=(k == DT - 1))
                            return mm
                        p.task("tensor", vmm, reads=abf_res + wh_res,
                               writes=(f"pv{sc}",), tag=f"p3.vmm{hp}.{c}")
                        def gmm(e, hp=hp, c=c, sc=sc):
                            for k in range(DT):
                                mm = e.matmul(
                                    pg[:, sc, :],
                                    WH[:, k, HIDDEN + hp * 128:
                                       HIDDEN + (hp + 1) * 128],
                                    ABF[:, k, c * 512:(c + 1) * 512],
                                    start=(k == 0), stop=(k == DT - 1))
                            return mm
                        p.task("tensor", gmm, reads=abf_res + wh_res,
                               writes=(f"pg{sc}",), tag=f"p3.gmm{hp}.{c}")
                        p.task("scalar", lambda e, hp=hp, c=c, sc=sc, pb=pb:
                               e.activation(
                                   VT[:, pb, c * 512:(c + 1) * 512], pv[:, sc, :],
                                   AF.Silu, bias=self.bh_sb[:, hp:hp + 1]),
                               reads=(f"pv{sc}", "bh_t"),
                               writes=(f"VT{pb}.{c}",), tag=f"p3.sv{hp}.{c}")
                        p.task("scalar", lambda e, hp=hp, c=c, sc=sc, pb=pb:
                               e.activation(
                                   GT[:, pb, c * 512:(c + 1) * 512], pg[:, sc, :],
                                   AF.Silu, bias=self.bh_sb[:, 16 + hp:17 + hp]),
                               reads=(f"pg{sc}", "bh_t"),
                               writes=(f"GT{pb}.{c}",), tag=f"p3.sg{hp}.{c}")
                    p.task("vector", lambda e, hp=hp, pb=pb: e.tensor_tensor(
                        VG[:, hp, :], VT[:, pb, :], GT[:, pb, :], op=ALU.mult),
                        reads=tuple(f"VT{pb}.{c}" for c in range(4))
                        + tuple(f"GT{pb}.{c}" for c in range(4)),
                        writes=(f"VGm{hp}",), tag=f"p3.vg{hp}")
                    p.task("vector", lambda e, hp=hp: e.tensor_scalar(
                        VG[:, hp, :], VG[:, hp, :],
                        self.rowsum[:, hp:hp + 1], None, op0=ALU.mult),
                        reads=(f"VGm{hp}", f"rowsum{hp}"),
                        writes=(f"VG{hp}",), tag=f"p3.vgr{hp}")
                vg_res = tuple(f"VG{hp}" for hp in range(NT))
                self.dump_bf("dbg_vg", lambda i: VG[:, i, :], NT, vg_res)
                p.flush("p3")
            if self.stop_after == "p3":
                return

            with ExitStack() as S4:
                WO = S4.enter_context(nc.sbuf_tensor("WO", [128, NT, D], BF16))
                XNB = S4.enter_context(nc.sbuf_tensor("XNB", [128, 2, N], F32))
                SB = S4.enter_context(nc.sbuf_tensor("SB", [128, 2, N], F32))
                ps = S4.enter_context(nc.psum_tensor("ps", [128, 2, 512], F32))
                for k in range(NT):
                    p.dma("gpsimd", WO[:, k, :],
                          h["W_out"][k * 128:(k + 1) * 128, :],
                          writes=(f"WO{k}",), tag=f"p4.wo{k}")
                wo_res = tuple(f"WO{k}" for k in range(NT))
                for mo in range(DT):
                    b = mo % 2
                    p.dma("sync", XNB[:, b, :], self.xn_scr[mo],
                          reads=tuple(f"xn_scr.{t}" for t in range(NT)),
                          writes=(f"XNB{b}",), tag=f"p4.xn{mo}")
                    for c in range(4):
                        pb = c % 2
                        def smm(e, mo=mo, c=c, pb=pb):
                            for k in range(NT):
                                mm = e.matmul(
                                    ps[:, pb, :],
                                    WO[:, k, mo * 128:(mo + 1) * 128],
                                    VG[:, k, c * 512:(c + 1) * 512],
                                    start=(k == 0), stop=(k == NT - 1))
                            return mm
                        p.task("tensor", smm, reads=vg_res + wo_res,
                               writes=(f"ps{pb}",), tag=f"p4.mm{mo}.{c}")
                        p.task("scalar", lambda e, c=c, pb=pb, b=b, mo=mo:
                               e.activation(
                                   SB[:, b, c * 512:(c + 1) * 512], ps[:, pb, :],
                                   AF.Identity, bias=self.bout_sb[:, mo:mo + 1]),
                               reads=(f"ps{pb}", "bout_t"), writes=(f"SB{b}",),
                               tag=f"p4.ev{mo}.{c}")
                    p.task("vector", lambda e, b=b: e.tensor_tensor(
                        SB[:, b, :], SB[:, b, :], XNB[:, b, :], op=ALU.add),
                        reads=(f"XNB{b}", f"SB{b}"),
                        writes=(f"SB{b}",), tag=f"p4.add{mo}")
                    p.dma("gpsimd", self.s_scr[mo], SB[:, b, :],
                          reads=(f"SB{b}",), writes=(f"s_scr{mo}",),
                          tag=f"p4.spill{mo}")
                    if "dbg_s" in self.dbg_h:
                        rid = f"dbg_s.{mo}"
                        p.dma("gpsimd", self.dbg_h["dbg_s"][mo], SB[:, b, :],
                              reads=(f"SB{b}",), writes=(rid,), tag=rid)
                        self.out_res.append(rid)
                p.flush("p4")

    def ffn_slot(self):
        nc, p, h = self.nc, self.p, self.h
        with ExitStack() as SF:
            XF = SF.enter_context(nc.sbuf_tensor("XF", [128, DT, N], F32))
            ROTT = SF.enter_context(nc.sbuf_tensor("ROTT", [128, 128], BF16))
            E2C = SF.enter_context(nc.sbuf_tensor("E2C", [128, 2], BF16))
            self.XF = XF
            self.ROTTsb, self.E2Csb = ROTT, E2C
            for k in range(DT):
                p.dma("sync", XF[:, k, :], self.xn_scr[k],
                      reads=tuple(f"xn_scr.{t}" for t in range(NT)),
                      writes=(f"XF{k}",), tag=f"pf.xf{k}")
            p.dma("sync", ROTT[:], h["rotT_bf"][:], writes=("ROTT",), tag="pf.rt")
            p.dma("sync", E2C[:], h["e2c_bf"][:], writes=("E2C",), tag="pf.e2c")
            for l in range(2):
                if self.ffn_layer(l):
                    return
            if self.stop_after is None or self.stop_after == "final":
                self.final_phase()

    def ffn_layer(self, l):
        nc, p, h = self.nc, self.p, self.h
        ABF, XF = self.ABF, self.XF
        sa = self.stop_after or ""
        xf_res = tuple(f"XF{k}" for k in range(DT))

        for k in range(DT):
            if k < 4:
                p.task("vector", lambda e, k=k: e.tensor_copy(
                    ABF[:, k, :], XF[:, k, :]),
                    reads=(f"XF{k}",), writes=(f"ABF.{l}{k}",), tag=f"f{l}.sh{k}")
            else:
                def shift(e, k=k):
                    e.memset(ABF[:, k, 0:1], 0.0)
                    return e.tensor_copy(ABF[:, k, 1:N], XF[:, k, 0:N - 1])
                p.task("vector", shift, reads=(f"XF{k}",),
                       writes=(f"ABF.{l}{k}",), tag=f"f{l}.sh{k}")
        abf_res = tuple(f"ABF.{l}{k}" for k in range(DT))

        with ExitStack() as SL:
            QP = SL.enter_context(nc.sbuf_tensor(f"QP{l}", [128, 4, N], BF16))
            KP = SL.enter_context(nc.sbuf_tensor(f"KP{l}", [128, 4, N], BF16))
            VTM = SL.enter_context(nc.sbuf_tensor(f"VTM{l}", [128, NT, INNER], BF16))

            with ExitStack() as S:
                ROT = S.enter_context(nc.sbuf_tensor(f"ROT{l}", [128, 2, N], BF16))
                WQKV = S.enter_context(
                    nc.sbuf_tensor(f"WQKV{l}", [128, DT, 512], BF16))
                QW = S.enter_context(nc.sbuf_tensor(f"QW{l}", [128, 2, N], BF16))
                SQ = S.enter_context(nc.sbuf_tensor(f"SQb{l}", [128, 2, 512], BF16))
                T1 = S.enter_context(nc.sbuf_tensor(f"T1{l}", [128, 1, 512], BF16))
                T2 = None
                VF = S.enter_context(nc.sbuf_tensor(f"VF{l}", [128, 2, N], BF16))
                ROWS = S.enter_context(nc.sbuf_tensor(f"ROWS{l}", [128, 2, N], F32))
                RSTDB = S.enter_context(nc.sbuf_tensor(f"RSTDB{l}", [2, 2, N], BF16))
                ESC = S.enter_context(nc.sbuf_tensor(f"ESC{l}", [2, 2, 128], BF16))
                QSL = S.enter_context(nc.sbuf_tensor(f"QSL{l}", [2, 2, 128], BF16))
                pqkv = S.enter_context(nc.psum_tensor(f"pqkv{l}", [128, 2, 512], F32))
                psq = S.enter_context(nc.psum_tensor(f"psq{l}", [2, 2, 512], F32))
                pex = S.enter_context(nc.psum_tensor(f"pex{l}", [128, 1, 512], F32))
                pR = S.enter_context(nc.psum_tensor(f"pR{l}", [128, 1, 512], F32))
                ptv = S.enter_context(nc.psum_tensor(f"ptv{l}", [128, 2, 1024], BF16))
                self.ROTsb = ROT

                for r in range(2):
                    p.dma("sync", ROT[:, r, :], h["rotary"][r],
                          writes=(f"ROT{r}",), tag=f"f{l}.rotq{r}")
                p.dma("gpsimd", QSL[:, 0, :], h["qs_tile"][l], writes=("QSL0",),
                      tag=f"f{l}.qs")
                p.dma("gpsimd", QSL[:, 1, :], h["ks_tile"][l], writes=("QSL1",),
                      tag=f"f{l}.ks")
                for r in range(2):
                    p.task("vector", lambda e, r=r: e.tensor_tensor(
                        ESC[:, r, :], self.e2t_sb[:], QSL[:, r, :], op=ALU.mult),
                        reads=(f"QSL{r}", "e2t"), writes=(f"ESC{r}",),
                        tag=f"f{l}.esc{r}")

                def post_work(m):
                    pb = m % 2
                    if m < 8:
                        self._qk_chain(l, m, pb, QP if m < 4 else KP,
                                       QW, SQ, T1, T2, ROWS, RSTDB, ESC,
                                       psq, pex, pR)
                        return
                    vm = m - 8
                    vf_res = tuple(f"VF{pb}.{c}" for c in range(4))
                    for jg in range(4):
                        sg = jg % 2
                        def vtp(e, vm=vm, jg=jg, sg=sg, pb=pb):
                            for jj in range(4):
                                jc = jg * 4 + jj
                                mm = e.transpose(
                                    ptv[:, sg, jj * 128:(jj + 1) * 128],
                                    VF[:, pb, jc * 128:(jc + 1) * 128],
                                    self.ident_bf[:])
                            return mm
                        p.task("tensor", vtp,
                               reads=vf_res + ("identity_bf",),
                               writes=(f"ptv{sg}",),
                               tag=f"f{l}.vtp{vm}.{jg}")
                        def vev(e, vm=vm, jg=jg, sg=sg):
                            return e.tensor_copy(
                                VTM[:, jg * 4:(jg + 1) * 4,
                                    vm * 128:(vm + 1) * 128],
                                ptv[:, sg, 0:512].rearrange(
                                    "p (jj c) -> p jj c", jj=4))
                        p.task("vector", vev, reads=(f"ptv{sg}",),
                               writes=(f"VTM{vm}.{jg}",),
                               tag=f"f{l}.vevt{vm}.{jg}")

                pending = None
                for stage in range(3):
                    for k in range(DT):
                        p.dma("gpsimd", WQKV[:, k, :],
                              h["wqkv"][l][k * 128:(k + 1) * 128,
                                           stage * 512:(stage + 1) * 512],
                              writes=(f"WQKV{k}",), tag=f"f{l}.wqkv{stage}.{k}")
                    wqkv_res = tuple(f"WQKV{k}" for k in range(DT))
                    for ms in range(4):
                        m = stage * 4 + ms
                        pb = m % 2
                        for c in range(4):
                            sc = c % 2
                            def qkvmm(e, ms=ms, c=c, sc=sc):
                                for k in range(DT):
                                    mm = e.matmul(
                                        pqkv[:, sc, :],
                                        WQKV[:, k, ms * 128:(ms + 1) * 128],
                                        ABF[:, k, c * 512:(c + 1) * 512],
                                        start=(k == 0), stop=(k == DT - 1))
                                return mm
                            p.task("tensor", qkvmm, reads=abf_res + wqkv_res,
                                   writes=(f"pqkv{sc}",), tag=f"f{l}.qkv{m}.{c}")
                            if m < 8:
                                p.task("scalar", lambda e, c=c, sc=sc, pb=pb:
                                       e.activation(
                                           QW[:, pb, c * 512:(c + 1) * 512],
                                           pqkv[:, sc, :], AF.Identity),
                                       reads=(f"pqkv{sc}",),
                                       writes=(f"QW{pb}.{c}",),
                                       tag=f"f{l}.qev{m}.{c}")
                            else:
                                p.task("vector", lambda e, c=c, sc=sc, pb=pb:
                                       e.tensor_copy(
                                           VF[:, pb, c * 512:(c + 1) * 512],
                                           pqkv[:, sc, :]),
                                       reads=(f"pqkv{sc}",), writes=(f"VF{pb}.{c}",),
                                       tag=f"f{l}.vev{m}.{c}")
                        if pending is not None:
                            post_work(pending)
                            if pending == 3:
                                for r in range(2):
                                    p.dma("sync", ROT[:, r, :], h["rotary"][2 + r],
                                          writes=(f"ROT{r}",), tag=f"f{l}.rotk{r}")
                        pending = m
                post_work(pending)
                qp_res = tuple(f"QPf{m}" for m in range(4))
                kp_res = tuple(f"KPf{m}" for m in range(4))
                vtm_res = tuple(f"VTM{vm}.{jg}" for vm in range(4) for jg in range(4))
                if l == 0:
                    self.dump_bf("dbg_qp", lambda i: QP[:, i, :], 4, qp_res)
                    self.dump_bf("dbg_kp", lambda i: KP[:, i, :], 4, kp_res)
                    self.dump_bf("dbg_vt", lambda i: VTM[:, i, :], NT, vtm_res)
                p.flush(f"f{l}.qkv")
            if sa == f"f{l}qkv":
                return True

            oa_res = tuple(f"OA{hp}.{w}" for hp in range(4) for w in range(NW))
            with ExitStack() as S34:
                OA = S34.enter_context(nc.sbuf_tensor(f"OA{l}", [128, 4, N], BF16))
                WOF = S34.enter_context(
                    nc.sbuf_tensor(f"WOF{l}", [128, 4, D], BF16))
                for k in range(4):
                    p.dma("gpsimd", WOF[:, k, :],
                          h["wo"][l][k * 128:(k + 1) * 128, :],
                          writes=(f"WOF{k}",), tag=f"f{l}.wo{k}")
                with ExitStack() as S3:
                    PB = S3.enter_context(
                        nc.sbuf_tensor(f"PB{l}", [128, 2, 2, 1024], BF16))
                    PT = S3.enter_context(
                        nc.sbuf_tensor(f"PT{l}", [128, 2, 8, 512], BF16))
                    MSK = S3.enter_context(
                        nc.sbuf_tensor(f"MSK{l}", [128, 4, 1024], BF16))
                    self.MSKsb = MSK
                    for r in range(4):
                        p.dma("sync", MSK[:, r, :], h["mask1"][r],
                              writes=(f"MSK{r}",), tag=f"f{l}.msk{r}")
                    pS = S3.enter_context(
                        nc.psum_tensor(f"pS{l}", [128, 2, 1024], F32))
                    pPT = S3.enter_context(
                        nc.psum_tensor(f"pPT{l}", [128, 2, 1024], BF16))
                    pO = S3.enter_context(
                        nc.psum_tensor(f"pO{l}", [128, 2, 512], F32))
                    if "dbg_s2" in self.dbg_h:
                        self.SDUMP = S3.enter_context(
                            nc.sbuf_tensor(f"SDUMP{l}", [128, 1024], F32))
                    self._local_attn(l, QP, KP, VTM, OA, PB, PT, pS, pPT, pO)
                    if l == 0:
                        self.dump_bf("dbg_oall", lambda i: OA[:, i, :], 4, oa_res)
                    p.flush(f"f{l}.la")
                if sa == f"f{l}la":
                    return True

                with ExitStack() as S3b:
                    pwo = S3b.enter_context(
                        nc.psum_tensor(f"pwo{l}", [128, 2, 512], F32))
                    wof_res = tuple(f"WOF{k}" for k in range(4))
                    for mo in range(DT):
                        for c in range(4):
                            pb = (mo * 4 + c) % 2
                            def womm(e, mo=mo, c=c, pb=pb):
                                for k in range(4):
                                    mm = e.matmul(
                                        pwo[:, pb, :],
                                        WOF[:, k, mo * 128:(mo + 1) * 128],
                                        OA[:, k, c * 512:(c + 1) * 512],
                                        start=(k == 0), stop=(k == 3))
                                return mm
                            p.task("tensor", womm, reads=oa_res + wof_res,
                                   writes=(f"pwo{pb}",), tag=f"f{l}.womm{mo}.{c}")
                            p.task("vector", lambda e, mo=mo, c=c, pb=pb:
                                   e.tensor_tensor(
                                       XF[:, mo, c * 512:(c + 1) * 512],
                                       XF[:, mo, c * 512:(c + 1) * 512],
                                       pwo[:, pb, :], op=ALU.add),
                                   reads=(f"pwo{pb}", f"XF{mo}"),
                                   writes=(f"XF{mo}",),
                                   tag=f"f{l}.woadd{mo}.{c}")
                    if l == 0:
                        self.dump_f32("dbg_xf", lambda i: XF[:, i, :], DT, xf_res)
                    p.flush(f"f{l}.attn")
            if sa == f"f{l}attn":
                return True

        with ExitStack() as S5:
            N2 = S5.enter_context(nc.sbuf_tensor(f"N2{l}", [128, DT, N], BF16))
            self.N2 = N2
            with ExitStack() as S5a:
                TDIF = S5a.enter_context(
                    nc.sbuf_tensor(f"TDIF{l}", [128, DT, N], BF16))
                MUB = S5a.enter_context(nc.sbuf_tensor(f"MUB{l}", [128, 2, N], BF16))
                SQS = S5a.enter_context(nc.sbuf_tensor(f"SQS{l}", [128, 2, 512], BF16))
                MRB = S5a.enter_context(nc.sbuf_tensor(f"MRB{l}", [1, 2, N], BF16))
                LNF = S5a.enter_context(nc.sbuf_tensor(f"LNF{l}", [1, N], F32))
                T2B = S5a.enter_context(nc.sbuf_tensor(f"T2B{l}", [128, 2, N], BF16))
                FLG = S5a.enter_context(nc.sbuf_tensor(f"FLG{l}", [128, 8], F32))
                FLB = S5a.enter_context(nc.sbuf_tensor(f"FLB{l}", [128, 8], F32))
                pst = S5a.enter_context(nc.psum_tensor(f"pst{l}", [1, 2, 512], F32))
                pex2 = S5a.enter_context(nc.psum_tensor(f"pex2{l}", [128, 2, 512], F32))
                p.dma("sync", FLG[:], h["flg_t"][l], writes=("FLG",), tag=f"f{l}.flg")
                p.dma("sync", FLB[:], h["flb_t"][l], writes=("FLB",), tag=f"f{l}.flb")
                for k in range(DT):
                    if k < 4:
                        p.task("vector", lambda e, k=k: e.tensor_copy(
                            ABF[:, k, :], XF[:, k, :]),
                            reads=(f"XF{k}",), writes=(f"ABF.{l}{k}",),
                            tag=f"f{l}.sh2{k}")
                    else:
                        def shift2(e, k=k):
                            e.memset(ABF[:, k, 0:1], 0.0)
                            return e.tensor_copy(ABF[:, k, 1:N], XF[:, k, 0:N - 1])
                        p.task("vector", shift2, reads=(f"XF{k}",),
                               writes=(f"ABF.{l}{k}",), tag=f"f{l}.sh2{k}")
                abf_res2 = tuple(f"ABF.{l}{k}" for k in range(DT))
                for c in range(4):
                    sc = c % 2
                    def sxmm(e, c=c, sc=sc):
                        for k in range(DT):
                            mm = e.matmul(pst[:, sc, :], self.ones_col[:],
                                          ABF[:, k, c * 512:(c + 1) * 512],
                                          start=(k == 0), stop=(k == DT - 1))
                        return mm
                    p.task("tensor", sxmm, reads=abf_res2 + ("ones_col_bf",),
                           writes=(f"pst{sc}",), tag=f"f{l}.sx{c}")
                    p.task("scalar", lambda e, c=c, sc=sc: e.activation(
                        MRB[:, 0, c * 512:(c + 1) * 512], pst[:, sc, :],
                        AF.Identity, scale=1.0 / D),
                        reads=(f"pst{sc}",), writes=(f"MRBmu.{c}",),
                        tag=f"f{l}.sxe{c}")
                mu_res = tuple(f"MRBmu.{c}" for c in range(4))
                for c in range(4):
                    pb = c % 2
                    p.task("tensor", lambda e, c=c, pb=pb: e.matmul(
                        pex2[:, pb, :], self.ones1[:],
                        MRB[:, 0, c * 512:(c + 1) * 512], start=True, stop=True),
                        reads=mu_res + ("ones1_bf",), writes=(f"pex2{pb}",),
                        tag=f"f{l}.mex{c}")
                    p.task("vector", lambda e, c=c, pb=pb: e.tensor_copy(
                        MUB[:, 0, c * 512:(c + 1) * 512], pex2[:, pb, :]),
                        reads=(f"pex2{pb}",), writes=(f"MUB0.{c}",),
                        tag=f"f{l}.mexe{c}")
                mub0_res = tuple(f"MUB0.{c}" for c in range(4))
                for k in range(DT):
                    p.task("vector", lambda e, k=k: e.tensor_tensor(
                        TDIF[:, k, :], ABF[:, k, :], MUB[:, 0, :], op=ALU.subtract),
                        reads=(f"ABF.{l}{k}",) + mub0_res, writes=(f"TDIF{k}",),
                        tag=f"f{l}.td{k}")
                tdif_res = tuple(f"TDIF{k}" for k in range(DT))
                for c in range(4):
                    sc = c % 2
                    for k in range(DT):
                        p.task("vector", lambda e, k=k, c=c: e.tensor_tensor(
                            SQS[:, k % 2, :], TDIF[:, k, c * 512:(c + 1) * 512],
                            TDIF[:, k, c * 512:(c + 1) * 512], op=ALU.mult),
                            reads=(f"TDIF{k}",), writes=(f"SQS{k % 2}",),
                            tag=f"f{l}.sq{c}.{k}")
                        p.task("tensor", lambda e, k=k, c=c, sc=sc: e.matmul(
                            pst[:, sc, :], self.ones_col[:], SQS[:, k % 2, :],
                            start=(k == 0), stop=(k == DT - 1)),
                            reads=(f"SQS{k % 2}", "ones_col_bf"),
                            writes=(f"pst{sc}",),
                            tag=f"f{l}.vmm{c}.{k}")
                    p.task("scalar", lambda e, c=c, sc=sc: e.activation(
                        LNF[:, c * 512:(c + 1) * 512], pst[:, sc, :], AF.Ln,
                        scale=1.0 / D, bias=self.epsv[0:1, :]),
                        reads=(f"pst{sc}", "epsv"), writes=(f"LNF.{c}",),
                        tag=f"f{l}.vln{c}")
                p.task("scalar", lambda e: e.activation(
                    MRB[:, 1, :], LNF[:], AF.Exp, scale=-0.5),
                    reads=tuple(f"LNF.{c}" for c in range(4)),
                    writes=("MRBrs",), tag=f"f{l}.vex")
                for c in range(4):
                    pb = c % 2
                    p.task("tensor", lambda e, c=c, pb=pb: e.matmul(
                        pex2[:, pb, :], self.ones1[:],
                        MRB[:, 1, c * 512:(c + 1) * 512], start=True, stop=True),
                        reads=("MRBrs", "ones1_bf"), writes=(f"pex2{pb}",),
                        tag=f"f{l}.rex{c}")
                    p.task("vector", lambda e, c=c, pb=pb: e.tensor_copy(
                        MUB[:, 1, c * 512:(c + 1) * 512], pex2[:, pb, :]),
                        reads=(f"pex2{pb}",), writes=(f"MUB1.{c}",),
                        tag=f"f{l}.rexe{c}")
                mub1_res = tuple(f"MUB1.{c}" for c in range(4))
                for k in range(DT):
                    pb = k % 2
                    p.task("vector", lambda e, k=k, pb=pb: e.tensor_tensor(
                        T2B[:, pb, :], TDIF[:, k, :], MUB[:, 1, :], op=ALU.mult),
                        reads=(f"TDIF{k}",) + mub1_res, writes=(f"T2B{pb}",),
                        tag=f"f{l}.n2m{k}")
                    p.task("scalar", lambda e, k=k, pb=pb: e.activation(
                        N2[:, k, :], T2B[:, pb, :], AF.Identity,
                        bias=FLB[:, k:k + 1], scale=FLG[:, k:k + 1]),
                        reads=(f"T2B{pb}", "FLG", "FLB"), writes=(f"N2.{k}",),
                        tag=f"f{l}.n2f{k}")
                n2_res = tuple(f"N2.{k}" for k in range(DT))
                if l == 0:
                    self.dump_bf("dbg_n2", lambda i: N2[:, i, :], DT, n2_res)
                p.flush(f"f{l}.ln")
            if sa == f"f{l}ln":
                return True

            with ExitStack() as S6:
                W1B = S6.enter_context(
                    nc.sbuf_tensor(f"W1B{l}", [128, 2, DT, 256], BF16))
                W2B = S6.enter_context(
                    nc.sbuf_tensor(f"W2B{l}", [128, 2, KH, 128], BF16))
                PROD = S6.enter_context(
                    nc.sbuf_tensor(f"PROD{l}", [128, KH, 1024], BF16))
                GEB = S6.enter_context(nc.sbuf_tensor(f"GEB{l}", [128, 2, 1024], BF16))
                B1AR = S6.enter_context(nc.sbuf_tensor(f"B1AR{l}", [1, HIDP], BF16))
                B1G = S6.enter_context(nc.sbuf_tensor(f"B1G{l}", [128, KH], F32))
                B2R = S6.enter_context(nc.sbuf_tensor(f"B2R{l}", [1, D], BF16))
                self.b1g_sb = B1G
                p.dma("gpsimd", B1AR[:], h["b1a_row"][l], writes=("B1AR",),
                      tag=f"f{l}.b1ar")
                p.dma("sync", B1G[:], h["b1g_t"][l], writes=("B1G",),
                      tag=f"f{l}.b1g")
                p.dma("gpsimd", B2R[:], h["b2_row"][l], writes=("B2R",),
                      tag=f"f{l}.b2r")
                prod_res = tuple(f"PROD.{m}" for m in range(KH))
                for nh in range(2):
                    nh0 = nh * 1024
                    with ExitStack() as S6a:
                        pa = S6a.enter_context(
                            nc.psum_tensor(f"pa{l}{nh}", [128, 2, 1024], F32))
                        pg2 = S6a.enter_context(
                            nc.psum_tensor(f"pg2{l}{nh}", [128, 2, 1024], F32))
                        def w1dma(m):
                            p.dma("gpsimd", W1B[:, m % 2, :, :],
                                  h["w1ag"][l][m].rearrange("(k p) c -> p k c",
                                                            p=128),
                                  writes=(f"W1B{m % 2}",), tag=f"f{l}.w1d{nh}.{m}")
                        w1dma(0)
                        w1dma(1)
                        for m in range(KH):
                            self._ffn_mm(l, nh, m, m % 2, pa, pg2, W1B, B1AR,
                                         GEB, PROD)
                            if m + 2 < KH:
                                w1dma(m + 2)
                        p.flush(f"f{l}.w1.{nh}")
                    with ExitStack() as S6b:
                        px = S6b.enter_context(
                            nc.psum_tensor(f"px{l}{nh}", [128, 2, 512], F32))
                        def w2dma(mo):
                            p.dma("gpsimd", W2B[:, mo % 2, :, :],
                                  h["w2cb"][l][mo].rearrange("(k p) c -> p k c",
                                                             p=128),
                                  writes=(f"W2B{mo % 2}",), tag=f"f{l}.w2d{nh}.{mo}")
                        w2dma(0)
                        w2dma(1)
                        for mo in range(DT):
                            s2 = mo % 2
                            for cc in range(2):
                                pb = cc
                                def w2mm(e, mo=mo, s2=s2, cc=cc, pb=pb):
                                    for k in range(KH):
                                        e.matmul(px[:, pb, :], W2B[:, s2, k, :],
                                                 PROD[:, k, cc * 512:(cc + 1) * 512],
                                                 start=(k == 0), stop=False)
                                    return e.matmul(
                                        px[:, pb, :],
                                        B2R[:, mo * 128:(mo + 1) * 128],
                                        self.ones512[:], start=False, stop=True)
                                p.task("tensor", w2mm,
                                       reads=prod_res + (f"W2B{s2}", "B2R",
                                                         "ones512_bf"),
                                       writes=(f"px{pb}",),
                                       tag=f"f{l}.w2mm{nh}.{mo}.{cc}")
                                p.task("vector", lambda e, mo=mo, cc=cc, nh0=nh0,
                                       pb=pb: e.tensor_tensor(
                                    XF[:, mo, nh0 + cc * 512:nh0 + (cc + 1) * 512],
                                    XF[:, mo, nh0 + cc * 512:nh0 + (cc + 1) * 512],
                                    px[:, pb, :], op=ALU.add),
                                    reads=(f"px{pb}", f"XF{mo}"),
                                    writes=(f"XF{mo}",),
                                    tag=f"f{l}.xadd{nh}.{mo}.{cc}")
                            if mo + 2 < DT:
                                w2dma(mo + 2)
                        p.flush(f"f{l}.w2.{nh}")
                if l == 0 and "dbg_xf2" in self.dbg_h:
                    self.dump_f32("dbg_xf2", lambda i: XF[:, i, :], DT, xf_res)
                p.flush(f"f{l}.ffn")
            if sa == f"f{l}ffn":
                return True
        return False

    def _ffn_mm(self, l, nh, m, sl, pa, pg2, W1B, B1AR, GEB, PROD):
        p = self.p
        nh0 = nh * 1024
        mp = m % 2
        n2_res = tuple(f"N2.{k}" for k in range(DT))
        for cc in range(2):
            def amm(e, m=m, sl=sl, cc=cc, nh0=nh0, mp=mp):
                for k in range(DT):
                    e.matmul(pa[:, mp, cc * 512:(cc + 1) * 512],
                             W1B[:, sl, k, 0:128],
                             self.N2[:, k, nh0 + cc * 512:nh0 + (cc + 1) * 512],
                             start=(k == 0), stop=False)
                return e.matmul(pa[:, mp, cc * 512:(cc + 1) * 512],
                                B1AR[:, m * 128:(m + 1) * 128],
                                self.ones512[:], start=False, stop=True)
            p.task("tensor", amm,
                   reads=n2_res + (f"W1B{sl}", "B1AR", "ones512_bf"),
                   writes=(f"pa{mp}",), tag=f"f{l}.amm{nh}.{m}.{cc}")
            def gmm(e, m=m, sl=sl, cc=cc, nh0=nh0, mp=mp):
                for k in range(DT):
                    mm = e.matmul(pg2[:, mp, cc * 512:(cc + 1) * 512],
                                  W1B[:, sl, k, 128:256],
                                  self.N2[:, k, nh0 + cc * 512:nh0 + (cc + 1) * 512],
                                  start=(k == 0), stop=(k == DT - 1))
                return mm
            p.task("tensor", gmm, reads=n2_res + (f"W1B{sl}",),
                   writes=(f"pg2{mp}",), tag=f"f{l}.gmm{nh}.{m}.{cc}")
        p.task("scalar", lambda e, m=m, mp=mp: e.activation(
            GEB[:, mp, :], pg2[:, mp, :],
            AF.Gelu, bias=self.b1g_sb[:, m:m + 1]),
            reads=(f"pg2{mp}", "B1G"), writes=(f"GEB{mp}", f"pg2{mp}"),
            tag=f"f{l}.gelu{nh}.{m}")
        p.task("vector", lambda e, m=m, mp=mp: e.tensor_tensor(
            PROD[:, m, :], pa[:, mp, :],
            GEB[:, mp, :], op=ALU.mult),
            reads=(f"GEB{mp}", f"pa{mp}"),
            writes=(f"PROD.{m}", f"pa{mp}"), tag=f"f{l}.prod{nh}.{m}")

    def _qk_chain(self, l, m, pb, DSTP, QW, SQ, T1, T2, ROWS, RSTDB, ESC,
                  psq, pex, pR):
        p = self.p
        is_q = m < 4
        mt = m if is_q else m - 4
        er = 0 if is_q else 1
        qw_res = tuple(f"QW{pb}.{c}" for c in range(4))
        dst_res = ("QPf" if is_q else "KPf") + str(mt)
        for c in range(4):
            p.task("vector", lambda e, c=c, pb=pb: e.tensor_tensor(
                SQ[:, pb, :], QW[:, pb, c * 512:(c + 1) * 512],
                QW[:, pb, c * 512:(c + 1) * 512], op=ALU.mult),
                reads=(f"QW{pb}.{c}",), writes=(f"SQ{pb}",), tag=f"f{l}.sqr{m}.{c}")
            p.task("tensor", lambda e, c=c, pb=pb: e.matmul(
                psq[:, pb, :], self.E2Csb[:], SQ[:, pb, :],
                start=True, stop=True),
                reads=(f"SQ{pb}", "E2C"), writes=(f"psq{pb}",),
                tag=f"f{l}.ssq{m}.{c}")
            p.task("scalar", lambda e, c=c, pb=pb: e.activation(
                ROWS[0:2, pb, c * 512:(c + 1) * 512], psq[:, pb, :], AF.Ln),
                reads=(f"psq{pb}",), writes=(f"LNR{pb}.{c}",),
                tag=f"f{l}.l2ln{m}.{c}")
        lnr_res = tuple(f"LNR{pb}.{c}" for c in range(4))
        p.task("scalar", lambda e, pb=pb: e.activation(
            ROWS[0:2, pb, :], ROWS[0:2, pb, :], AF.Exp, scale=-0.5),
            reads=lnr_res, writes=(f"LNE{pb}",), tag=f"f{l}.l2ex{m}")
        p.task("vector", lambda e, pb=pb: e.tensor_scalar(
            RSTDB[:, pb, :], ROWS[0:2, pb, :], 1e12, None, op0=ALU.min),
            reads=(f"LNE{pb}",), writes=(f"RSTDB{pb}",), tag=f"f{l}.clip{m}")
        for c in range(4):
            p.task("tensor", lambda e, c=c, pb=pb: e.matmul(
                pex[:, 0, :], ESC[:, er, :], RSTDB[:, pb, c * 512:(c + 1) * 512],
                start=True, stop=True),
                reads=(f"RSTDB{pb}", f"ESC{er}"), writes=("pex0",),
                tag=f"f{l}.exp{m}.{c}")
            p.task("vector", lambda e, c=c: e.tensor_tensor(
                QW[:, pb, c * 512:(c + 1) * 512],
                QW[:, pb, c * 512:(c + 1) * 512], pex[:, 0, :], op=ALU.mult),
                reads=("pex0",) + qw_res, writes=(f"QW{pb}.{c}",),
                tag=f"f{l}.qn{m}.{c}")
        for c in range(4):
            p.task("tensor", lambda e, c=c: e.matmul(
                pR[:, 0, :], self.ROTTsb[:],
                QW[:, pb, c * 512:(c + 1) * 512], start=True, stop=True),
                reads=(f"QW{pb}.{c}", "ROTT"), writes=("pR0",),
                tag=f"f{l}.rot{m}.{c}")
            p.task("vector", lambda e, c=c, mt=mt: e.tensor_tensor(
                DSTP[:, mt, c * 512:(c + 1) * 512], pR[:, 0, :],
                self.ROTsb[:, 1, c * 512:(c + 1) * 512], op=ALU.mult),
                reads=("pR0", "ROT1"), writes=(dst_res,),
                tag=f"f{l}.t2{m}.{c}")
            p.task("vector", lambda e, c=c: e.tensor_tensor(
                T1[:, 0, :], QW[:, pb, c * 512:(c + 1) * 512],
                self.ROTsb[:, 0, c * 512:(c + 1) * 512], op=ALU.mult),
                reads=(f"QW{pb}.{c}", "ROT0"), writes=("T10",),
                tag=f"f{l}.t1{m}.{c}")
            p.task("vector", lambda e, c=c, mt=mt: e.tensor_tensor(
                DSTP[:, mt, c * 512:(c + 1) * 512],
                DSTP[:, mt, c * 512:(c + 1) * 512], T1[:, 0, :],
                op=ALU.add),
                reads=("T10", dst_res), writes=(dst_res,),
                tag=f"f{l}.rsum{m}.{c}")

    def _local_attn(self, l, QP, KP, VTM, OA, PB, PT, pS, pPT, pO):
        p = self.p
        small = self.small
        qp_res = tuple(f"QPf{m}" for m in range(4))
        kp_res = tuple(f"KPf{m}" for m in range(4))
        vtm_res = tuple(f"VTM{vm}.{jg}" for vm in range(4) for jg in range(4))

        def band(w, it):
            if w == 0:
                return 0, (it + 1) * 128
            return it * 128, 640

        def s_task(hp, w, it, u, jt0, iv):
            jlo, W = band(w, it)
            moff = 512 if w == 0 else 0
            chunks = []
            pos = jlo
            while pos < jlo + W:
                end = min(jlo + W, (pos // 512 + 1) * 512)
                chunks.append((pos, end - pos))
                pos = end
            if w == 0:
                mblocks = [(it * 128, 128)]
            else:
                mblocks = [(jlo, 128), (jlo + 512, 128)]
            def smm(e, hp=hp, w=w, iv=iv, jt0=jt0, u=u, chunks=chunks,
                    moff=moff, mblocks=mblocks):
                for (lo, n) in chunks:
                    e.matmul(
                        pS[:, u, lo:lo + n],
                        QP[u * 64:(u + 1) * 64, hp, iv:iv + 128],
                        KP[u * 64:(u + 1) * 64, hp,
                           jt0 * 128 + lo:jt0 * 128 + lo + n],
                        start=True, stop=False)
                for (lo, n) in mblocks:
                    mm = e.matmul(
                        pS[:, u, lo:lo + n],
                        self.ident_bf[:],
                        self.MSKsb[:, it, moff + lo:moff + lo + n],
                        start=False, stop=True)
                return mm
            p.task("tensor", smm,
                   reads=qp_res + kp_res + ("identity_bf", f"MSK{it}"),
                   writes=(f"pS{u}",), tag=f"f{l}.S{hp}.{w}.{it}.{u}")

        def softmax(hp, w, it):
            jlo, W = band(w, it)
            for u in range(2):
                sc = 8 + u * 8 + (it % 2) * 4
                p.task("vector", lambda e, u=u, sc=sc, jlo=jlo, W=W: e.tensor_reduce(
                    small[:, sc:sc + 1], pS[:, u, jlo:jlo + W], axis=AX.X,
                    op=ALU.max, negate=True),
                    reads=(f"pS{u}",), writes=(f"negm{u}.{it % 2}",),
                    tag=f"f{l}.mx{hp}.{w}.{it}.{u}")
            for u in range(2):
                sc = 8 + u * 8 + (it % 2) * 4
                p.task("scalar", lambda e, u=u, sc=sc, it=it, jlo=jlo, W=W:
                       e.activation(
                           PB[:, it % 2, u, jlo:jlo + W], pS[:, u, jlo:jlo + W],
                           AF.Exp, bias=small[:, sc:sc + 1],
                           accum_out=small[:, sc + 1:sc + 2]),
                       reads=(f"pS{u}", f"negm{u}.{it % 2}"),
                       writes=(f"PB{u}.{it % 2}", f"z{u}.{it % 2}"),
                       tag=f"f{l}.ex{hp}.{w}.{it}.{u}")
            for u in range(2):
                sc = 8 + u * 8 + (it % 2) * 4
                p.task("vector", lambda e, sc=sc: e.reciprocal(
                    small[:, sc + 2:sc + 3], small[:, sc + 1:sc + 2]),
                    reads=(f"z{u}.{it % 2}",), writes=(f"rz{u}.{it % 2}",),
                    tag=f"f{l}.rz{hp}.{w}.{it}.{u}")
            for u in range(2):
                sc = 8 + u * 8 + (it % 2) * 4
                p.task("vector", lambda e, u=u, sc=sc, it=it, jlo=jlo, W=W:
                       e.tensor_scalar(
                           PB[:, it % 2, u, jlo:jlo + W],
                           PB[:, it % 2, u, jlo:jlo + W],
                           small[:, sc + 2:sc + 3], None, op0=ALU.mult),
                       reads=(f"PB{u}.{it % 2}", f"rz{u}.{it % 2}"),
                       writes=(f"PB{u}.{it % 2}",),
                       tag=f"f{l}.nm{hp}.{w}.{it}.{u}")
                if l == 0 and hp == 0 and w == 1 and "dbg_pb" in self.dbg_h:
                    rid = f"dbg_pb.{it * 2 + u}"
                    p.dma("gpsimd", self.dbg_h["dbg_pb"][it * 2 + u],
                          PB[:, it % 2, u, :], reads=(f"PB{u}.{it % 2}",),
                          writes=(rid,), tag=rid)
                    self.out_res.append(rid)

        def emit_tp(hp, w, it):
            jlo, W = band(w, it)
            jb0 = jlo // 128
            nb = W // 128
            for u in range(2):
                def ptp(e, u=u, it=it, jb0=jb0, nb=nb):
                    for jj in range(nb):
                        jb = jb0 + jj
                        mm = e.transpose(
                            pPT[:, u, jj * 128:(jj + 1) * 128],
                            PB[:, it % 2, u, jb * 128:(jb + 1) * 128],
                            self.ident_bf[:])
                    return mm
                p.task("tensor", ptp, reads=(f"PB{u}.{it % 2}", "identity_bf"),
                       writes=(f"pPT{u}",), tag=f"f{l}.tp{hp}.{w}.{it}.{u}")
                def pev(e, u=u, it=it, jb0=jb0, nb=nb):
                    return e.activation(
                        PT[:, u, jb0:jb0 + nb, it * 128:(it + 1) * 128],
                        pPT[:, u, 0:nb * 128].rearrange("p (a b) -> p a b", b=128),
                        AF.Identity)
                p.task("scalar", pev, reads=(f"pPT{u}",),
                       writes=(f"PTc{u}.{it}",),
                       tag=f"f{l}.pe{hp}.{w}.{it}.{u}")

        def icols(w, jt, njt):
            if w == 0:
                return jt * 128, 512 - jt * 128
            lo = max(0, jt - 4) * 128
            hi = min(4, jt + 1) * 128
            return lo, hi - lo

        for hp in range(4):
            for w in range(NW):
                jt0 = 0 if w == 0 else (w - 1) * 4
                njt = 4 if w == 0 else 8
                for it in range(4):
                    iv = w * 512 + it * 128
                    for u in range(2):
                        s_task(hp, w, it, u, jt0, iv)
                    if it > 0:
                        emit_tp(hp, w, it - 1)
                    softmax(hp, w, it)
                emit_tp(hp, w, 3)
                pt_res = tuple(f"PTc{u}.{it}" for u in range(2) for it in range(4))
                pbo = w % 2
                jt_order = list(range(njt)) if w == 0 else [3, 4, 2, 5, 1, 6, 0, 7]
                def omm(e, hp=hp, w=w, jt0=jt0, pbo=pbo, jt_order=jt_order):
                    for u in range(2):
                        hh = hp * 2 + u
                        first = True
                        for jt in jt_order:
                            ilo, n = icols(w, jt, 0)
                            mm = e.matmul(
                                pO[u * 64:(u + 1) * 64, pbo, ilo:ilo + n],
                                VTM[:, jt0 + jt, hh * 64:(hh + 1) * 64],
                                PT[:, u, jt, ilo:ilo + n],
                                start=first, stop=(jt == jt_order[-1]),
                                tile_position=(0, u * 64))
                            first = False
                    return mm
                p.task("tensor", omm, reads=pt_res + vtm_res,
                       writes=(f"pO{pbo}",), tag=f"f{l}.O{hp}.{w}")
                p.task("scalar", lambda e, hp=hp, w=w, pbo=pbo: e.activation(
                    OA[:, hp, w * 512:(w + 1) * 512], pO[:, pbo, :],
                    AF.Identity),
                    reads=(f"pO{pbo}",), writes=(f"OA{hp}.{w}",),
                    tag=f"f{l}.Oe{hp}.{w}")

    def final_phase(self):
        nc, p, h = self.nc, self.p, self.h
        XF = self.XF
        with ExitStack() as S:
            SS = S.enter_context(nc.sbuf_tensor("SS", [128, 3, N], F32))
            XB = S.enter_context(nc.sbuf_tensor("XB", [128, 3, D], F32))
            OT = S.enter_context(nc.sbuf_tensor("OT", [128, 3, D], F32))
            pfin = S.enter_context(nc.psum_tensor("pfin", [128, 2, 1024], F32))
            for k in range(DT):
                b = k % 3
                p.task("scalar", lambda e, k=k: e.activation(
                    XF[:, k, :], XF[:, k, :], AF.Silu),
                    reads=(f"XF{k}",), writes=(f"XF{k}",), tag=f"fin.silu{k}")
                if "dbg_g" in self.dbg_h:
                    rid = f"dbg_g.{k}"
                    p.dma("sync", self.dbg_h["dbg_g"][k], XF[:, k, :],
                          reads=(f"XF{k}",), writes=(rid,), tag=rid)
                    self.out_res.append(rid)
                p.dma("sync" if k % 2 == 0 else "scalar", SS[:, b, :],
                      self.s_scr[k],
                      reads=(f"s_scr{k}",), writes=(f"SS{b}",), tag=f"fin.s{k}")
                p.task("vector", lambda e, k=k, b=b: e.tensor_tensor(
                    XF[:, k, :], XF[:, k, :], SS[:, b, :], op=ALU.mult),
                    reads=(f"XF{k}", f"SS{b}"), writes=(f"XF{k}",),
                    tag=f"fin.sg{k}")
            for t in range(NT):
                b = t % 3
                p.dma("sync" if t % 2 == 0 else "scalar", XB[:, b, :],
                      h["x"][t * 128:(t + 1) * 128, :],
                      reads=(f"XB{b}c",), writes=(f"XB{b}",), tag=f"fin.x{t}")
                bp = t % 2
                def tpf(e, t=t, bp=bp):
                    for k in range(DT):
                        mm = e.transpose(pfin[:, bp, k * 128:(k + 1) * 128],
                                         XF[:, k, t * 128:(t + 1) * 128],
                                         self.ident_f32[:])
                    return mm
                p.task("tensor", tpf,
                       reads=tuple(f"XF{k}" for k in range(DT)) + ("identity_f32",),
                       writes=(f"pfin{bp}",), tag=f"fin.tp{t}")
                p.task("vector", lambda e, t=t, b=b, bp=bp: e.tensor_tensor(
                    OT[:, b, :], XB[:, b, :], pfin[:, bp, :], op=ALU.add),
                    reads=(f"pfin{bp}", f"XB{b}"), writes=(f"OT{b}", f"XB{b}c"),
                    tag=f"fin.add{t}")
                rid = f"out{t}"
                p.dma("gpsimd", self.out_h[t * 128:(t + 1) * 128, :], OT[:, b, :],
                      reads=(f"OT{b}",), writes=(rid,), tag=rid)
                self.out_res.append(rid)
            p.flush("final")




def build_nc(debug=(), stop_after=None):
    b = Build(debug, stop_after)
    nc, dbg = b.build()
    return nc, dbg


def kernel(**inputs):
    g = prep_inputs(inputs)
    c = host_constants()
    nc, _ = build_nc()
    base = {}
    for n in INPUT_NAMES:
        base[n] = g[n]
    for n in CONST_NAMES:
        base[n] = c[n]
    in_maps = []
    for b in range(B):
        m = dict(base)
        m["x"] = np.ascontiguousarray(g["x"][b])
        in_maps.append(m)
    res = run_bass_kernel_spmd(nc, in_maps, core_ids=list(range(B)))
    return np.stack([res.results[b]["out"] for b in range(B)], axis=0)


# version 20 (final)
# speedup vs baseline: 1.0971x; 1.0380x over previous
import numpy as np
import ml_dtypes
from contextlib import ExitStack

import concourse.bass as bass
import concourse.mybir as mybir
from concourse.bass_utils import run_bass_kernel_spmd

F32 = mybir.dt.float32
BF16 = mybir.dt.bfloat16
AF = mybir.ActivationFunctionType
ALU = mybir.AluOpType
AX = mybir.AxisListType
BF = ml_dtypes.bfloat16

B, N, D = 8, 2048, 1024
NT, DT = N // 128, D // 128
HIDDEN, QKD = 2048, 2048
HEADS, DH, INNER = 8, 64, 512
WIN, NW = 512, 4
HID, HIDP = 2730, 2816
KH = HIDP // 128
EPS = 1e-5
NEG = -1e30

ENGINES = ("tensor", "vector", "scalar", "gpsimd", "sync")
TRUNC_M = 12
CHAIN_ON = True
VPART = 3



class Prog:

    R = 8

    def __init__(self, nc, stack):
        self.nc = nc
        self.sems = {e: stack.enter_context(nc.semaphore("sem_" + e)) for e in ENGINES}
        self.dsems = {e: [stack.enter_context(nc.semaphore(f"dsem_{e}{r}"))
                          for r in range(self.R)] for e in ENGINES}
        self.cnt = {e: 0 for e in ENGINES}
        self.dcnt = {e: 0 for e in ENGINES}
        self.drot = {e: [0] * self.R for e in ENGINES}
        self.cov = {e: {} for e in ENGINES}
        self.pending = {e: [] for e in ENGINES}
        self.res = {}
        self.n_tasks = 0

    def task(self, eng, emit, reads=(), writes=(), tag="", n_dma=0):
        deps = {}

        def add(tok):
            if tok is None:
                return
            k, v = tok[:-1], tok[-1]
            if deps.get(k, 0) < v:
                deps[k] = v

        for r in reads:
            e = self.res.get(r)
            if e:
                add(e["w"])
        for r in writes:
            e = self.res.get(r)
            if e:
                add(e["w"])
                for t in e["r"]:
                    add(t)
        if eng == "tensor":
            deps = {k: v for k, v in deps.items() if k != ("c", "tensor")}
        if n_dma:
            assert n_dma == 1
            r = self.dcnt[eng] % self.R
            self.dcnt[eng] += 1
            prev = self.drot[eng][r]
            self.drot[eng][r] = prev + 16
            if prev and deps.get(("d", eng, r), 0) < prev:
                deps[("d", eng, r)] = prev
            comp = ("d", eng, r, prev + 16)
            self.pending[eng].append((deps, emit, r, tag))
        else:
            self.cnt[eng] += 1
            comp = ("c", eng, self.cnt[eng])
            self.pending[eng].append((deps, emit, None, tag))
        self.n_tasks += 1
        for r in reads:
            self.res.setdefault(r, {"w": None, "r": []})["r"].append(comp)
        for r in writes:
            self.res[r] = {"w": comp, "r": []}

    def dma(self, eng, out, in_, reads=(), writes=(), tag=""):
        def emit(e, out=out, in_=in_):
            return [e.dma_start(out=out, in_=in_)]
        self.task(eng, emit, reads, writes, tag, n_dma=1)

    def wait_only(self, eng, deps, tag="wait"):
        self.pending[eng].append((dict(deps), None, 0, tag))

    def flush(self, name):
        for e in ENGINES:
            if self.dcnt[e]:
                self.wait_only(e, {("d", e, r): self.drot[e][r]
                                   for r in range(self.R) if self.drot[e][r]},
                               "fence")
        with self.nc.named_scope(name), self.nc.Block() as block:
            for e in ENGINES:
                tasks = self.pending[e]
                self.pending[e] = []
                if not tasks:
                    continue

                def gen(e=e, tasks=tasks):
                    def _(eng):
                        cov = self.cov[e]
                        for (deps, emit, dslot, tag) in tasks:
                            for key, val in sorted(deps.items()):
                                if cov.get(key, 0) >= val:
                                    continue
                                if key[0] == "d":
                                    sem = self.dsems[key[1]][key[2]]
                                else:
                                    sem = self.sems[key[1]]
                                eng.wait_ge(sem, val)
                                cov[key] = val
                            if emit is None:
                                continue
                            if dslot is not None:
                                insts = emit(eng)
                                assert len(insts) == 1, tag
                                insts[0].then_inc(self.dsems[e][dslot], 16)
                            else:
                                inst = emit(eng)
                                if inst is None:
                                    inst = eng.nop()
                                inst.then_inc(self.sems[e], 1)
                    return _
                getattr(block, e)(gen())
        for e in ENGINES:
            for e2 in ENGINES:
                self.cov[e][("c", e2)] = self.cnt[e2]
                for r in range(self.R):
                    self.cov[e][("d", e2, r)] = self.drot[e2][r]


def host_constants():
    c = {}
    c["identity_bf"] = np.eye(128, dtype=BF)
    c["identity_f32"] = np.eye(128, dtype=np.float32)
    c["ones1_bf"] = np.ones((1, 128), dtype=BF)
    c["ones_col_bf"] = np.ones((128, 1), dtype=BF)
    c["ones512_bf"] = np.ones((1, 512), dtype=BF)
    e2c = np.zeros((128, 2), dtype=np.float32)
    e2c[0:64, 0] = 1.0
    e2c[64:128, 1] = 1.0
    c["e2c_bf"] = e2c.astype(BF)
    e2t = np.zeros((2, 128), dtype=np.float32)
    e2t[0, 0:64] = 1.0
    e2t[1, 64:128] = 1.0
    c["e2t"] = e2t
    R = np.zeros((128, 128), dtype=np.float32)
    for base in (0, 64):
        for i in range(32):
            R[base + i, base + 32 + i] = -1.0
            R[base + 32 + i, base + i] = 1.0
    c["rotT_bf"] = R.T.copy().astype(BF)
    half = np.arange(0, DH, 2, dtype=np.float64)
    inv_freq = 1.0 / (10000.0 ** (half / DH))
    tpos = np.arange(N, dtype=np.float64)
    freqs = np.concatenate([tpos[:, None] * inv_freq[None, :]] * 2, axis=-1)
    base = (half + 0.4 * DH) / (1.4 * DH)
    power = (tpos - N // 2) / float(WIN // 2)
    scale = np.concatenate([base[None, :] ** power[:, None]] * 2, axis=-1)
    cos, sin = np.cos(freqs), np.sin(freqs)
    QKS = 8.0
    tab = np.stack([(cos * scale * QKS).T, (sin * scale * QKS).T,
                    (cos / scale).T, (sin / scale).T])
    c["rotary"] = np.tile(tab, (1, 2, 1)).astype(BF)
    m1 = np.zeros((4, 128, 1024), dtype=BF)
    j = np.arange(1024)
    for it in range(4):
        for pp in range(128):
            il = it * 128 + pp
            m1[it, pp] = np.where((j >= il) & (j <= il + WIN), 0.0, NEG).astype(BF)
    c["mask1"] = m1
    return c


def prep_inputs(inputs):
    g = {}
    f32 = np.float32
    asf = lambda a: np.ascontiguousarray(np.asarray(a, dtype=f32))
    g["x"] = asf(inputs["L0_attn_x"])
    g["W_hidden"] = asf(inputs["L0_attn_W_hidden"])
    g["W_qk"] = asf(inputs["L0_attn_W_qk"])
    g["W_out"] = asf(inputs["L0_attn_W_out"])
    g["bh_t"] = asf(inputs["L0_attn_b_hidden"]).reshape(32, 128).T.copy()
    g["bqk_row"] = asf(inputs["L0_attn_b_qk"]).reshape(1, QKD)
    g["bout_t"] = asf(inputs["L0_attn_b_out"]).reshape(8, 128).T.copy()
    g["lng_t"] = asf(inputs["L0_attn_ln_g"]).reshape(8, 128).T.copy()
    g["lnb_t"] = asf(inputs["L0_attn_ln_b"]).reshape(8, 128).T.copy()
    osg = asf(inputs["L0_attn_os_gamma"])
    osb = asf(inputs["L0_attn_os_beta"])
    g["gb_row"] = np.ascontiguousarray(
        np.concatenate([osg[0], osb[0], osg[1], osb[1]]).reshape(1, 4 * QKD))
    g["wqkv"] = asf(inputs["L0_ffn_wqkv"])
    g["wo"] = asf(inputs["L0_ffn_wo"])
    w1 = asf(inputs["L0_ffn_w1"])
    w2 = asf(inputs["L0_ffn_w2"])
    b1 = asf(inputs["L0_ffn_b1"])
    b2 = asf(inputs["L0_ffn_b2"])
    w1a = np.zeros((2, D, HIDP), f32)
    w1g = np.zeros((2, D, HIDP), f32)
    w1a[:, :, :HID] = w1[:, :, :HID]
    w1g[:, :, :HID] = w1[:, :, HID:]
    w1ag = np.zeros((2, KH, D, 256), f32)
    for m in range(KH):
        w1ag[:, m, :, 0:128] = w1a[:, :, m * 128:(m + 1) * 128]
        w1ag[:, m, :, 128:256] = w1g[:, :, m * 128:(m + 1) * 128]
    g["w1ag"] = np.ascontiguousarray(w1ag)
    w2p = np.zeros((2, HIDP, D), f32)
    w2p[:, :HID, :] = w2
    w2cb = np.zeros((2, 8, HIDP, 128), f32)
    for mo in range(8):
        w2cb[:, mo] = w2p[:, :, mo * 128:(mo + 1) * 128]
    g["w2cb"] = np.ascontiguousarray(w2cb)
    g["b2_row"] = np.ascontiguousarray(b2.reshape(2, 1, D))
    b1a = np.zeros((2, HIDP), f32)
    b1g = np.zeros((2, HIDP), f32)
    b1a[:, :HID] = b1[:, :HID]
    b1g[:, :HID] = b1[:, HID:]
    g["b1a_row"] = np.ascontiguousarray(b1a.reshape(2, 1, HIDP))
    g["b1g_t"] = np.ascontiguousarray(b1g.reshape(2, KH, 128).transpose(0, 2, 1))
    g["flg_t"] = np.ascontiguousarray(
        asf(inputs["L0_ffn_ln_g"]).reshape(2, 8, 128).transpose(0, 2, 1))
    g["flb_t"] = np.ascontiguousarray(
        asf(inputs["L0_ffn_ln_b"]).reshape(2, 8, 128).transpose(0, 2, 1))
    qs = asf(inputs["L0_ffn_q_scale"])
    ks = asf(inputs["L0_ffn_k_scale"])
    g["qs_tile"] = np.ascontiguousarray(
        np.broadcast_to(np.tile(qs, (1, 2))[:, None, :], (2, 2, 128)).copy())
    g["ks_tile"] = np.ascontiguousarray(
        np.broadcast_to(np.tile(ks, (1, 2))[:, None, :], (2, 2, 128)).copy())
    return g


INPUT_NAMES = [
    "W_hidden", "W_qk", "W_out", "bh_t", "bqk_row", "bout_t", "lng_t", "lnb_t",
    "gb_row", "wqkv", "wo", "w1ag", "w2cb", "b2_row", "b1a_row", "b1g_t",
    "flg_t", "flb_t", "qs_tile", "ks_tile"]
CONST_NAMES = ["e2t", "e2c_bf", "rotT_bf", "identity_f32", "identity_bf",
               "ones1_bf", "ones_col_bf", "ones512_bf", "rotary", "mask1"]


class Build:
    def __init__(self, debug=(), stop_after=None):
        self.nc = bass.Bass("TRN2", target_bir_lowering=False)
        self.dbg = set(debug)
        self.stop_after = stop_after
        self.h = {}
        self.dbg_h = {}
        self.out_res = []

    def inp(self, name, shape, dt=F32):
        self.h[name] = self.nc.dram_tensor(
            name, list(shape), dt, kind="ExternalInput").ap()

    def declare_io(self):
        nc = self.nc
        self.inp("x", (N, D))
        self.inp("W_hidden", (D, 2 * HIDDEN))
        self.inp("W_qk", (D, QKD))
        self.inp("W_out", (HIDDEN, D))
        self.inp("bh_t", (128, 32))
        self.inp("bqk_row", (1, QKD))
        self.inp("bout_t", (128, 8))
        self.inp("lng_t", (128, 8))
        self.inp("lnb_t", (128, 8))
        self.inp("gb_row", (1, 4 * QKD))
        self.inp("wqkv", (2, D, 3 * INNER))
        self.inp("wo", (2, INNER, D))
        self.inp("w1ag", (2, KH, D, 256))
        self.inp("w2cb", (2, 8, HIDP, 128))
        self.inp("b2_row", (2, 1, D))
        self.inp("b1a_row", (2, 1, HIDP))
        self.inp("b1g_t", (2, 128, KH))
        self.inp("flg_t", (2, 128, 8))
        self.inp("flb_t", (2, 128, 8))
        self.inp("qs_tile", (2, 2, 128))
        self.inp("ks_tile", (2, 2, 128))
        self.inp("e2t", (2, 128))
        self.inp("e2c_bf", (128, 2), BF16)
        self.inp("rotT_bf", (128, 128), BF16)
        self.inp("identity_f32", (128, 128))
        self.inp("identity_bf", (128, 128), BF16)
        self.inp("ones1_bf", (1, 128), BF16)
        self.inp("ones_col_bf", (128, 1), BF16)
        self.inp("ones512_bf", (1, 512), BF16)
        self.inp("rotary", (4, 128, N), BF16)
        self.inp("mask1", (4, 128, 1024), BF16)
        self.out_h = nc.dram_tensor("out", [N, D], F32, kind="ExternalOutput").ap()
        self.xn_scr = nc.dram_tensor("xn_scr", [DT, 128, N], F32).ap()
        self.s_scr = nc.dram_tensor("s_scr", [DT, 128, N], F32).ap()
        for name, shape in [
                ("dbg_xn", (DT, 128, N)), ("dbg_normed", (DT, 128, N)),
                ("dbg_qk", (NT, 128, QKD)), ("dbg_rowsum", (128, NT)),
                ("dbg_vg", (NT, 128, N)), ("dbg_s", (DT, 128, N)),
                ("dbg_qp", (4, 128, N)), ("dbg_kp", (4, 128, N)),
                ("dbg_vt", (NT, 128, INNER)),
                ("dbg_oall", (4, 128, N)), ("dbg_xf", (DT, 128, N)),
                ("dbg_n2", (DT, 128, N)), ("dbg_h1", (KH, 128, N)),
                ("dbg_pb", (8, 128, 1024)), ("dbg_s2", (8, 128, 1024)),
                ("dbg_xf2", (DT, 128, N))]:
            if name in self.dbg:
                self.dbg_h[name] = self.nc.dram_tensor(
                    name, list(shape), F32, kind="ExternalOutput").ap()

    def dump_bf(self, name, src_fn, count, reads):
        hd = self.dbg_h.get(name)
        if hd is None:
            return
        for i in range(count):
            rid = f"{name}.{i}"
            self.p.dma("gpsimd", hd[i], src_fn(i), reads=reads, writes=(rid,),
                       tag=f"dbg.{name}{i}")
            self.out_res.append(rid)

    def dump_f32(self, name, src_fn, count, reads):
        hd = self.dbg_h.get(name)
        if hd is None:
            return
        for i in range(count):
            rid = f"{name}.{i}"
            self.p.dma("sync", hd[i], src_fn(i), reads=reads, writes=(rid,),
                       tag=f"dbg.{name}{i}")
            self.out_res.append(rid)

    def build(self):
        self.declare_io()
        nc = self.nc
        with ExitStack() as TOP:
            E = TOP.enter_context
            self.p = Prog(nc, TOP)
            p = self.p

            self.ABF = E(nc.sbuf_tensor("ABF", [128, DT, N], BF16))
            self.ident_bf = E(nc.sbuf_tensor("ident_bf", [128, 128], BF16))
            self.ident_f32 = E(nc.sbuf_tensor("ident_f32", [128, 128], F32))
            self.ones1 = E(nc.sbuf_tensor("ones1", [1, 128], BF16))
            self.rowsum = E(nc.sbuf_tensor("rowsum", [128, NT], F32))
            self.small = E(nc.sbuf_tensor("small", [128, 48], F32))
            self.bh_sb = E(nc.sbuf_tensor("bh_sb", [128, 32], F32))
            self.bout_sb = E(nc.sbuf_tensor("bout_sb", [128, 8], F32))
            self.lng_sb = E(nc.sbuf_tensor("lng_sb", [128, 8], F32))
            self.lnb_sb = E(nc.sbuf_tensor("lnb_sb", [128, 8], F32))
            self.epsv = E(nc.sbuf_tensor("epsv", [128, 1], F32))
            self.ones_col = E(nc.sbuf_tensor("ones_col", [128, 1], BF16))
            self.ones512 = E(nc.sbuf_tensor("ones512", [1, 512], BF16))
            self.e2t_sb = E(nc.sbuf_tensor("e2t_sb", [2, 128], F32))

            for nm, sb in [("identity_bf", self.ident_bf),
                           ("identity_f32", self.ident_f32),
                           ("ones1_bf", self.ones1), ("bh_t", self.bh_sb),
                           ("ones_col_bf", self.ones_col),
                           ("ones512_bf", self.ones512), ("e2t", self.e2t_sb),
                           ("bout_t", self.bout_sb), ("lng_t", self.lng_sb),
                           ("lnb_t", self.lnb_sb)]:
                p.dma("sync", sb[:], self.h[nm][:], writes=(nm,), tag="const")
            p.task("vector", lambda e: e.memset(self.epsv[:], EPS),
                   writes=("epsv",), tag="c.eps")

            with ExitStack() as SQG:
                self.QK = SQG.enter_context(
                    nc.sbuf_tensor("QK", [128, NT, QKD], BF16))
                self.GBC = SQG.enter_context(
                    nc.sbuf_tensor("GBC", [128, 4, QKD], BF16))
                with ExitStack() as SWQK:
                    self.WQK = SWQK.enter_context(
                        nc.sbuf_tensor("WQK", [128, DT, QKD], BF16))
                    self.phase0()
                    if self.stop_after != "p0":
                        self.attn_p1()
                if self.stop_after not in ("p0", "p1"):
                    self.attn_p2()
            if self.stop_after not in ("p0", "p1", "p2"):
                self.attn_p34()
                if self.stop_after not in ("p3", "p4"):
                    self.ffn_slot()

            p.wait_only("sync", self._res_deps(self.out_res), "drain")
            p.flush("drain")
        return nc, self.dbg_h

    def _res_deps(self, names):
        deps = {}
        for r in names:
            e = self.p.res.get(r)
            if e and e["w"] is not None:
                tok = e["w"]
                k, v = tok[:-1], tok[-1]
                if deps.get(k, 0) < v:
                    deps[k] = v
        return deps

    def phase0(self):
        nc, p = self.nc, self.p
        h = self.h
        with ExitStack() as S:
            XT = S.enter_context(nc.sbuf_tensor("XT", [128, 4, D], F32))
            XNT = S.enter_context(nc.sbuf_tensor("XNT", [128, 4, D], F32))
            NTB = S.enter_context(nc.sbuf_tensor("NTB", [128, 4, D], BF16))
            XNW = S.enter_context(nc.sbuf_tensor("XNW", [128, 2, DT, 128], F32))
            ST6 = S.enter_context(nc.sbuf_tensor("ST6", [128, 4, 12], F32))
            MV = S.enter_context(nc.sbuf_tensor("MV", [128, 4, 2], F32))
            pbf = S.enter_context(nc.psum_tensor("pbf", [128, 2, 1024], BF16))
            pxf = S.enter_context(nc.psum_tensor("pxf", [128, 2, 1024], F32))
            small = self.small
            ABF = self.ABF

            def stats(src_ap, src_res, rstd_col, tagp, b):
                p.task("vector", lambda e, v=src_ap, b=b: (
                    e.bn_stats(ST6[:, b, 0:6], v[:, 0:512]),
                    e.bn_stats(ST6[:, b, 6:12], v[:, 512:1024]))[-1],
                    reads=(src_res,), writes=(f"ST6{b}",), tag=tagp + ".bn")
                p.task("vector", lambda e, b=b: e.bn_aggr(MV[:, b, :], ST6[:, b, :]),
                       reads=(f"ST6{b}",), writes=(f"MV{b}",), tag=tagp + ".ag")
                p.task("scalar", lambda e, b=b: e.activation(
                    small[:, 8 + b:9 + b], MV[:, b, 1:2], AF.Ln,
                    bias=self.epsv[:], scale=1.0),
                    reads=(f"MV{b}", "epsv"), writes=(f"lnv{b}",), tag=tagp + ".ln")
                p.task("scalar", lambda e, rc=rstd_col, b=b: e.activation(
                    small[:, rc:rc + 1], small[:, 8 + b:9 + b], AF.Exp, scale=-0.5),
                    reads=(f"lnv{b}",), writes=(f"rstd{rstd_col}",), tag=tagp + ".ex")

            def stageA(t):
                b = t % 4
                p.dma("sync", XT[:, b, :], h["x"][t * 128:(t + 1) * 128, :],
                      writes=(f"XT{b}",), tag=f"p0.load{t}")
                stats(XT[:, b, :], f"XT{b}", 16 + b, f"p0.s1_{t}", b)

            def stageB(t):
                b = t % 4
                p.task("vector", lambda e, b=b: e.tensor_scalar(
                    XNT[:, b, :], XT[:, b, :], MV[:, b, 0:1], small[:, 16 + b:17 + b],
                    op0=ALU.subtract, op1=ALU.mult),
                    reads=(f"XT{b}", f"MV{b}", f"rstd{16 + b}"),
                    writes=(f"XNT{b}",), tag=f"p0.xn{t}")
                stats(XNT[:, b, :], f"XNT{b}", 20 + b, f"p0.s2_{t}", b)

            def stageC(t):
                b = t % 4
                bp = t % 2
                p.task("vector", lambda e, b=b: e.tensor_scalar(
                    NTB[:, b, :], XNT[:, b, :], MV[:, b, 0:1], small[:, 20 + b:21 + b],
                    op0=ALU.subtract, op1=ALU.mult),
                    reads=(f"XNT{b}", f"MV{b}", f"rstd{20 + b}"),
                    writes=(f"NTB{b}",), tag=f"p0.nt{t}")

                def tp_n(e, b=b, bp=bp):
                    for k in range(DT):
                        mm = e.transpose(pbf[:, bp, k * 128:(k + 1) * 128],
                                         NTB[:, b, k * 128:(k + 1) * 128],
                                         self.ident_bf[:])
                    return mm
                p.task("tensor", tp_n, reads=(f"NTB{b}", "identity_bf"),
                       writes=(f"pbf{bp}",), tag=f"p0.tpn{t}")

                def ev_n(e, t=t, bp=bp):
                    for k in range(DT):
                        mm = e.activation(
                            ABF[:, k, t * 128:(t + 1) * 128],
                            pbf[:, bp, k * 128:(k + 1) * 128],
                            AF.Identity, bias=self.lnb_sb[:, k:k + 1],
                            scale=self.lng_sb[:, k:k + 1])
                    return mm
                p.task("scalar", ev_n, reads=(f"pbf{bp}", "lng_t", "lnb_t"),
                       writes=(f"ABF.{t}",), tag=f"p0.evn{t}")

                def tp_x(e, b=b, bp=bp):
                    for k in range(DT):
                        mm = e.transpose(pxf[:, bp, k * 128:(k + 1) * 128],
                                         XNT[:, b, k * 128:(k + 1) * 128],
                                         self.ident_f32[:])
                    return mm
                p.task("tensor", tp_x, reads=(f"XNT{b}", "identity_f32"),
                       writes=(f"pxf{bp}",), tag=f"p0.tpx{t}")

                def ev_x(e, t=t, b=b, bp=bp):
                    return e.tensor_copy(
                        XNW[:, bp, :, :],
                        pxf[:, bp, :].rearrange("p (a b) -> p a b", b=128))
                p.task("vector", ev_x, reads=(f"pxf{bp}",),
                       writes=(f"XNW{bp}",), tag=f"p0.evx{t}")
                p.dma("gpsimd",
                      self.xn_scr.rearrange("k p n -> p k n")[:, :, t * 128:(t + 1) * 128],
                      XNW[:, bp, :, :],
                      reads=(f"XNW{bp}",), writes=(f"xn_scr.{t}",),
                      tag=f"p0.spill{t}")

            stageA(0)
            stageA(1)
            stageB(0)
            for t in range(NT):
                if t < DT:
                    p.dma("gpsimd", self.WQK[:, t, :],
                          h["W_qk"][t * 128:(t + 1) * 128, :],
                          writes=(f"WQK{t}",), tag=f"p0.wqk{t}")
                if t + 2 < NT:
                    stageA(t + 2)
                if t + 1 < NT:
                    stageB(t + 1)
                stageC(t)

            allxn = tuple(f"xn_scr.{t}" for t in range(NT))
            for k in range(DT):
                p.wait_only("sync", {}, "noop")
            if "dbg_xn" in self.dbg_h:
                for k in range(DT):
                    rid = f"dbg_xn.{k}"
                    p.dma("gpsimd", self.dbg_h["dbg_xn"][k], self.xn_scr[k],
                          reads=allxn, writes=(rid,), tag=rid)
                    self.out_res.append(rid)
            self.dump_bf("dbg_normed", lambda i: ABF[:, i, :], DT,
                         tuple(f"ABF.{t}" for t in range(NT)))
            p.flush("p0")

    def attn_p1(self):
        nc, p, h = self.nc, self.p, self.h
        ABF, QK, GBC, WQK = self.ABF, self.QK, self.GBC, self.WQK
        abf_res = tuple(f"ABF.{t}" for t in range(NT))
        with ExitStack() as S1a:
            GBR = S1a.enter_context(nc.sbuf_tensor("GBR", [1, 4 * QKD], BF16))
            BQR = S1a.enter_context(nc.sbuf_tensor("BQR", [1, QKD], BF16))
            pqk = S1a.enter_context(nc.psum_tensor("pqk", [128, 2, 512], F32))
            pgb = S1a.enter_context(nc.psum_tensor("pgb", [128, 2, 512], F32))

            p.dma("gpsimd", GBR[:], h["gb_row"][:], writes=("GBR",), tag="p1.gbr")
            p.dma("gpsimd", BQR[:], h["bqk_row"][:], writes=("BQR",), tag="p1.bqr")
            for r in range(4):
                for c in range(4):
                    pb = (r * 4 + c) % 2
                    p.task("tensor", lambda e, r=r, c=c, pb=pb: e.matmul(
                        pgb[:, pb, :], self.ones1[:],
                        GBR[:, r * QKD + c * 512:r * QKD + (c + 1) * 512],
                        start=True, stop=True),
                        reads=("GBR", "ones1_bf"), writes=(f"pgb{pb}",),
                        tag=f"p1.gbmm{r}.{c}")
                    p.task("vector", lambda e, r=r, c=c, pb=pb: e.tensor_copy(
                        GBC[:, r, c * 512:(c + 1) * 512], pgb[:, pb, :]),
                        reads=(f"pgb{pb}",), writes=(f"GBC{r}.{c}",),
                        tag=f"p1.gbev{r}.{c}")

            wqk_res = tuple(f"WQK{k}" for k in range(DT))
            for t in range(NT):
                for c in range(4):
                    pb = (t * 4 + c) % 2
                    def qkmm(e, t=t, c=c, pb=pb):
                        for k in range(DT):
                            e.matmul(pqk[:, pb, :],
                                     ABF[:, k, t * 128:(t + 1) * 128],
                                     WQK[:, k, c * 512:(c + 1) * 512],
                                     start=(k == 0), stop=False)
                        return e.matmul(pqk[:, pb, :], self.ones1[:],
                                        BQR[:, c * 512:(c + 1) * 512],
                                        start=False, stop=True)
                    p.task("tensor", qkmm,
                           reads=abf_res + wqk_res + ("BQR", "ones1_bf"),
                           writes=(f"pqk{pb}",), tag=f"p1.mm{t}.{c}")
                    p.task("scalar", lambda e, t=t, c=c, pb=pb: e.activation(
                        QK[:, t, c * 512:(c + 1) * 512], pqk[:, pb, :],
                        AF.Silu),
                        reads=(f"pqk{pb}",), writes=(f"QK{t}.{c}",),
                        tag=f"p1.silu{t}.{c}")
            qk_res = tuple(f"QK{t}.{c}" for t in range(NT) for c in range(4))
            self.dump_bf("dbg_qk", lambda i: QK[:, i, :], NT, qk_res)
            p.flush("p1")

    def attn_p2(self):
        nc, p, h = self.nc, self.p, self.h
        QK, GBC = self.QK, self.GBC
        gbc_res = tuple(f"GBC{r}.{c}" for r in range(4) for c in range(4))
        small = self.small
        with ExitStack() as S2:
            KF = S2.enter_context(nc.sbuf_tensor("KF", [128, NT, QKD], BF16))
            QT2 = S2.enter_context(nc.sbuf_tensor("QT2", [128, 2, NT, 256], BF16))
            EB = S2.enter_context(nc.sbuf_tensor("EB", [128, QKD], BF16))
            psim = S2.enter_context(nc.psum_tensor("psim", [128, 2, QKD], F32))

            for t in range(NT):
                p.task("vector", lambda e, t=t: e.tensor_tensor(
                    KF[:, t, :], QK[:, t, :], GBC[:, 2, :], op=ALU.mult),
                    reads=(f"QK{t}.0", f"QK{t}.1", f"QK{t}.2", f"QK{t}.3")
                    + gbc_res,
                    writes=(f"KF{t}",), tag=f"p2.km{t}")
                p.task("vector", lambda e, t=t: e.tensor_tensor(
                    KF[:, t, :], KF[:, t, :], GBC[:, 3, :], op=ALU.add),
                    reads=(f"KF{t}",), writes=(f"KF{t}",), tag=f"p2.ka{t}")
            kf_res = tuple(f"KF{t}" for t in range(NT))

            def emit_qprep(dp):
                qp = dp % 2
                for kt in range(NT):
                    p.task("vector", lambda e, dp=dp, kt=kt, qp=qp: e.tensor_tensor(
                        QT2[:, qp, kt, :], QK[:, kt, dp * 256:(dp + 1) * 256],
                        GBC[:, 0, dp * 256:(dp + 1) * 256], op=ALU.mult),
                        reads=(f"QK{kt}.{dp // 2}",) + gbc_res,
                        writes=(f"QT2{qp}.{kt}",), tag=f"p2.qm{dp}.{kt}")
                    p.task("vector", lambda e, dp=dp, kt=kt, qp=qp: e.tensor_tensor(
                        QT2[:, qp, kt, :], QT2[:, qp, kt, :],
                        GBC[:, 1, dp * 256:(dp + 1) * 256], op=ALU.add),
                        reads=(f"QT2{qp}.{kt}",), writes=(f"QT2{qp}.{kt}",),
                        tag=f"p2.qa{dp}.{kt}")

            emit_qprep(0)
            for dp in range(8):
                qp = dp % 2
                if dp + 1 < 8:
                    emit_qprep(dp + 1)
                qt_res = tuple(f"QT2{qp}.{kt}" for kt in range(NT))
                for dt in range(dp * 2, dp * 2 + 2):
                    pb = dt % 2
                    dl = (dt % 2) * 128
                    def simmm(e, dt=dt, pb=pb, dl=dl, qp=qp):
                        for c in range(4):
                            for kt in range(NT):
                                mm = e.matmul(
                                    psim[:, pb, c * 512:(c + 1) * 512],
                                    QT2[:, qp, kt, dl:dl + 128],
                                    KF[:, kt, c * 512:(c + 1) * 512],
                                    start=(kt == 0), stop=(kt == NT - 1))
                        return mm
                    p.task("tensor", simmm, reads=qt_res + kf_res,
                           writes=(f"psim{pb}",), tag=f"p2.sim{dt}")
                    sb = pb * 4
                    p.task("vector", lambda e, pb=pb, sb=sb: e.tensor_reduce(
                        small[:, sb:sb + 1], psim[:, pb, :], axis=AX.X, op=ALU.max),
                        reads=(f"psim{pb}",), writes=(f"p2max{pb}",),
                        tag=f"p2.max{dt}")
                    p.task("vector", lambda e, sb=sb: e.tensor_scalar(
                        small[:, sb + 1:sb + 2], small[:, sb:sb + 1], -1.0 / N,
                        None, op0=ALU.mult),
                        reads=(f"p2max{pb}",), writes=(f"p2bias{pb}",),
                        tag=f"p2.b{dt}")
                    p.task("scalar", lambda e, pb=pb, sb=sb: e.activation(
                        EB[:, :], psim[:, pb, :], AF.Exp,
                        bias=small[:, sb + 1:sb + 2], scale=1.0 / N,
                        accum_out=small[:, sb + 2:sb + 3]),
                        reads=(f"psim{pb}", f"p2bias{pb}"),
                        writes=("EB0", f"p2z{pb}"), tag=f"p2.exp{dt}")
                    p.task("vector", lambda e, sb=sb: e.reciprocal(
                        small[:, sb + 3:sb + 4], small[:, sb + 2:sb + 3]),
                        reads=(f"p2z{pb}",), writes=(f"p2rz{pb}",),
                        tag=f"p2.rz{dt}")
                    p.task("vector", lambda e, dt=dt, sb=sb: e.tensor_tensor(
                        self.rowsum[:, dt:dt + 1], small[:, sb + 2:sb + 3],
                        small[:, sb + 3:sb + 4], op=ALU.mult),
                        reads=(f"p2z{pb}", f"p2rz{pb}"),
                        writes=(f"rowsum{dt}",), tag=f"p2.rs{dt}")
            if "dbg_rowsum" in self.dbg_h:
                p.dma("sync", self.dbg_h["dbg_rowsum"][:], self.rowsum[:],
                      reads=tuple(f"rowsum{dt}" for dt in range(NT)),
                      writes=("dbg_rowsum.0",), tag="dbg.rowsum")
                self.out_res.append("dbg_rowsum.0")
            p.flush("p2")

    def attn_p34(self):
        nc, p, h = self.nc, self.p, self.h
        ABF = self.ABF
        abf_res = tuple(f"ABF.{t}" for t in range(NT))
        with ExitStack() as S3:
            VG = S3.enter_context(nc.sbuf_tensor("VG", [128, NT, N], BF16))
            with ExitStack() as S3a:
                WH = S3a.enter_context(
                    nc.sbuf_tensor("WH", [128, DT, 2 * HIDDEN], BF16))
                VT = S3a.enter_context(nc.sbuf_tensor("VT", [128, 2, 2048], BF16))
                GT = S3a.enter_context(nc.sbuf_tensor("GT", [128, 2, 2048], BF16))
                pv = S3a.enter_context(nc.psum_tensor("pv", [128, 2, 512], F32))
                pg = S3a.enter_context(nc.psum_tensor("pg", [128, 2, 512], F32))
                for k in range(DT):
                    p.dma("gpsimd", WH[:, k, :],
                          h["W_hidden"][k * 128:(k + 1) * 128, :],
                          writes=(f"WH{k}",), tag=f"p3.wh{k}")
                wh_res = tuple(f"WH{k}" for k in range(DT))
                for hp in range(NT):
                    pb = hp % 2
                    for c in range(4):
                        sc = c % 2
                        def vmm(e, hp=hp, c=c, sc=sc):
                            for k in range(DT):
                                mm = e.matmul(
                                    pv[:, sc, :],
                                    WH[:, k, hp * 128:(hp + 1) * 128],
                                    ABF[:, k, c * 512:(c + 1) * 512],
                                    start=(k == 0), stop=(k == DT - 1))
                            return mm
                        p.task("tensor", vmm, reads=abf_res + wh_res,
                               writes=(f"pv{sc}",), tag=f"p3.vmm{hp}.{c}")
                        def gmm(e, hp=hp, c=c, sc=sc):
                            for k in range(DT):
                                mm = e.matmul(
                                    pg[:, sc, :],
                                    WH[:, k, HIDDEN + hp * 128:
                                       HIDDEN + (hp + 1) * 128],
                                    ABF[:, k, c * 512:(c + 1) * 512],
                                    start=(k == 0), stop=(k == DT - 1))
                            return mm
                        p.task("tensor", gmm, reads=abf_res + wh_res,
                               writes=(f"pg{sc}",), tag=f"p3.gmm{hp}.{c}")
                        p.task("scalar", lambda e, hp=hp, c=c, sc=sc, pb=pb:
                               e.activation(
                                   VT[:, pb, c * 512:(c + 1) * 512], pv[:, sc, :],
                                   AF.Silu, bias=self.bh_sb[:, hp:hp + 1]),
                               reads=(f"pv{sc}", "bh_t"),
                               writes=(f"VT{pb}.{c}",), tag=f"p3.sv{hp}.{c}")
                        p.task("scalar", lambda e, hp=hp, c=c, sc=sc, pb=pb:
                               e.activation(
                                   GT[:, pb, c * 512:(c + 1) * 512], pg[:, sc, :],
                                   AF.Silu, bias=self.bh_sb[:, 16 + hp:17 + hp]),
                               reads=(f"pg{sc}", "bh_t"),
                               writes=(f"GT{pb}.{c}",), tag=f"p3.sg{hp}.{c}")
                    p.task("vector", lambda e, hp=hp, pb=pb: e.tensor_tensor(
                        VG[:, hp, :], VT[:, pb, :], GT[:, pb, :], op=ALU.mult),
                        reads=tuple(f"VT{pb}.{c}" for c in range(4))
                        + tuple(f"GT{pb}.{c}" for c in range(4)),
                        writes=(f"VGm{hp}",), tag=f"p3.vg{hp}")
                    p.task("vector", lambda e, hp=hp: e.tensor_scalar(
                        VG[:, hp, :], VG[:, hp, :],
                        self.rowsum[:, hp:hp + 1], None, op0=ALU.mult),
                        reads=(f"VGm{hp}", f"rowsum{hp}"),
                        writes=(f"VG{hp}",), tag=f"p3.vgr{hp}")
                vg_res = tuple(f"VG{hp}" for hp in range(NT))
                self.dump_bf("dbg_vg", lambda i: VG[:, i, :], NT, vg_res)
                p.flush("p3")
            if self.stop_after == "p3":
                return

            with ExitStack() as S4:
                WO = S4.enter_context(nc.sbuf_tensor("WO", [128, NT, D], BF16))
                XNB = S4.enter_context(nc.sbuf_tensor("XNB", [128, 2, N], F32))
                SB = S4.enter_context(nc.sbuf_tensor("SB", [128, 2, N], F32))
                ps = S4.enter_context(nc.psum_tensor("ps", [128, 2, 512], F32))
                for k in range(NT):
                    p.dma("gpsimd", WO[:, k, :],
                          h["W_out"][k * 128:(k + 1) * 128, :],
                          writes=(f"WO{k}",), tag=f"p4.wo{k}")
                wo_res = tuple(f"WO{k}" for k in range(NT))
                for mo in range(DT):
                    b = mo % 2
                    p.dma("sync", XNB[:, b, :], self.xn_scr[mo],
                          reads=tuple(f"xn_scr.{t}" for t in range(NT)),
                          writes=(f"XNB{b}",), tag=f"p4.xn{mo}")
                    for c in range(4):
                        pb = c % 2
                        def smm(e, mo=mo, c=c, pb=pb):
                            for k in range(NT):
                                mm = e.matmul(
                                    ps[:, pb, :],
                                    WO[:, k, mo * 128:(mo + 1) * 128],
                                    VG[:, k, c * 512:(c + 1) * 512],
                                    start=(k == 0), stop=(k == NT - 1))
                            return mm
                        p.task("tensor", smm, reads=vg_res + wo_res,
                               writes=(f"ps{pb}",), tag=f"p4.mm{mo}.{c}")
                        p.task("scalar", lambda e, c=c, pb=pb, b=b, mo=mo:
                               e.activation(
                                   SB[:, b, c * 512:(c + 1) * 512], ps[:, pb, :],
                                   AF.Identity, bias=self.bout_sb[:, mo:mo + 1]),
                               reads=(f"ps{pb}", "bout_t"), writes=(f"SB{b}",),
                               tag=f"p4.ev{mo}.{c}")
                    p.task("vector", lambda e, b=b: e.tensor_tensor(
                        SB[:, b, :], SB[:, b, :], XNB[:, b, :], op=ALU.add),
                        reads=(f"XNB{b}", f"SB{b}"),
                        writes=(f"SB{b}",), tag=f"p4.add{mo}")
                    p.dma("gpsimd", self.s_scr[mo], SB[:, b, :],
                          reads=(f"SB{b}",), writes=(f"s_scr{mo}",),
                          tag=f"p4.spill{mo}")
                    if "dbg_s" in self.dbg_h:
                        rid = f"dbg_s.{mo}"
                        p.dma("gpsimd", self.dbg_h["dbg_s"][mo], SB[:, b, :],
                              reads=(f"SB{b}",), writes=(rid,), tag=rid)
                        self.out_res.append(rid)
                p.flush("p4")

    def ffn_slot(self):
        nc, p, h = self.nc, self.p, self.h
        with ExitStack() as SF:
            XF = SF.enter_context(nc.sbuf_tensor("XF", [128, DT, N], F32))
            ROTT = SF.enter_context(nc.sbuf_tensor("ROTT", [128, 128], BF16))
            E2C = SF.enter_context(nc.sbuf_tensor("E2C", [128, 2], BF16))
            self.XF = XF
            self.ROTTsb, self.E2Csb = ROTT, E2C
            for k in range(DT):
                p.dma("sync", XF[:, k, :], self.xn_scr[k],
                      reads=tuple(f"xn_scr.{t}" for t in range(NT)),
                      writes=(f"XF{k}",), tag=f"pf.xf{k}")
            p.dma("sync", ROTT[:], h["rotT_bf"][:], writes=("ROTT",), tag="pf.rt")
            p.dma("sync", E2C[:], h["e2c_bf"][:], writes=("E2C",), tag="pf.e2c")
            for l in range(2):
                if self.ffn_layer(l):
                    return
            if self.stop_after is None or self.stop_after == "final":
                self.final_phase()

    def ffn_layer(self, l):
        nc, p, h = self.nc, self.p, self.h
        ABF, XF = self.ABF, self.XF
        sa = self.stop_after or ""
        xf_res = tuple(f"XF{k}" for k in range(DT))

        for k in range(DT):
            if k < 4:
                p.task("vector", lambda e, k=k: e.tensor_copy(
                    ABF[:, k, :], XF[:, k, :]),
                    reads=(f"XF{k}",), writes=(f"ABF.{l}{k}",), tag=f"f{l}.sh{k}")
            else:
                def shift(e, k=k):
                    e.memset(ABF[:, k, 0:1], 0.0)
                    return e.tensor_copy(ABF[:, k, 1:N], XF[:, k, 0:N - 1])
                p.task("vector", shift, reads=(f"XF{k}",),
                       writes=(f"ABF.{l}{k}",), tag=f"f{l}.sh{k}")
        abf_res = tuple(f"ABF.{l}{k}" for k in range(DT))

        with ExitStack() as SL:
            QP = SL.enter_context(nc.sbuf_tensor(f"QP{l}", [128, 4, N], BF16))
            KP = SL.enter_context(nc.sbuf_tensor(f"KP{l}", [128, 4, N], BF16))
            VTM = SL.enter_context(nc.sbuf_tensor(f"VTM{l}", [128, NT, INNER], BF16))

            with ExitStack() as S:
                ROT = S.enter_context(nc.sbuf_tensor(f"ROT{l}", [128, 2, N], BF16))
                WQKV = S.enter_context(
                    nc.sbuf_tensor(f"WQKV{l}", [128, DT, 512], BF16))
                QW = S.enter_context(nc.sbuf_tensor(f"QW{l}", [128, 2, N], BF16))
                SQ = S.enter_context(nc.sbuf_tensor(f"SQb{l}", [128, 2, 512], BF16))
                T1 = S.enter_context(nc.sbuf_tensor(f"T1{l}", [128, 1, 512], BF16))
                T2 = None
                VF = S.enter_context(nc.sbuf_tensor(f"VF{l}", [128, 2, N], BF16))
                ROWS = S.enter_context(nc.sbuf_tensor(f"ROWS{l}", [128, 2, N], F32))
                RSTDB = S.enter_context(nc.sbuf_tensor(f"RSTDB{l}", [2, 2, N], BF16))
                ESC = S.enter_context(nc.sbuf_tensor(f"ESC{l}", [2, 2, 128], BF16))
                QSL = S.enter_context(nc.sbuf_tensor(f"QSL{l}", [2, 2, 128], BF16))
                pqkv = S.enter_context(nc.psum_tensor(f"pqkv{l}", [128, 2, 512], F32))
                psq = S.enter_context(nc.psum_tensor(f"psq{l}", [2, 2, 512], F32))
                pex = S.enter_context(nc.psum_tensor(f"pex{l}", [128, 1, 512], F32))
                pR = S.enter_context(nc.psum_tensor(f"pR{l}", [128, 1, 512], F32))
                ptv = S.enter_context(nc.psum_tensor(f"ptv{l}", [128, 2, 1024], BF16))
                self.ROTsb = ROT

                for r in range(2):
                    p.dma("sync", ROT[:, r, :], h["rotary"][r],
                          writes=(f"ROT{r}",), tag=f"f{l}.rotq{r}")
                p.dma("gpsimd", QSL[:, 0, :], h["qs_tile"][l], writes=("QSL0",),
                      tag=f"f{l}.qs")
                p.dma("gpsimd", QSL[:, 1, :], h["ks_tile"][l], writes=("QSL1",),
                      tag=f"f{l}.ks")
                for r in range(2):
                    p.task("vector", lambda e, r=r: e.tensor_tensor(
                        ESC[:, r, :], self.e2t_sb[:], QSL[:, r, :], op=ALU.mult),
                        reads=(f"QSL{r}", "e2t"), writes=(f"ESC{r}",),
                        tag=f"f{l}.esc{r}")

                def post_work(m):
                    pb = m % 2
                    if m < 8:
                        self._qk_chain(l, m, pb, QP if m < 4 else KP,
                                       QW, SQ, T1, T2, ROWS, RSTDB, ESC,
                                       psq, pex, pR)
                        return
                    vm = m - 8
                    vf_res = tuple(f"VF{pb}.{c}" for c in range(4))
                    for jg in range(4):
                        sg = jg % 2
                        def vtp(e, vm=vm, jg=jg, sg=sg, pb=pb):
                            for jj in range(4):
                                jc = jg * 4 + jj
                                mm = e.transpose(
                                    ptv[:, sg, jj * 128:(jj + 1) * 128],
                                    VF[:, pb, jc * 128:(jc + 1) * 128],
                                    self.ident_bf[:])
                            return mm
                        p.task("tensor", vtp,
                               reads=vf_res + ("identity_bf",),
                               writes=(f"ptv{sg}",),
                               tag=f"f{l}.vtp{vm}.{jg}")
                        def vev(e, vm=vm, jg=jg, sg=sg):
                            return e.tensor_copy(
                                VTM[:, jg * 4:(jg + 1) * 4,
                                    vm * 128:(vm + 1) * 128],
                                ptv[:, sg, 0:512].rearrange(
                                    "p (jj c) -> p jj c", jj=4))
                        p.task("vector", vev, reads=(f"ptv{sg}",),
                               writes=(f"VTM{vm}.{jg}",),
                               tag=f"f{l}.vevt{vm}.{jg}")

                pending = None
                for stage in range(3):
                    for k in range(DT):
                        p.dma("gpsimd", WQKV[:, k, :],
                              h["wqkv"][l][k * 128:(k + 1) * 128,
                                           stage * 512:(stage + 1) * 512],
                              writes=(f"WQKV{k}",), tag=f"f{l}.wqkv{stage}.{k}")
                    wqkv_res = tuple(f"WQKV{k}" for k in range(DT))
                    for ms in range(4):
                        m = stage * 4 + ms
                        pb = m % 2
                        for c in range(4):
                            sc = c % 2
                            def qkvmm(e, ms=ms, c=c, sc=sc):
                                for k in range(DT):
                                    mm = e.matmul(
                                        pqkv[:, sc, :],
                                        WQKV[:, k, ms * 128:(ms + 1) * 128],
                                        ABF[:, k, c * 512:(c + 1) * 512],
                                        start=(k == 0), stop=(k == DT - 1))
                                return mm
                            p.task("tensor", qkvmm, reads=abf_res + wqkv_res,
                                   writes=(f"pqkv{sc}",), tag=f"f{l}.qkv{m}.{c}")
                            if m < 8:
                                p.task("scalar", lambda e, c=c, sc=sc, pb=pb:
                                       e.activation(
                                           QW[:, pb, c * 512:(c + 1) * 512],
                                           pqkv[:, sc, :], AF.Identity),
                                       reads=(f"pqkv{sc}",),
                                       writes=(f"QW{pb}.{c}",),
                                       tag=f"f{l}.qev{m}.{c}")
                            else:
                                p.task("vector", lambda e, c=c, sc=sc, pb=pb:
                                       e.tensor_copy(
                                           VF[:, pb, c * 512:(c + 1) * 512],
                                           pqkv[:, sc, :]),
                                       reads=(f"pqkv{sc}",), writes=(f"VF{pb}.{c}",),
                                       tag=f"f{l}.vev{m}.{c}")
                        if pending is not None:
                            post_work(pending)
                            if pending == 3:
                                for r in range(2):
                                    p.dma("sync", ROT[:, r, :], h["rotary"][2 + r],
                                          writes=(f"ROT{r}",), tag=f"f{l}.rotk{r}")
                        pending = m
                post_work(pending)
                qp_res = tuple(f"QPf{m}" for m in range(4))
                kp_res = tuple(f"KPf{m}" for m in range(4))
                vtm_res = tuple(f"VTM{vm}.{jg}" for vm in range(4) for jg in range(4))
                if l == 0:
                    self.dump_bf("dbg_qp", lambda i: QP[:, i, :], 4, qp_res)
                    self.dump_bf("dbg_kp", lambda i: KP[:, i, :], 4, kp_res)
                    self.dump_bf("dbg_vt", lambda i: VTM[:, i, :], NT, vtm_res)
                p.flush(f"f{l}.qkv")
            if sa == f"f{l}qkv":
                return True

            oa_res = tuple(f"OA{hp}.{w}" for hp in range(4) for w in range(NW))
            with ExitStack() as S34:
                OA = S34.enter_context(nc.sbuf_tensor(f"OA{l}", [128, 4, N], BF16))
                WOF = S34.enter_context(
                    nc.sbuf_tensor(f"WOF{l}", [128, 4, D], BF16))
                for k in range(4):
                    p.dma("gpsimd", WOF[:, k, :],
                          h["wo"][l][k * 128:(k + 1) * 128, :],
                          writes=(f"WOF{k}",), tag=f"f{l}.wo{k}")
                with ExitStack() as S3:
                    PB = S3.enter_context(
                        nc.sbuf_tensor(f"PB{l}", [128, 2, 2, 1024], BF16))
                    PT = S3.enter_context(
                        nc.sbuf_tensor(f"PT{l}", [128, 2, 8, 512], BF16))
                    MSK = S3.enter_context(
                        nc.sbuf_tensor(f"MSK{l}", [128, 4, 1024], BF16))
                    self.MSKsb = MSK
                    for r in range(4):
                        p.dma("sync", MSK[:, r, :], h["mask1"][r],
                              writes=(f"MSK{r}",), tag=f"f{l}.msk{r}")
                    pS = S3.enter_context(
                        nc.psum_tensor(f"pS{l}", [128, 2, 1024], F32))
                    pPT = S3.enter_context(
                        nc.psum_tensor(f"pPT{l}", [128, 2, 1024], BF16))
                    pO = S3.enter_context(
                        nc.psum_tensor(f"pO{l}", [128, 2, 512], F32))
                    if "dbg_s2" in self.dbg_h:
                        self.SDUMP = S3.enter_context(
                            nc.sbuf_tensor(f"SDUMP{l}", [128, 1024], F32))
                    self._local_attn(l, QP, KP, VTM, OA, PB, PT, pS, pPT, pO)
                    if l == 0:
                        self.dump_bf("dbg_oall", lambda i: OA[:, i, :], 4, oa_res)
                    p.flush(f"f{l}.la")
                if sa == f"f{l}la":
                    return True

                with ExitStack() as S3b:
                    pwo = S3b.enter_context(
                        nc.psum_tensor(f"pwo{l}", [128, 2, 512], F32))
                    wof_res = tuple(f"WOF{k}" for k in range(4))
                    for mo in range(DT):
                        for c in range(4):
                            pb = (mo * 4 + c) % 2
                            def womm(e, mo=mo, c=c, pb=pb):
                                for k in range(4):
                                    mm = e.matmul(
                                        pwo[:, pb, :],
                                        WOF[:, k, mo * 128:(mo + 1) * 128],
                                        OA[:, k, c * 512:(c + 1) * 512],
                                        start=(k == 0), stop=(k == 3))
                                return mm
                            p.task("tensor", womm, reads=oa_res + wof_res,
                                   writes=(f"pwo{pb}",), tag=f"f{l}.womm{mo}.{c}")
                            p.task("vector", lambda e, mo=mo, c=c, pb=pb:
                                   e.tensor_tensor(
                                       XF[:, mo, c * 512:(c + 1) * 512],
                                       XF[:, mo, c * 512:(c + 1) * 512],
                                       pwo[:, pb, :], op=ALU.add),
                                   reads=(f"pwo{pb}", f"XF{mo}"),
                                   writes=(f"XF{mo}",),
                                   tag=f"f{l}.woadd{mo}.{c}")
                    if l == 0:
                        self.dump_f32("dbg_xf", lambda i: XF[:, i, :], DT, xf_res)
                    p.flush(f"f{l}.attn")
            if sa == f"f{l}attn":
                return True

        with ExitStack() as S5:
            N2 = S5.enter_context(nc.sbuf_tensor(f"N2{l}", [128, DT, N], BF16))
            self.N2 = N2
            with ExitStack() as S5a:
                TDIF = S5a.enter_context(
                    nc.sbuf_tensor(f"TDIF{l}", [128, DT, N], BF16))
                MUB = S5a.enter_context(nc.sbuf_tensor(f"MUB{l}", [128, 2, N], BF16))
                SQS = S5a.enter_context(nc.sbuf_tensor(f"SQS{l}", [128, 2, 512], BF16))
                MRB = S5a.enter_context(nc.sbuf_tensor(f"MRB{l}", [1, 2, N], BF16))
                LNF = S5a.enter_context(nc.sbuf_tensor(f"LNF{l}", [1, N], F32))
                T2B = S5a.enter_context(nc.sbuf_tensor(f"T2B{l}", [128, 2, N], BF16))
                FLG = S5a.enter_context(nc.sbuf_tensor(f"FLG{l}", [128, 8], F32))
                FLB = S5a.enter_context(nc.sbuf_tensor(f"FLB{l}", [128, 8], F32))
                pst = S5a.enter_context(nc.psum_tensor(f"pst{l}", [1, 2, 512], F32))
                pex2 = S5a.enter_context(nc.psum_tensor(f"pex2{l}", [128, 2, 512], F32))
                p.dma("sync", FLG[:], h["flg_t"][l], writes=("FLG",), tag=f"f{l}.flg")
                p.dma("sync", FLB[:], h["flb_t"][l], writes=("FLB",), tag=f"f{l}.flb")
                for k in range(DT):
                    if k < 4:
                        p.task("vector", lambda e, k=k: e.tensor_copy(
                            ABF[:, k, :], XF[:, k, :]),
                            reads=(f"XF{k}",), writes=(f"ABF.{l}{k}",),
                            tag=f"f{l}.sh2{k}")
                    else:
                        def shift2(e, k=k):
                            e.memset(ABF[:, k, 0:1], 0.0)
                            return e.tensor_copy(ABF[:, k, 1:N], XF[:, k, 0:N - 1])
                        p.task("vector", shift2, reads=(f"XF{k}",),
                               writes=(f"ABF.{l}{k}",), tag=f"f{l}.sh2{k}")
                abf_res2 = tuple(f"ABF.{l}{k}" for k in range(DT))
                for c in range(4):
                    sc = c % 2
                    def sxmm(e, c=c, sc=sc):
                        for k in range(DT):
                            mm = e.matmul(pst[:, sc, :], self.ones_col[:],
                                          ABF[:, k, c * 512:(c + 1) * 512],
                                          start=(k == 0), stop=(k == DT - 1))
                        return mm
                    p.task("tensor", sxmm, reads=abf_res2 + ("ones_col_bf",),
                           writes=(f"pst{sc}",), tag=f"f{l}.sx{c}")
                    p.task("scalar", lambda e, c=c, sc=sc: e.activation(
                        MRB[:, 0, c * 512:(c + 1) * 512], pst[:, sc, :],
                        AF.Identity, scale=1.0 / D),
                        reads=(f"pst{sc}",), writes=(f"MRBmu.{c}",),
                        tag=f"f{l}.sxe{c}")
                mu_res = tuple(f"MRBmu.{c}" for c in range(4))
                for c in range(4):
                    pb = c % 2
                    p.task("tensor", lambda e, c=c, pb=pb: e.matmul(
                        pex2[:, pb, :], self.ones1[:],
                        MRB[:, 0, c * 512:(c + 1) * 512], start=True, stop=True),
                        reads=mu_res + ("ones1_bf",), writes=(f"pex2{pb}",),
                        tag=f"f{l}.mex{c}")
                    p.task("vector", lambda e, c=c, pb=pb: e.tensor_copy(
                        MUB[:, 0, c * 512:(c + 1) * 512], pex2[:, pb, :]),
                        reads=(f"pex2{pb}",), writes=(f"MUB0.{c}",),
                        tag=f"f{l}.mexe{c}")
                mub0_res = tuple(f"MUB0.{c}" for c in range(4))
                for k in range(DT):
                    p.task("vector", lambda e, k=k: e.tensor_tensor(
                        TDIF[:, k, :], ABF[:, k, :], MUB[:, 0, :], op=ALU.subtract),
                        reads=(f"ABF.{l}{k}",) + mub0_res, writes=(f"TDIF{k}",),
                        tag=f"f{l}.td{k}")
                tdif_res = tuple(f"TDIF{k}" for k in range(DT))
                for c in range(4):
                    sc = c % 2
                    for k in range(DT):
                        p.task("vector", lambda e, k=k, c=c: e.tensor_tensor(
                            SQS[:, k % 2, :], TDIF[:, k, c * 512:(c + 1) * 512],
                            TDIF[:, k, c * 512:(c + 1) * 512], op=ALU.mult),
                            reads=(f"TDIF{k}",), writes=(f"SQS{k % 2}",),
                            tag=f"f{l}.sq{c}.{k}")
                        p.task("tensor", lambda e, k=k, c=c, sc=sc: e.matmul(
                            pst[:, sc, :], self.ones_col[:], SQS[:, k % 2, :],
                            start=(k == 0), stop=(k == DT - 1)),
                            reads=(f"SQS{k % 2}", "ones_col_bf"),
                            writes=(f"pst{sc}",),
                            tag=f"f{l}.vmm{c}.{k}")
                    p.task("scalar", lambda e, c=c, sc=sc: e.activation(
                        LNF[:, c * 512:(c + 1) * 512], pst[:, sc, :], AF.Ln,
                        scale=1.0 / D, bias=self.epsv[0:1, :]),
                        reads=(f"pst{sc}", "epsv"), writes=(f"LNF.{c}",),
                        tag=f"f{l}.vln{c}")
                p.task("scalar", lambda e: e.activation(
                    MRB[:, 1, :], LNF[:], AF.Exp, scale=-0.5),
                    reads=tuple(f"LNF.{c}" for c in range(4)),
                    writes=("MRBrs",), tag=f"f{l}.vex")
                for c in range(4):
                    pb = c % 2
                    p.task("tensor", lambda e, c=c, pb=pb: e.matmul(
                        pex2[:, pb, :], self.ones1[:],
                        MRB[:, 1, c * 512:(c + 1) * 512], start=True, stop=True),
                        reads=("MRBrs", "ones1_bf"), writes=(f"pex2{pb}",),
                        tag=f"f{l}.rex{c}")
                    p.task("vector", lambda e, c=c, pb=pb: e.tensor_copy(
                        MUB[:, 1, c * 512:(c + 1) * 512], pex2[:, pb, :]),
                        reads=(f"pex2{pb}",), writes=(f"MUB1.{c}",),
                        tag=f"f{l}.rexe{c}")
                mub1_res = tuple(f"MUB1.{c}" for c in range(4))
                for k in range(DT):
                    pb = k % 2
                    p.task("vector", lambda e, k=k, pb=pb: e.tensor_tensor(
                        T2B[:, pb, :], TDIF[:, k, :], MUB[:, 1, :], op=ALU.mult),
                        reads=(f"TDIF{k}",) + mub1_res, writes=(f"T2B{pb}",),
                        tag=f"f{l}.n2m{k}")
                    p.task("scalar", lambda e, k=k, pb=pb: e.activation(
                        N2[:, k, :], T2B[:, pb, :], AF.Identity,
                        bias=FLB[:, k:k + 1], scale=FLG[:, k:k + 1]),
                        reads=(f"T2B{pb}", "FLG", "FLB"), writes=(f"N2.{k}",),
                        tag=f"f{l}.n2f{k}")
                n2_res = tuple(f"N2.{k}" for k in range(DT))
                if l == 0:
                    self.dump_bf("dbg_n2", lambda i: N2[:, i, :], DT, n2_res)
                p.flush(f"f{l}.ln")
            if sa == f"f{l}ln":
                return True

            with ExitStack() as S6:
                W1B = S6.enter_context(
                    nc.sbuf_tensor(f"W1B{l}", [128, 2, DT, 256], BF16))
                W2B = S6.enter_context(
                    nc.sbuf_tensor(f"W2B{l}", [128, 2, KH, 128], BF16))
                PROD = S6.enter_context(
                    nc.sbuf_tensor(f"PROD{l}", [128, KH, 1024], BF16))
                GEB = S6.enter_context(nc.sbuf_tensor(f"GEB{l}", [128, 2, 1024], BF16))
                B1AR = S6.enter_context(nc.sbuf_tensor(f"B1AR{l}", [1, HIDP], BF16))
                B1G = S6.enter_context(nc.sbuf_tensor(f"B1G{l}", [128, KH], F32))
                B2R = S6.enter_context(nc.sbuf_tensor(f"B2R{l}", [1, D], BF16))
                self.b1g_sb = B1G
                p.dma("gpsimd", B1AR[:], h["b1a_row"][l], writes=("B1AR",),
                      tag=f"f{l}.b1ar")
                p.dma("sync", B1G[:], h["b1g_t"][l], writes=("B1G",),
                      tag=f"f{l}.b1g")
                p.dma("gpsimd", B2R[:], h["b2_row"][l], writes=("B2R",),
                      tag=f"f{l}.b2r")
                prod_res = tuple(f"PROD.{m}" for m in range(KH))
                for nh in range(2):
                    nh0 = nh * 1024
                    with ExitStack() as S6a:
                        pa = S6a.enter_context(
                            nc.psum_tensor(f"pa{l}{nh}", [128, 2, 1024], F32))
                        pg2 = S6a.enter_context(
                            nc.psum_tensor(f"pg2{l}{nh}", [128, 2, 1024], F32))
                        def w1dma(m):
                            p.dma("gpsimd", W1B[:, m % 2, :, :],
                                  h["w1ag"][l][m].rearrange("(k p) c -> p k c",
                                                            p=128),
                                  writes=(f"W1B{m % 2}",), tag=f"f{l}.w1d{nh}.{m}")
                        w1dma(0)
                        w1dma(1)
                        for m in range(KH):
                            self._ffn_mm(l, nh, m, m % 2, pa, pg2, W1B, B1AR,
                                         GEB, PROD)
                            if m + 2 < KH:
                                w1dma(m + 2)
                        p.flush(f"f{l}.w1.{nh}")
                    with ExitStack() as S6b:
                        px = S6b.enter_context(
                            nc.psum_tensor(f"px{l}{nh}", [128, 2, 512], F32))
                        def w2dma(mo):
                            p.dma("gpsimd", W2B[:, mo % 2, :, :],
                                  h["w2cb"][l][mo].rearrange("(k p) c -> p k c",
                                                             p=128),
                                  writes=(f"W2B{mo % 2}",), tag=f"f{l}.w2d{nh}.{mo}")
                        w2dma(0)
                        w2dma(1)
                        for mo in range(DT):
                            s2 = mo % 2
                            for cc in range(2):
                                pb = cc
                                def w2mm(e, mo=mo, s2=s2, cc=cc, pb=pb):
                                    for k in range(KH):
                                        e.matmul(px[:, pb, :], W2B[:, s2, k, :],
                                                 PROD[:, k, cc * 512:(cc + 1) * 512],
                                                 start=(k == 0), stop=False)
                                    return e.matmul(
                                        px[:, pb, :],
                                        B2R[:, mo * 128:(mo + 1) * 128],
                                        self.ones512[:], start=False, stop=True)
                                p.task("tensor", w2mm,
                                       reads=prod_res + (f"W2B{s2}", "B2R",
                                                         "ones512_bf"),
                                       writes=(f"px{pb}",),
                                       tag=f"f{l}.w2mm{nh}.{mo}.{cc}")
                                p.task("vector", lambda e, mo=mo, cc=cc, nh0=nh0,
                                       pb=pb: e.tensor_tensor(
                                    XF[:, mo, nh0 + cc * 512:nh0 + (cc + 1) * 512],
                                    XF[:, mo, nh0 + cc * 512:nh0 + (cc + 1) * 512],
                                    px[:, pb, :], op=ALU.add),
                                    reads=(f"px{pb}", f"XF{mo}"),
                                    writes=(f"XF{mo}",),
                                    tag=f"f{l}.xadd{nh}.{mo}.{cc}")
                            if mo + 2 < DT:
                                w2dma(mo + 2)
                        p.flush(f"f{l}.w2.{nh}")
                if l == 0 and "dbg_xf2" in self.dbg_h:
                    self.dump_f32("dbg_xf2", lambda i: XF[:, i, :], DT, xf_res)
                p.flush(f"f{l}.ffn")
            if sa == f"f{l}ffn":
                return True
        return False

    def _ffn_mm(self, l, nh, m, sl, pa, pg2, W1B, B1AR, GEB, PROD):
        p = self.p
        nh0 = nh * 1024
        mp = m % 2
        n2_res = tuple(f"N2.{k}" for k in range(DT))
        for cc in range(2):
            def amm(e, m=m, sl=sl, cc=cc, nh0=nh0, mp=mp):
                for k in range(DT):
                    e.matmul(pa[:, mp, cc * 512:(cc + 1) * 512],
                             W1B[:, sl, k, 0:128],
                             self.N2[:, k, nh0 + cc * 512:nh0 + (cc + 1) * 512],
                             start=(k == 0), stop=False)
                return e.matmul(pa[:, mp, cc * 512:(cc + 1) * 512],
                                B1AR[:, m * 128:(m + 1) * 128],
                                self.ones512[:], start=False, stop=True)
            p.task("tensor", amm,
                   reads=n2_res + (f"W1B{sl}", "B1AR", "ones512_bf"),
                   writes=(f"pa{mp}",), tag=f"f{l}.amm{nh}.{m}.{cc}")
            def gmm(e, m=m, sl=sl, cc=cc, nh0=nh0, mp=mp):
                for k in range(DT):
                    mm = e.matmul(pg2[:, mp, cc * 512:(cc + 1) * 512],
                                  W1B[:, sl, k, 128:256],
                                  self.N2[:, k, nh0 + cc * 512:nh0 + (cc + 1) * 512],
                                  start=(k == 0), stop=(k == DT - 1))
                return mm
            p.task("tensor", gmm, reads=n2_res + (f"W1B{sl}",),
                   writes=(f"pg2{mp}",), tag=f"f{l}.gmm{nh}.{m}.{cc}")
        p.task("scalar", lambda e, m=m, mp=mp: e.activation(
            GEB[:, mp, :], pg2[:, mp, :],
            AF.Gelu, bias=self.b1g_sb[:, m:m + 1]),
            reads=(f"pg2{mp}", "B1G"), writes=(f"GEB{mp}", f"pg2{mp}"),
            tag=f"f{l}.gelu{nh}.{m}")
        p.task("vector", lambda e, m=m, mp=mp: e.tensor_tensor(
            PROD[:, m, :], pa[:, mp, :],
            GEB[:, mp, :], op=ALU.mult),
            reads=(f"GEB{mp}", f"pa{mp}"),
            writes=(f"PROD.{m}", f"pa{mp}"), tag=f"f{l}.prod{nh}.{m}")

    def _qk_chain(self, l, m, pb, DSTP, QW, SQ, T1, T2, ROWS, RSTDB, ESC,
                  psq, pex, pR):
        p = self.p
        is_q = m < 4
        mt = m if is_q else m - 4
        er = 0 if is_q else 1
        qw_res = tuple(f"QW{pb}.{c}" for c in range(4))
        dst_res = ("QPf" if is_q else "KPf") + str(mt)
        for c in range(4):
            p.task("vector", lambda e, c=c, pb=pb: e.tensor_tensor(
                SQ[:, pb, :], QW[:, pb, c * 512:(c + 1) * 512],
                QW[:, pb, c * 512:(c + 1) * 512], op=ALU.mult),
                reads=(f"QW{pb}.{c}",), writes=(f"SQ{pb}",), tag=f"f{l}.sqr{m}.{c}")
            p.task("tensor", lambda e, c=c, pb=pb: e.matmul(
                psq[:, pb, :], self.E2Csb[:], SQ[:, pb, :],
                start=True, stop=True),
                reads=(f"SQ{pb}", "E2C"), writes=(f"psq{pb}",),
                tag=f"f{l}.ssq{m}.{c}")
            p.task("scalar", lambda e, c=c, pb=pb: e.activation(
                ROWS[0:2, pb, c * 512:(c + 1) * 512], psq[:, pb, :], AF.Ln),
                reads=(f"psq{pb}",), writes=(f"LNR{pb}.{c}",),
                tag=f"f{l}.l2ln{m}.{c}")
        lnr_res = tuple(f"LNR{pb}.{c}" for c in range(4))
        p.task("scalar", lambda e, pb=pb: e.activation(
            ROWS[0:2, pb, :], ROWS[0:2, pb, :], AF.Exp, scale=-0.5),
            reads=lnr_res, writes=(f"LNE{pb}",), tag=f"f{l}.l2ex{m}")
        p.task("vector", lambda e, pb=pb: e.tensor_scalar(
            RSTDB[:, pb, :], ROWS[0:2, pb, :], 1e12, None, op0=ALU.min),
            reads=(f"LNE{pb}",), writes=(f"RSTDB{pb}",), tag=f"f{l}.clip{m}")
        for c in range(4):
            p.task("tensor", lambda e, c=c, pb=pb: e.matmul(
                pex[:, 0, :], ESC[:, er, :], RSTDB[:, pb, c * 512:(c + 1) * 512],
                start=True, stop=True),
                reads=(f"RSTDB{pb}", f"ESC{er}"), writes=("pex0",),
                tag=f"f{l}.exp{m}.{c}")
            p.task("vector", lambda e, c=c: e.tensor_tensor(
                QW[:, pb, c * 512:(c + 1) * 512],
                QW[:, pb, c * 512:(c + 1) * 512], pex[:, 0, :], op=ALU.mult),
                reads=("pex0",) + qw_res, writes=(f"QW{pb}.{c}",),
                tag=f"f{l}.qn{m}.{c}")
        for c in range(4):
            p.task("tensor", lambda e, c=c: e.matmul(
                pR[:, 0, :], self.ROTTsb[:],
                QW[:, pb, c * 512:(c + 1) * 512], start=True, stop=True),
                reads=(f"QW{pb}.{c}", "ROTT"), writes=("pR0",),
                tag=f"f{l}.rot{m}.{c}")
            p.task("vector", lambda e, c=c, mt=mt: e.tensor_tensor(
                DSTP[:, mt, c * 512:(c + 1) * 512], pR[:, 0, :],
                self.ROTsb[:, 1, c * 512:(c + 1) * 512], op=ALU.mult),
                reads=("pR0", "ROT1"), writes=(dst_res,),
                tag=f"f{l}.t2{m}.{c}")
            p.task("vector", lambda e, c=c: e.tensor_tensor(
                T1[:, 0, :], QW[:, pb, c * 512:(c + 1) * 512],
                self.ROTsb[:, 0, c * 512:(c + 1) * 512], op=ALU.mult),
                reads=(f"QW{pb}.{c}", "ROT0"), writes=("T10",),
                tag=f"f{l}.t1{m}.{c}")
            p.task("vector", lambda e, c=c, mt=mt: e.tensor_tensor(
                DSTP[:, mt, c * 512:(c + 1) * 512],
                DSTP[:, mt, c * 512:(c + 1) * 512], T1[:, 0, :],
                op=ALU.add),
                reads=("T10", dst_res), writes=(dst_res,),
                tag=f"f{l}.rsum{m}.{c}")

    def _local_attn(self, l, QP, KP, VTM, OA, PB, PT, pS, pPT, pO):
        p = self.p
        small = self.small
        qp_res = tuple(f"QPf{m}" for m in range(4))
        kp_res = tuple(f"KPf{m}" for m in range(4))
        vtm_res = tuple(f"VTM{vm}.{jg}" for vm in range(4) for jg in range(4))

        def band(w, it):
            if w == 0:
                return 0, (it + 1) * 128
            return it * 128, 640

        def s_task(hp, w, it, u, jt0, iv):
            jlo, W = band(w, it)
            moff = 512 if w == 0 else 0
            chunks = []
            pos = jlo
            while pos < jlo + W:
                end = min(jlo + W, (pos // 512 + 1) * 512)
                chunks.append((pos, end - pos))
                pos = end
            if w == 0:
                mblocks = [(it * 128, 128)]
            else:
                mblocks = [(jlo, 128), (jlo + 512, 128)]
            def smm(e, hp=hp, w=w, iv=iv, jt0=jt0, u=u, chunks=chunks,
                    moff=moff, mblocks=mblocks):
                for (lo, n) in chunks:
                    e.matmul(
                        pS[:, u, lo:lo + n],
                        QP[u * 64:(u + 1) * 64, hp, iv:iv + 128],
                        KP[u * 64:(u + 1) * 64, hp,
                           jt0 * 128 + lo:jt0 * 128 + lo + n],
                        start=True, stop=False)
                for (lo, n) in mblocks:
                    mm = e.matmul(
                        pS[:, u, lo:lo + n],
                        self.ident_bf[:],
                        self.MSKsb[:, it, moff + lo:moff + lo + n],
                        start=False, stop=True)
                return mm
            p.task("tensor", smm,
                   reads=qp_res + kp_res + ("identity_bf", f"MSK{it}"),
                   writes=(f"pS{u}",), tag=f"f{l}.S{hp}.{w}.{it}.{u}")

        def softmax(hp, w, it):
            jlo, W = band(w, it)
            for u in range(2):
                sc = 8 + u * 8 + (it % 2) * 4
                p.task("vector", lambda e, u=u, sc=sc, jlo=jlo, W=W: e.tensor_reduce(
                    small[:, sc:sc + 1], pS[:, u, jlo:jlo + W], axis=AX.X,
                    op=ALU.max, negate=True),
                    reads=(f"pS{u}",), writes=(f"negm{u}.{it % 2}",),
                    tag=f"f{l}.mx{hp}.{w}.{it}.{u}")
            for u in range(2):
                sc = 8 + u * 8 + (it % 2) * 4
                p.task("scalar", lambda e, u=u, sc=sc, it=it, jlo=jlo, W=W:
                       e.activation(
                           PB[:, it % 2, u, jlo:jlo + W], pS[:, u, jlo:jlo + W],
                           AF.Exp, bias=small[:, sc:sc + 1],
                           accum_out=small[:, sc + 1:sc + 2]),
                       reads=(f"pS{u}", f"negm{u}.{it % 2}"),
                       writes=(f"PB{u}.{it % 2}", f"z{u}.{it % 2}"),
                       tag=f"f{l}.ex{hp}.{w}.{it}.{u}")
            for u in range(2):
                sc = 8 + u * 8 + (it % 2) * 4
                p.task("vector", lambda e, sc=sc: e.reciprocal(
                    small[:, sc + 2:sc + 3], small[:, sc + 1:sc + 2]),
                    reads=(f"z{u}.{it % 2}",), writes=(f"rz{u}.{it % 2}",),
                    tag=f"f{l}.rz{hp}.{w}.{it}.{u}")
            for u in range(2):
                sc = 8 + u * 8 + (it % 2) * 4
                p.task("vector", lambda e, u=u, sc=sc, it=it, jlo=jlo, W=W:
                       e.tensor_scalar(
                           PB[:, it % 2, u, jlo:jlo + W],
                           PB[:, it % 2, u, jlo:jlo + W],
                           small[:, sc + 2:sc + 3], None, op0=ALU.mult),
                       reads=(f"PB{u}.{it % 2}", f"rz{u}.{it % 2}"),
                       writes=(f"PB{u}.{it % 2}",),
                       tag=f"f{l}.nm{hp}.{w}.{it}.{u}")
                if l == 0 and hp == 0 and w == 1 and "dbg_pb" in self.dbg_h:
                    rid = f"dbg_pb.{it * 2 + u}"
                    p.dma("gpsimd", self.dbg_h["dbg_pb"][it * 2 + u],
                          PB[:, it % 2, u, :], reads=(f"PB{u}.{it % 2}",),
                          writes=(rid,), tag=rid)
                    self.out_res.append(rid)

        def emit_tp(hp, w, it):
            jlo, W = band(w, it)
            jb0 = jlo // 128
            nb = W // 128
            for u in range(2):
                def ptp(e, u=u, it=it, jb0=jb0, nb=nb):
                    for jj in range(nb):
                        jb = jb0 + jj
                        mm = e.transpose(
                            pPT[:, u, jj * 128:(jj + 1) * 128],
                            PB[:, it % 2, u, jb * 128:(jb + 1) * 128],
                            self.ident_bf[:])
                    return mm
                p.task("tensor", ptp, reads=(f"PB{u}.{it % 2}", "identity_bf"),
                       writes=(f"pPT{u}",), tag=f"f{l}.tp{hp}.{w}.{it}.{u}")
                def pev(e, u=u, it=it, jb0=jb0, nb=nb):
                    return e.activation(
                        PT[:, u, jb0:jb0 + nb, it * 128:(it + 1) * 128],
                        pPT[:, u, 0:nb * 128].rearrange("p (a b) -> p a b", b=128),
                        AF.Identity)
                p.task("scalar", pev, reads=(f"pPT{u}",),
                       writes=(f"PTc{u}.{it}",),
                       tag=f"f{l}.pe{hp}.{w}.{it}.{u}")

        def icols(w, jt, njt):
            if w == 0:
                return jt * 128, 512 - jt * 128
            lo = max(0, jt - 4) * 128
            hi = min(4, jt + 1) * 128
            return lo, hi - lo

        pt_res = tuple(f"PTc{u}.{it}" for u in range(2) for it in range(4))

        def emit_omm(hp, w, jt0, njt):
            pbo = w % 2
            jt_order = list(range(njt)) if w == 0 else [3, 4, 2, 5, 1, 6, 0, 7]
            def omm(e, hp=hp, w=w, jt0=jt0, pbo=pbo, jt_order=jt_order):
                for u in range(2):
                    hh = hp * 2 + u
                    first = True
                    for jt in jt_order:
                        ilo, n = icols(w, jt, 0)
                        mm = e.matmul(
                            pO[u * 64:(u + 1) * 64, pbo, ilo:ilo + n],
                            VTM[:, jt0 + jt, hh * 64:(hh + 1) * 64],
                            PT[:, u, jt, ilo:ilo + n],
                            start=first, stop=(jt == jt_order[-1]),
                            tile_position=(0, u * 64))
                        first = False
                return mm
            p.task("tensor", omm, reads=pt_res + vtm_res,
                   writes=(f"pO{pbo}",), tag=f"f{l}.O{hp}.{w}")
            p.task("scalar", lambda e, hp=hp, w=w, pbo=pbo: e.activation(
                OA[:, hp, w * 512:(w + 1) * 512], pO[:, pbo, :],
                AF.Identity),
                reads=(f"pO{pbo}",), writes=(f"OA{hp}.{w}",),
                tag=f"f{l}.Oe{hp}.{w}")

        pend_o = None
        for hp in range(4):
            for w in range(NW):
                jt0 = 0 if w == 0 else (w - 1) * 4
                njt = 4 if w == 0 else 8
                for it in range(4):
                    iv = w * 512 + it * 128
                    for u in range(2):
                        s_task(hp, w, it, u, jt0, iv)
                    if it == 1 and pend_o is not None:
                        emit_omm(*pend_o)
                        pend_o = None
                    if it > 0:
                        emit_tp(hp, w, it - 1)
                    softmax(hp, w, it)
                emit_tp(hp, w, 3)
                pend_o = (hp, w, jt0, njt)
        emit_omm(*pend_o)

    def final_phase(self):
        nc, p, h = self.nc, self.p, self.h
        XF = self.XF
        with ExitStack() as S:
            SS = S.enter_context(nc.sbuf_tensor("SS", [128, 3, N], F32))
            XB = S.enter_context(nc.sbuf_tensor("XB", [128, 3, D], F32))
            OT = S.enter_context(nc.sbuf_tensor("OT", [128, 3, D], F32))
            pfin = S.enter_context(nc.psum_tensor("pfin", [128, 2, 1024], F32))
            for k in range(DT):
                b = k % 3
                p.task("scalar", lambda e, k=k: e.activation(
                    XF[:, k, :], XF[:, k, :], AF.Silu),
                    reads=(f"XF{k}",), writes=(f"XF{k}",), tag=f"fin.silu{k}")
                if "dbg_g" in self.dbg_h:
                    rid = f"dbg_g.{k}"
                    p.dma("sync", self.dbg_h["dbg_g"][k], XF[:, k, :],
                          reads=(f"XF{k}",), writes=(rid,), tag=rid)
                    self.out_res.append(rid)
                p.dma("sync" if k % 2 == 0 else "scalar", SS[:, b, :],
                      self.s_scr[k],
                      reads=(f"s_scr{k}",), writes=(f"SS{b}",), tag=f"fin.s{k}")
                p.task("vector", lambda e, k=k, b=b: e.tensor_tensor(
                    XF[:, k, :], XF[:, k, :], SS[:, b, :], op=ALU.mult),
                    reads=(f"XF{k}", f"SS{b}"), writes=(f"XF{k}",),
                    tag=f"fin.sg{k}")
            for t in range(NT):
                b = t % 3
                p.dma("sync" if t % 2 == 0 else "scalar", XB[:, b, :],
                      h["x"][t * 128:(t + 1) * 128, :],
                      reads=(f"XB{b}c",), writes=(f"XB{b}",), tag=f"fin.x{t}")
                bp = t % 2
                def tpf(e, t=t, bp=bp):
                    for k in range(DT):
                        mm = e.transpose(pfin[:, bp, k * 128:(k + 1) * 128],
                                         XF[:, k, t * 128:(t + 1) * 128],
                                         self.ident_f32[:])
                    return mm
                p.task("tensor", tpf,
                       reads=tuple(f"XF{k}" for k in range(DT)) + ("identity_f32",),
                       writes=(f"pfin{bp}",), tag=f"fin.tp{t}")
                p.task("vector", lambda e, t=t, b=b, bp=bp: e.tensor_tensor(
                    OT[:, b, :], XB[:, b, :], pfin[:, bp, :], op=ALU.add),
                    reads=(f"pfin{bp}", f"XB{b}"), writes=(f"OT{b}", f"XB{b}c"),
                    tag=f"fin.add{t}")
                rid = f"out{t}"
                p.dma("gpsimd", self.out_h[t * 128:(t + 1) * 128, :], OT[:, b, :],
                      reads=(f"OT{b}",), writes=(rid,), tag=rid)
                self.out_res.append(rid)
            p.flush("final")




def build_nc(debug=(), stop_after=None):
    b = Build(debug, stop_after)
    nc, dbg = b.build()
    return nc, dbg


def kernel(**inputs):
    g = prep_inputs(inputs)
    c = host_constants()
    nc, _ = build_nc()
    base = {}
    for n in INPUT_NAMES:
        base[n] = g[n]
    for n in CONST_NAMES:
        base[n] = c[n]
    in_maps = []
    for b in range(B):
        m = dict(base)
        m["x"] = np.ascontiguousarray(g["x"][b])
        in_maps.append(m)
    res = run_bass_kernel_spmd(nc, in_maps, core_ids=list(range(B)))
    return np.stack([res.results[b]["out"] for b in range(B)], axis=0)
